# version 01
# baseline (speedup 1.0000x reference)
import contextlib
import numpy as np
import concourse.bass as bass
import concourse.mybir as mybir
from concourse.bass import IndirectOffsetOnAxis
from concourse.bass_utils import run_bass_kernel_spmd
from concourse.alu_op_type import AluOpType as OP

F32 = mybir.dt.float32
BF16 = mybir.dt.bfloat16
U32 = mybir.dt.uint32
I32 = mybir.dt.int32
AF = mybir.ActivationFunctionType

D = 1024
ND = 8
E = 8
NCORE = 8
F_UP = 2730
NF = 22
FCH = [128] * 21 + [42]
EPS = 1e-5


class Prog:
    def __init__(self):
        self.items = []
        self.sems = {}

    def sem(self, name):
        self.sems.setdefault(name, 0)

    def n(self, name):
        return self.sems[name]

    def op(self, eng, emitfn, sem=None, k=1, emit_k=None):
        if sem is not None:
            self.sems[sem] = self.sems.get(sem, 0) + k
            tgt = self.sems[sem]
            ek = k if emit_k is None else emit_k

            def fn(e, H, emitfn=emitfn, sem=sem, ek=ek):
                emitfn(e, H).then_inc(H[sem], ek)
            self.items.append((eng, fn))
            return tgt
        self.items.append((eng, lambda e, H, emitfn=emitfn: emitfn(e, H)))
        return None

    def wait(self, eng, sem, val=None):
        if val is None:
            val = self.sems.get(sem, 0)
        if val > 0:
            self.items.append(
                (eng, lambda e, H, sem=sem, val=val: e.wait_ge(H[sem], val)))
        return val


def build_program(seq=2048, dbg=False):
    NT = seq // 128
    NQ = seq // 512
    NTOK = NCORE * seq
    KSEL = NTOK // E
    NSL = KSEL // 128
    NPQ = NTOK // 128
    BIS_F = NTOK // 64
    NKQ = KSEL // 512
    PACKW = 2 * NPQ
    NBITS = max(1, (NPQ - 1).bit_length())
    MIDW = NF * 512

    nc = bass.Bass("TRN2", target_bir_lowering=False, debug=False,
                   detect_race_conditions=False)

    xT = nc.dram_tensor("xT", [D, seq], F32, kind="ExternalInput").ap()
    wzT = [nc.dram_tensor(f"wzT{l}", [D, D], F32, kind="ExternalInput").ap()
           for l in range(2)]
    whT = [nc.dram_tensor(f"whT{l}", [D, D], F32, kind="ExternalInput").ap()
           for l in range(2)]
    bzin = nc.dram_tensor("bzin", [128, 2 * ND], F32, kind="ExternalInput").ap()
    bhin = nc.dram_tensor("bhin", [128, 2 * ND], F32, kind="ExternalInput").ap()
    wr = nc.dram_tensor("wr", [D, E], F32, kind="ExternalInput").ap()
    br = nc.dram_tensor("br", [E, 1], F32, kind="ExternalInput").ap()
    wu = nc.dram_tensor("wu", [D, F_UP], F32, kind="ExternalInput").ap()
    bu = nc.dram_tensor("bu", [128, NF], F32, kind="ExternalInput").ap()
    wd = nc.dram_tensor("wd", [F_UP, D], F32, kind="ExternalInput").ap()
    bd = nc.dram_tensor("bd", [1, D], F32, kind="ExternalInput").ap()
    cmask = nc.dram_tensor("cmask", [128, NCORE], F32, kind="ExternalInput").ap()
    rowsel = nc.dram_tensor("rowsel", [E, 1], U32, kind="ExternalInput").ap()
    onehot = nc.dram_tensor("onehot", [E, 1], F32, kind="ExternalInput").ap()
    out = nc.dram_tensor("out", [seq, D], F32, kind="ExternalOutput").ap()
    dbg_t = {}
    if dbg:
        dbg_t["x1tm"] = nc.dram_tensor("dbg_x1tm", [seq, D], BF16,
                                       kind="ExternalOutput").ap()
        dbg_t["probs"] = nc.dram_tensor("dbg_probs", [E, seq], F32,
                                        kind="ExternalOutput").ap()
        dbg_t["tau"] = nc.dram_tensor("dbg_tau", [64, 4], F32,
                                      kind="ExternalOutput").ap()
        dbg_t["ids"] = nc.dram_tensor("dbg_ids", [KSEL, 1], U32,
                                      kind="ExternalOutput").ap()
        dbg_t["gslot"] = nc.dram_tensor("dbg_gslot", [1, KSEL], F32,
                                        kind="ExternalOutput").ap()
        dbg_t["y"] = nc.dram_tensor("dbg_y", [seq, D], BF16,
                                    kind="ExternalOutput").ap()
        dbg_t["toks"] = nc.dram_tensor("dbg_toks", [KSEL, D], BF16,
                                       kind="ExternalOutput").ap()
        dbg_t["mid0"] = nc.dram_tensor("dbg_mid0", [128, NF * 512], BF16,
                                       kind="ExternalOutput").ap()
        dbg_t["yin"] = nc.dram_tensor("dbg_yin", [NTOK, D], BF16,
                                      kind="ExternalOutput").ap()
        dbg_t["xbf"] = nc.dram_tensor("dbg_xbf", [128, ND * seq], BF16,
                                      kind="ExternalOutput").ap()
        dbg_t["h1"] = nc.dram_tensor("dbg_h1", [128, ND * seq], BF16,
                                     kind="ExternalOutput").ap()
        dbg_t["car"] = nc.dram_tensor("dbg_car", [128, NCORE * 32], F32,
                                      kind="ExternalOutput").ap()
        dbg_t["y1"] = nc.dram_tensor("dbg_y1", [128, ND * seq], BF16,
                                     kind="ExternalOutput").ap()
        dbg_t["hst"] = nc.dram_tensor("dbg_hst", [128, 2 * ND], F32,
                                      kind="ExternalOutput").ap()
        dbg_t["car0"] = nc.dram_tensor("dbg_car0", [128, NCORE * 32], F32,
                                       kind="ExternalOutput").ap()
        dbg_t["cmsk"] = nc.dram_tensor("dbg_cmsk", [128, NCORE], F32,
                                       kind="ExternalOutput").ap()

    d_car_in = nc.dram_tensor("d_car_in", [128, 32], F32).ap()
    d_car_out = nc.dram_tensor("d_car_out", [128 * NCORE, 32], F32).ap()
    d_probs_in = nc.dram_tensor("d_probs_in", [E, seq], F32).ap()
    d_probs_out = nc.dram_tensor("d_probs_out", [E * NCORE, seq], F32).ap()
    d_x1_in = nc.dram_tensor("d_x1_in", [seq, D], BF16).ap()
    d_x1_out = nc.dram_tensor("d_x1_out", [NTOK, D], BF16).ap()
    d_y_in = nc.dram_tensor("d_y_in", [NTOK, D], BF16).ap()
    d_y_out = nc.dram_tensor("d_y_out", [seq, D], BF16).ap()
    d_escore = nc.dram_tensor("d_escore", [NTOK, 1], F32).ap()
    d_ids = nc.dram_tensor("d_ids", [KSEL, 1], U32).ap()
    d_rows = nc.dram_tensor("d_rows", [2, NTOK], F32).ap()
    d_grow = nc.dram_tensor("d_grow", [1, KSEL], F32).ap()
    d_bar_in = nc.dram_tensor("d_bar_in", [8, 4], F32).ap()
    d_bar_out = nc.dram_tensor("d_bar_out", [8, 4], F32).ap()

    P = Prog()
    for s in ["s_in", "s_pe", "s_act", "s_dve", "s_gp", "s_gdma", "s_adma",
              "s_zero", "s_cc", "s_st", "s_cw", "s_cs"]:
        P.sem(s)

    def cc_delay(P, X, n=150):
        for _ in range(n):
            P.op("gpsimd", lambda e, H: e.memset(X["zrow"][:, 0:512], 0.0))

    def cw():
        P.wait("vector", "s_cw", P.n("s_cw"))

    def cs():
        P.wait("scalar", "s_cs", P.n("s_cs"))

    ctx = contextlib.ExitStack()
    X = {}

    def T(name, shape, dt):
        X[name] = ctx.enter_context(nc.sbuf_tensor(name, shape, dt))
        return X[name]

    T("xbf", [128, ND * seq], BF16)
    T("h1a", [128, ND * seq], BF16)
    T("hlc", [128, ND * seq], BF16)
    T("ptl", [128, ND * seq], BF16)
    T("cbuf", [128, seq], BF16)
    T("vbuf", [128, seq], BF16)
    T("spb", [128, 1024], BF16)
    T("sigk", [128, 512], BF16)
    T("wstg", [128, 2048], F32)
    T("wzb", [128, D], BF16)
    T("whb", [128, D], BF16)
    T("wub2", [128, 2048], BF16)
    T("wdb2", [128, 2048], BF16)
    T("iota128f", [128, 128], F32)
    T("iotapf", [128, 1], F32)
    T("iotapi", [128, 1], I32)
    T("identb", [128, 128], BF16)
    T("trilb", [128, 128], BF16)
    T("onesb", [128, 128], BF16)
    T("ones64", [64, 64], BF16)
    T("ones8f", [E, 1], F32)
    T("bzt", [128, 2 * ND], F32)
    T("bht", [128, 2 * ND], F32)
    T("bzn", [128, 2 * ND], F32)
    T("bh05", [128, 2 * ND], F32)
    T("wrb", [128, ND * E], BF16)
    T("brt", [E, 1], F32)
    T("oneh", [E, 1], F32)
    T("rsel", [E, 1], U32)
    T("cmsk", [128, NCORE], F32)
    T("carst", [128, 32], F32)
    T("carall", [128, NCORE * 32 + 64], F32)
    T("hstart", [128, 128], F32)
    T("tmpc", [128, 64], F32)
    T("hsB", [128, 128], F32)
    T("mstat", [128, NT], F32)
    T("rstdv", [128, NT], F32)
    T("mrv", [128, NT], F32)
    T("stsc", [128, 1024], F32)
    T("zrow", [128, 1024], BF16)
    T("zero1", [128, 1], F32)
    T("zero1b", [128, 1], BF16)
    T("sc64", [64, BIS_F], F32)
    T("j64", [64, BIS_F], BF16)
    T("lo", [64, 1], F32)
    T("hi", [64, 1], F32)
    T("midv", [64, 1], F32)
    T("cnt", [64, 1], F32)
    T("cntb", [64, 1], BF16)
    T("bmask", [64, 1], I32)
    T("nbmask", [64, 1], I32)
    T("mtot", [64, 1], F32)
    T("rneed", [64, 1], F32)
    T("tau128", [128, 1], F32)
    T("r128", [128, 1], F32)
    T("npp", [128, 1], F32)
    T("mpp", [128, 1], F32)
    T("mppb", [128, 1], BF16)
    T("mhib", [128, 1], BF16)
    T("mlob", [128, 1], BF16)
    T("t1c", [128, 1], F32)
    T("t2c", [128, 1], F32)
    T("tpackb", [128, 128], BF16)
    T("idtiles", [128, NSL], U32)
    T("gslot_tm", [128, NSL], F32)
    T("but", [128, NF], F32)
    T("bdt", [128, D], F32)
    T("lnm", [128, 1], F32)
    T("lnv", [128, 1], F32)
    T("lnr", [128, 1], F32)
    T("epsc", [128, 1], F32)

    with ctx:
        xbf, h1a, hlc, ptl = X["xbf"], X["h1a"], X["hlc"], X["ptl"]

        def bh2(n):
            return n // 2
        rawsb = xbf[:, bh2(0):bh2(8192)].bitcast(F32)[0:E, :]
        rows8 = xbf[:, bh2(8192):bh2(16384)].bitcast(F32)[0:E, :]
        prsb = xbf[:, bh2(16384):bh2(24576)].bitcast(F32)[0:E, :]
        iotaf2 = xbf[:, bh2(24576):bh2(24576) + 2048].bitcast(F32)
        iotaf2 = xbf[:, bh2(24576):bh2(32768)].bitcast(F32)
        psrow = xbf[:, bh2(0):bh2(8192)].bitcast(F32)[0:1, :]
        workA = xbf[:, bh2(8192):bh2(16384)].bitcast(F32)[0:1, :]
        workB = xbf[:, bh2(16384):bh2(24576)].bitcast(F32)[0:1, :]
        idrowu = xbf[:, bh2(8192):bh2(16384)].bitcast(U32)[0:1, :]
        gexp = xbf[:, bh2(0):bh2(8192)].bitcast(F32)[0:E, :]
        grecip = xbf[:, bh2(16384):bh2(24576)].bitcast(F32)[0:1, :]
        gslotr = xbf[:, bh2(24576):bh2(32768)].bitcast(F32)[0:1, :]
        mid = xbf[:, 0:MIDW]
        _cur = [0]

        def _cv(nbf):
            a, b = _cur[0], _cur[0] + nbf
            _cur[0] = b
            return h1a[:, a:b]
        m12 = _cv(KSEL)
        jhT = _cv(512)
        r2T = _cv(512)
        psrowb = _cv(KSEL)[0:1, :]
        jrowb = _cv(KSEL)[0:1, :]
        mgt = _cv(2 * NPQ).bitcast(F32)
        meq = _cv(2 * NPQ).bitcast(F32)
        msel = _cv(2 * NPQ).bitcast(F32)
        tcum = _cv(2 * NPQ).bitcast(F32)
        scrm = _cv(2 * NPQ).bitcast(F32)
        sc128 = _cv(2 * NPQ).bitcast(F32)
        fpV = _cv(2 * PACKW).bitcast(F32)
        fpM = _cv(2 * PACKW).bitcast(F32)
        fpmv = _cv(2 * PACKW).bitcast(F32)
        fpnm = _cv(2 * PACKW).bitcast(F32)
        fpB = [_cv(2 * PACKW).bitcast(F32) for _ in range(NBITS)]
        fpmvi = _cv(PACKW).bitcast(mybir.dt.uint8)[:, 0:PACKW]
        assert _cur[0] <= ND * seq, _cur[0]
        wrow = ptl[:, 0:1024]
        accq = ptl[:, 1024:1024 + 2048].bitcast(F32)
        x1re = ptl[:, 3072:4096]
        yre = ptl[:, 4096:5120]
        osta = ptl[:, 5120:7168].bitcast(F32)

        PSE = contextlib.ExitStack()
        psK = PSE.enter_context(nc.psum_tensor("psK", [128, 1024], F32))
        psP = PSE.enter_context(nc.psum_tensor("psP", [128, 1024], F32))
        psT = PSE.enter_context(nc.psum_tensor("psT", [128, 2048], BF16))
        psS = PSE.enter_context(nc.psum_tensor("psS", [128, 1024], F32))
        psTf = psT.ap().bitcast(F32)

        def psKb(i):
            return psK[:, i * 512:(i + 1) * 512]

        def psPb(i):
            return psP[:, i * 512:(i + 1) * 512]

        def psSb(i):
            return psS[:, i * 512:(i + 1) * 512]

        def psTb(i):
            return psT[:, i * 1024:(i + 1) * 1024]

        def psTfb(i):
            return psTf[:, i * 512:(i + 1) * 512]

        def dt_(t, d):
            return t[:, d * seq:(d + 1) * seq]

        P.op("sync", lambda e, H: e.dma_start(out=X["bzt"][:], in_=bzin[:, :]),
             "s_in", 16)
        P.op("sync", lambda e, H: e.dma_start(out=X["bht"][:], in_=bhin[:, :]),
             "s_in", 16)
        P.op("sync", lambda e, H: e.dma_start(
            out=X["wstg"][:, 0:ND * E],
            in_=wr.rearrange("(c p) e -> p c e", p=128)), "s_in", 16)
        P.op("sync", lambda e, H: e.dma_start(out=X["brt"][:], in_=br[:, :]),
             "s_in", 16)
        P.op("sync", lambda e, H: e.dma_start(out=X["cmsk"][:], in_=cmask[:, :]),
             "s_in", 16)
        P.op("sync", lambda e, H: e.dma_start(out=X["oneh"][:], in_=onehot[:, :]),
             "s_in", 16)
        P.op("sync", lambda e, H: e.dma_start(out=X["rsel"][:], in_=rowsel[:, :]),
             "s_in", 16)
        P.op("sync", lambda e, H: e.dma_start(out=X["but"][:], in_=bu[:, :]),
             "s_in", 16)
        P.op("sync", lambda e, H: e.dma_start(
            out=X["bdt"][:], in_=bd[0:1, :].partition_broadcast(128)),
            "s_in", 16)
        in0 = P.n("s_in")

        P.op("gpsimd", lambda e, H: e.iota(
            X["iotapi"][:], pattern=[[0, 1]], base=0, channel_multiplier=1))
        P.op("gpsimd", lambda e, H: e.iota(
            X["iota128f"].bitcast(I32)[:], pattern=[[1, 128]], base=0,
            channel_multiplier=0), "s_gp")
        gp0 = P.n("s_gp")

        P.wait("vector", "s_gp", gp0)
        P.op("vector", lambda e, H: e.tensor_copy(out=X["iotapf"][:],
                                                  in_=X["iotapi"][:]))
        P.op("vector", lambda e, H: e.tensor_copy(
            out=X["iota128f"][:], in_=X["iota128f"].bitcast(I32)[:]), "s_cw")
        cw()
        P.op("vector", lambda e, H: e.tensor_tensor(
            out=X["identb"][:], in0=X["iota128f"][:],
            in1=X["iotapf"][:].to_broadcast([128, 128]), op=OP.is_equal))
        P.op("vector", lambda e, H: e.tensor_tensor(
            out=X["trilb"][:], in0=X["iota128f"][:],
            in1=X["iotapf"][:].to_broadcast([128, 128]), op=OP.is_gt))
        P.op("vector", lambda e, H: e.memset(X["onesb"][:], 1.0))
        P.op("vector", lambda e, H: e.memset(X["ones64"][:], 1.0))
        P.op("vector", lambda e, H: e.memset(X["ones8f"][:], 1.0))
        P.op("vector", lambda e, H: e.memset(X["zero1"][:], 0.0))
        P.op("vector", lambda e, H: e.memset(X["zero1b"][:], 0.0))
        P.op("vector", lambda e, H: e.memset(X["epsc"][:], EPS))
        P.op("vector", lambda e, H: e.memset(X["zrow"][:], 0.0))
        P.op("vector", lambda e, H: e.memset(X["carst"][:], 0.0))
        P.op("vector", lambda e, H: e.memset(X["hstart"][:], 0.0))
        P.op("vector", lambda e, H: e.memset(X["hsB"][:], 0.0))
        P.op("vector", lambda e, H: e.memset(
            X["carall"][:, NCORE * 32:NCORE * 32 + 64], 0.0))
        P.wait("vector", "s_in", in0)
        P.op("vector", lambda e, H: e.tensor_scalar(
            out=X["bzn"][:], in0=X["bzt"][:], scalar1=-1.0, scalar2=None,
            op0=OP.mult))
        P.op("vector", lambda e, H: e.tensor_scalar(
            out=X["bh05"][:], in0=X["bht"][:], scalar1=0.5, scalar2=None,
            op0=OP.add))
        P.op("vector", lambda e, H: e.tensor_copy(
            out=X["wrb"][:], in_=X["wstg"][:, 0:ND * E]), "s_dve")
        v0 = P.n("s_dve")

        P.wait("scalar", "s_dve", v0)
        P.op("scalar", lambda e, H: e.dma_start(
            out=d_bar_in[:, :], in_=X["zrow"].bitcast(F32)[0:8, 0:4]),
            "s_zero", 16)
        bar_z = P.n("s_zero")
        for i in range(NTOK // 128):
            P.op("scalar", lambda e, H, i=i: e.dma_start(
                out=d_y_in[i * 128:(i + 1) * 128, :], in_=X["zrow"][:, 0:D]),
                "s_zero", 16)

        P.wait("sync", "s_dve", v0)
        for d in range(ND):
            P.wait("sync", "s_gp", P.n("s_gp"))
            P.op("sync", lambda e, H, d=d: e.dma_start(
                out=X["wstg"][:, 0:seq], in_=xT[d * 128:(d + 1) * 128, :]),
                "s_in", 16)
            xl = P.n("s_in")
            P.wait("gpsimd", "s_in", xl)
            P.op("gpsimd", lambda e, H, d=d: e.tensor_copy(
                out=dt_(xbf, d), in_=X["wstg"][:, 0:seq]), "s_gp")
        xld = P.n("s_gp")
        if dbg:
            P.wait("scalar", "s_gp", xld)
            P.op("scalar", lambda e, H: e.dma_start(
                out=dbg_t["xbf"][:, :], in_=xbf[:]), "s_adma", 16)

        def mingru_layer(l, src, dst, ccol):
            for d in range(ND):
                P.wait("sync", "s_gp", P.n("s_gp"))
                P.op("sync", lambda e, H, d=d, l=l: e.dma_start(
                    out=X["wstg"][:, 0:D],
                    in_=wzT[l].rearrange("(kc p) h -> p kc h", p=128)[
                        :, :, d * 128:(d + 1) * 128]), "s_in", 16)
                wl1 = P.n("s_in")
                P.wait("gpsimd", "s_in", wl1)
                P.wait("gpsimd", "s_pe", P.n("s_pe"))
                P.op("gpsimd", lambda e, H: e.tensor_copy(
                    out=X["wzb"][:], in_=X["wstg"][:, 0:D]), "s_gp")
                P.wait("sync", "s_gp", P.n("s_gp"))
                P.op("sync", lambda e, H, d=d, l=l: e.dma_start(
                    out=X["wstg"][:, D:2 * D],
                    in_=whT[l].rearrange("(kc p) h -> p kc h", p=128)[
                        :, :, d * 128:(d + 1) * 128]), "s_in", 16)
                wl2 = P.n("s_in")
                P.wait("gpsimd", "s_in", wl2)
                P.op("gpsimd", lambda e, H: e.tensor_copy(
                    out=X["whb"][:], in_=X["wstg"][:, D:2 * D]), "s_gp")
                wcast = P.n("s_gp")

                P.wait("tensor", "s_gp", wcast)
                for q in range(NQ):
                    qb = q % 2
                    if not (l == 0 and d == 0 and q < 2):
                        P.wait("tensor", "s_act", P.n("s_act"))
                        P.wait("tensor", "s_dve", P.n("s_dve"))
                    for kc in range(ND):
                        P.op("tensor", lambda e, H, q=q, kc=kc, qb=qb,
                             src=src: e.matmul(
                                 psKb(qb), X["wzb"][:, kc * 128:(kc + 1) * 128],
                                 dt_(src, kc)[:, q * 512:(q + 1) * 512],
                                 start=(kc == 0), stop=(kc == ND - 1)))
                    for kc in range(ND):
                        P.op("tensor", lambda e, H, q=q, kc=kc, qb=qb,
                             src=src: e.matmul(
                                 psPb(qb), X["whb"][:, kc * 128:(kc + 1) * 128],
                                 dt_(src, kc)[:, q * 512:(q + 1) * 512],
                                 start=(kc == 0), stop=(kc == ND - 1)),
                             sem="s_pe" if kc == ND - 1 else None)
                    mmd = P.n("s_pe")
                    P.wait("scalar", "s_pe", mmd)
                    P.op("scalar", lambda e, H, q=q, qb=qb, d=d, l=l:
                         e.activation(
                             out=X["cbuf"][:, q * 512:(q + 1) * 512],
                             in_=psKb(qb), func=AF.Sigmoid,
                             bias=X["bzn"][:, l * ND + d:l * ND + d + 1],
                             scale=-1.0))
                    P.op("scalar", lambda e, H, q=q, qb=qb, d=d, l=l:
                         e.activation(
                             out=X["spb"][:, qb * 512:(qb + 1) * 512],
                             in_=psPb(qb), func=AF.Sigmoid,
                             bias=X["bht"][:, l * ND + d:l * ND + d + 1],
                             scale=1.0), "s_act")
                    acd = P.n("s_act")
                    P.wait("vector", "s_act", acd)
                    P.op("vector", lambda e, H, q=q, qb=qb, d=d, l=l:
                         e.scalar_tensor_tensor(
                             out=X["vbuf"][:, q * 512:(q + 1) * 512],
                             in0=psPb(qb),
                             scalar=X["bh05"][:, l * ND + d:l * ND + d + 1],
                             in1=X["spb"][:, qb * 512:(qb + 1) * 512],
                             op0=OP.add, op1=OP.max))
                    P.op("vector", lambda e, H, q=q: e.tensor_scalar(
                        out=X["sigk"][:],
                        in0=X["cbuf"][:, q * 512:(q + 1) * 512],
                        scalar1=-1.0, scalar2=1.0, op0=OP.mult, op1=OP.add))
                    P.op("vector", lambda e, H, q=q: e.tensor_tensor(
                        out=X["vbuf"][:, q * 512:(q + 1) * 512],
                        in0=X["vbuf"][:, q * 512:(q + 1) * 512],
                        in1=X["sigk"][:], op=OP.mult), "s_dve")
                P.op("vector", lambda e, H, d=d, dst=dst: e.tensor_tensor_scan(
                    out=dt_(dst, d), data0=X["cbuf"][:], data1=X["vbuf"][:],
                    initial=0.0, op0=OP.mult, op1=OP.add))
                P.op("vector", lambda e, H, d=d: e.tensor_tensor_scan(
                    out=dt_(ptl, d), data0=X["cbuf"][:],
                    data1=X["zero1b"][:].to_broadcast([128, seq]), initial=1.0,
                    op0=OP.mult, op1=OP.add))
                P.op("vector", lambda e, H, d=d, cc=ccol, dst=dst:
                     e.tensor_copy(out=X["carst"][:, cc + d:cc + d + 1],
                                   in_=dt_(dst, d)[:, seq - 1:seq]))
                P.op("vector", lambda e, H, d=d, cc=ccol: e.tensor_copy(
                    out=X["carst"][:, cc + 8 + d:cc + 8 + d + 1],
                    in_=dt_(ptl, d)[:, seq - 1:seq]), "s_dve")
            sc_done = P.n("s_dve")
            P.wait("gpsimd", "s_dve", sc_done)
            P.wait("gpsimd", "s_zero", bar_z)
            P.op("gpsimd", lambda e, H: e.dma_start(
                out=d_car_in[:, :], in_=X["carst"][:]), "s_gdma", 16)
            P.wait("gpsimd", "s_gdma")
            P.op("gpsimd", lambda e, H: e.collective_compute(
                "AllGather", OP.bypass, replica_groups=[list(range(NCORE))],
                ins=[d_car_in.opt()], outs=[d_car_out.opt()]), "s_cc", 1)
            P.op("gpsimd", lambda e, H: e.collective_compute(
                "AllReduce", OP.add, replica_groups=[list(range(NCORE))],
                ins=[d_bar_in.opt()], outs=[d_bar_out.opt()]), "s_cc", 1)
            P.wait("gpsimd", "s_cc")
            cc_delay(P, X)
            P.op("gpsimd", lambda e, H: e.dma_start(
                out=X["carall"][:, 0:NCORE * 32],
                in_=d_car_out.rearrange("(r p) c -> p r c", p=128)),
                "s_gdma", 16)
            if dbg and ccol == 0:
                P.op("gpsimd", lambda e, H: e.dma_start(
                    out=dbg_t["car0"][:, :], in_=X["carall"][:, 0:NCORE * 32]),
                    "s_gdma", 16)
                P.op("gpsimd", lambda e, H: e.dma_start(
                    out=dbg_t["cmsk"][:, :], in_=X["cmsk"][:]), "s_gdma", 16)
            cl = P.n("s_gdma")
            P.wait("vector", "s_gdma", cl)
            hso = l * 64
            bufs = ["hstart", "hsB"]
            P.op("vector", lambda e, H, hso=hso: e.memset(
                X["hstart"][:, hso:hso + 64], 0.5), "s_cw")
            cw()
            for r in range(NCORE):
                hcol = r * 32 + ccol
                pcol = r * 32 + ccol + 8
                sa = bufs[r % 2]
                da = bufs[(r + 1) % 2]
                P.op("vector", lambda e, H, pcol=pcol, sa=sa, hso=hso:
                     e.tensor_tensor(
                         out=X["tmpc"][:], in0=X["carall"][:, pcol:pcol + 64],
                         in1=X[sa][:, hso:hso + 64], op=OP.mult))
                P.op("vector", lambda e, H, hcol=hcol: e.tensor_tensor(
                    out=X["tmpc"][:], in0=X["tmpc"][:],
                    in1=X["carall"][:, hcol:hcol + 64], op=OP.add))
                P.op("vector", lambda e, H, r=r, sa=sa, da=da, hso=hso:
                     e.scalar_tensor_tensor(
                         out=X[da][:, hso:hso + 64], in0=X["tmpc"][:],
                         scalar=X["cmsk"][:, r:r + 1],
                         in1=X[sa][:, hso:hso + 64], op0=OP.mult, op1=OP.add))
                P.op("vector", lambda e, H, r=r, sa=sa, hso=hso:
                     e.tensor_scalar(
                         out=X["tmpc"][:], in0=X[sa][:, hso:hso + 64],
                         scalar1=X["cmsk"][:, r:r + 1], scalar2=None,
                         op0=OP.mult))
                P.op("vector", lambda e, H, da=da, hso=hso: e.tensor_tensor(
                    out=X[da][:, hso:hso + 64], in0=X[da][:, hso:hso + 64],
                    in1=X["tmpc"][:], op=OP.subtract), "s_cw")
                cw()
            for d in range(ND):
                P.op("vector", lambda e, H, d=d, hso=hso, dst=dst:
                     e.scalar_tensor_tensor(
                         out=dt_(dst, d), in0=dt_(ptl, d),
                         scalar=X["hstart"][:, hso + d:hso + d + 1],
                         in1=dt_(dst, d), op0=OP.mult, op1=OP.add),
                     sem="s_dve" if d == ND - 1 else None)
            return P.n("s_dve")

        P.wait("tensor", "s_gp", xld)
        P.wait("vector", "s_gp", xld)
        l0_done = mingru_layer(0, xbf, h1a, 0)
        if dbg:
            P.wait("scalar", "s_dve", l0_done)
            P.op("scalar", lambda e, H: e.dma_start(
                out=dbg_t["hst"][:, :], in_=X["hstart"][:, 0:16]),
                "s_adma", 16)
        if dbg:
            P.wait("scalar", "s_dve", l0_done)
            P.op("scalar", lambda e, H: e.dma_start(
                out=dbg_t["h1"][:, :], in_=h1a[:]), "s_adma", 16)
            P.op("scalar", lambda e, H: e.dma_start(
                out=dbg_t["car"][:, :], in_=X["carall"][:, 0:NCORE * 32]),
                "s_adma", 16)
            P.wait("vector", "s_adma", P.n("s_adma"))
        mingru_layer(1, h1a, hlc, 16)

        for d in range(ND):
            P.op("vector", lambda e, H, d=d: e.tensor_tensor(
                out=dt_(hlc, d), in0=dt_(hlc, d), in1=dt_(xbf, d), op=OP.add),
                sem="s_dve" if d == ND - 1 else None)
        y1_done = P.n("s_dve")
        if dbg:
            P.wait("scalar", "s_dve", y1_done)
            P.op("scalar", lambda e, H: e.dma_start(
                out=dbg_t["y1"][:, :], in_=hlc[:]), "s_adma", 16)
            P.wait("vector", "s_adma", P.n("s_adma"))

        P.wait("tensor", "s_dve", y1_done)
        for q in range(NQ):
            if q >= 2:
                P.wait("tensor", "s_act", P.n("s_act"))
            for d in range(ND):
                P.op("tensor", lambda e, H, q=q, d=d: e.matmul(
                    psSb(q % 2)[0:E, :], X["wrb"][:, d * E:(d + 1) * E],
                    dt_(hlc, d)[:, q * 512:(q + 1) * 512],
                    start=(d == 0), stop=(d == ND - 1)),
                    sem="s_pe" if d == ND - 1 else None)
            rq = P.n("s_pe")
            P.wait("scalar", "s_pe", rq)
            P.op("scalar", lambda e, H, q=q: e.copy(
                out=rawsb[:, q * 512:(q + 1) * 512], in_=psSb(q % 2)[0:E, :]),
                "s_act")
        raw_done = P.n("s_act")

        P.wait("tensor", "s_act", raw_done)
        for j in range(NT):
            tb = j % 2
            if j >= 2:
                P.wait("tensor", "s_dve", P.n("s_dve"))
            for d in range(ND):
                P.op("tensor", lambda e, H, j=j, d=d, tb=tb: e.transpose(
                    psTb(tb)[:, d * 128:(d + 1) * 128],
                    dt_(hlc, d)[:, j * 128:(j + 1) * 128], X["identb"][:]),
                    sem="s_pe" if d == ND - 1 else None)
            tp = P.n("s_pe")
            P.wait("vector", "s_pe", tp)
            P.op("vector", lambda e, H, j=j, tb=tb: e.tensor_copy(
                out=ptl[:, j * 1024:(j + 1) * 1024], in_=psTb(tb)[:]), "s_dve")
        y1tm_done = P.n("s_dve")
        y1tm = ptl

        for j in range(NT):
            P.op("vector", lambda e, H, j=j: e.tensor_scalar(
                out=X["stsc"][:], in0=y1tm[:, j * 1024:(j + 1) * 1024],
                scalar1=1.0, scalar2=0.0, op0=OP.mult, op1=OP.add,
                accum_out=X["mstat"][:, j:j + 1]), "s_dve")
            sqv = P.n("s_dve")
            P.wait("scalar", "s_dve", sqv)
            P.op("scalar", lambda e, H, j=j: e.activation(
                out=X["stsc"][:], in_=y1tm[:, j * 1024:(j + 1) * 1024],
                func=AF.Square, bias=0.0, scale=1.0), "s_act")
            P.wait("vector", "s_act", P.n("s_act"))
            P.op("vector", lambda e, H, j=j: e.tensor_scalar(
                out=X["stsc"][:], in0=X["stsc"][:], scalar1=1.0, scalar2=0.0,
                op0=OP.mult, op1=OP.add, accum_out=X["rstdv"][:, j:j + 1]),
                "s_dve")
        P.op("vector", lambda e, H: e.tensor_scalar(
            out=X["mstat"][:], in0=X["mstat"][:], scalar1=1.0 / D,
            scalar2=None, op0=OP.mult), "s_cw")
        cw()
        P.op("vector", lambda e, H: e.tensor_tensor(
            out=X["mrv"][:], in0=X["mstat"][:], in1=X["mstat"][:],
            op=OP.mult), "s_cw")
        cw()
        P.op("vector", lambda e, H: e.scalar_tensor_tensor(
            out=X["rstdv"][:], in0=X["rstdv"][:], scalar=1.0 / D,
            in1=X["mrv"][:], op0=OP.mult, op1=OP.subtract), "s_dve")
        var_done = P.n("s_dve")
        P.wait("scalar", "s_dve", var_done)
        P.op("scalar", lambda e, H: e.activation(
            out=X["rstdv"][:], in_=X["rstdv"][:], func=AF.Ln, bias=X["epsc"][:],
            scale=1.0), "s_cs")
        cs()
        P.op("scalar", lambda e, H: e.activation(
            out=X["rstdv"][:], in_=X["rstdv"][:], func=AF.Exp, bias=0.0,
            scale=-0.5), "s_act")
        rstd_done = P.n("s_act")
        P.wait("vector", "s_act", rstd_done)
        P.op("vector", lambda e, H: e.tensor_tensor(
            out=X["mrv"][:], in0=X["mstat"][:], in1=X["rstdv"][:], op=OP.mult),
            "s_dve")
        mr_done = P.n("s_dve")

        P.wait("scalar", "s_dve", mr_done)
        for j in range(NT):
            P.op("scalar", lambda e, H, j=j: e.dma_start(
                out=d_rows[0:1, j * 128:(j + 1) * 128].rearrange("o t -> t o"),
                in_=X["rstdv"][:, j:j + 1]), "s_adma", 16)
            P.op("scalar", lambda e, H, j=j: e.dma_start(
                out=d_rows[1:2, j * 128:(j + 1) * 128].rearrange("o t -> t o"),
                in_=X["mrv"][:, j:j + 1]), "s_adma", 16)
        srw = P.n("s_adma")
        P.wait("tensor", "s_act", raw_done)
        for d in range(ND):
            P.op("tensor", lambda e, H, d=d: e.matmul(
                psSb(0)[0:E, 0:1], X["wrb"][:, d * E:(d + 1) * E],
                X["onesb"][:, 0:1], start=(d == 0), stop=(d == ND - 1)),
                sem="s_pe" if d == ND - 1 else None)
        sw_done = P.n("s_pe")
        P.wait("vector", "s_pe", sw_done)
        P.op("vector", lambda e, H: e.tensor_scalar(
            out=X["cnt"][0:E, :], in0=psSb(0)[0:E, 0:1], scalar1=-1.0,
            scalar2=None, op0=OP.mult), "s_dve")
        negsw_done = P.n("s_dve")

        P.wait("sync", "s_adma", srw)
        P.op("sync", lambda e, H: e.dma_start(
            out=rows8[:, 0:seq], in_=d_rows[0:1, 0:seq].partition_broadcast(E)),
            "s_in", 16)
        r8a = P.n("s_in")
        P.wait("vector", "s_in", r8a)
        P.op("vector", lambda e, H: e.tensor_tensor(
            out=rawsb[:, 0:seq], in0=rawsb[:, 0:seq], in1=rows8[:, 0:seq],
            op=OP.mult), "s_dve")
        lg1 = P.n("s_dve")
        P.wait("sync", "s_dve", lg1)
        P.op("sync", lambda e, H: e.dma_start(
            out=rows8[:, 0:seq], in_=d_rows[1:2, 0:seq].partition_broadcast(E)),
            "s_in", 16)
        r8b = P.n("s_in")
        P.wait("vector", "s_in", r8b)
        P.wait("vector", "s_dve", negsw_done)
        P.op("vector", lambda e, H: e.scalar_tensor_tensor(
            out=rawsb[:, 0:seq], in0=rows8[:, 0:seq],
            scalar=X["cnt"][0:E, 0:1], in1=rawsb[:, 0:seq],
            op0=OP.mult, op1=OP.add), "s_dve")
        lg2 = P.n("s_dve")
        P.wait("scalar", "s_dve", lg2)
        P.op("scalar", lambda e, H: e.activation(
            out=prsb[:, 0:seq], in_=rawsb[:, 0:seq], func=AF.Exp,
            bias=X["brt"][:], scale=1.0), "s_act")
        exp_done = P.n("s_act")
        P.wait("tensor", "s_act", exp_done)
        for q in range(NQ):
            P.op("tensor", lambda e, H, q=q: e.matmul(
                psSb(1)[0:1, :], X["ones8f"][:, 0:1],
                prsb[:, q * 512:(q + 1) * 512], start=True, stop=True), "s_pe")
            dq = P.n("s_pe")
            P.wait("vector", "s_pe", dq)
            P.op("vector", lambda e, H, q=q: e.reciprocal(
                out=rawsb[0:1, q * 512:(q + 1) * 512], in_=psSb(1)[0:1, :]),
                "s_dve")
            P.wait("tensor", "s_dve", P.n("s_dve"))
        rec_done = P.n("s_dve")
        P.wait("scalar", "s_dve", rec_done)
        P.op("scalar", lambda e, H: e.dma_start(
            out=d_rows[0:1, 0:seq], in_=rawsb[0:1, 0:seq]), "s_adma", 16)
        P.wait("sync", "s_adma", P.n("s_adma"))
        P.op("sync", lambda e, H: e.dma_start(
            out=rows8[:, 0:seq], in_=d_rows[0:1, 0:seq].partition_broadcast(E)),
            "s_in", 16)
        r8c = P.n("s_in")
        P.wait("vector", "s_in", r8c)
        P.op("vector", lambda e, H: e.tensor_tensor(
            out=prsb[:, 0:seq], in0=prsb[:, 0:seq], in1=rows8[:, 0:seq],
            op=OP.mult), "s_dve")
        probs_done = P.n("s_dve")
        P.wait("scalar", "s_dve", probs_done)
        P.op("scalar", lambda e, H: e.dma_start(
            out=d_probs_in[:, :], in_=prsb[:, 0:seq]), "s_adma", 16)
        pw = P.n("s_adma")
        P.wait("gpsimd", "s_adma", pw)
        P.op("gpsimd", lambda e, H: e.collective_compute(
            "AllGather", OP.bypass, replica_groups=[list(range(NCORE))],
            ins=[d_probs_in.opt()], outs=[d_probs_out.opt()]), "s_cc", 1)
        P.op("gpsimd", lambda e, H: e.collective_compute(
            "AllReduce", OP.add, replica_groups=[list(range(NCORE))],
            ins=[d_bar_in.opt()], outs=[d_bar_out.opt()]), "s_cc", 1)
        cc_probs = P.n("s_cc")
        if dbg:
            P.wait("scalar", "s_dve", probs_done)
            P.op("scalar", lambda e, H: e.dma_start(
                out=dbg_t["probs"][:, :], in_=prsb[:, 0:seq]), "s_st", 16)

        P.wait("vector", "s_dve", probs_done)
        for j in range(NT):
            P.op("vector", lambda e, H, j=j: e.tensor_scalar(
                out=hlc[:, j * 1024:(j + 1) * 1024],
                in0=y1tm[:, j * 1024:(j + 1) * 1024],
                scalar1=X["mstat"][:, j:j + 1], scalar2=X["rstdv"][:, j:j + 1],
                op0=OP.subtract, op1=OP.mult),
                sem="s_dve" if j == NT - 1 else None)
        x1_done = P.n("s_dve")
        x1tm = hlc
        P.wait("scalar", "s_dve", x1_done)
        for j in range(NT):
            P.op("scalar", lambda e, H, j=j: e.dma_start(
                out=d_x1_in[j * 128:(j + 1) * 128, :],
                in_=x1tm[:, j * 1024:(j + 1) * 1024]), "s_adma", 16)
        x1w = P.n("s_adma")
        P.wait("gpsimd", "s_adma", x1w)
        P.op("gpsimd", lambda e, H: e.collective_compute(
            "AllGather", OP.bypass, replica_groups=[list(range(NCORE))],
            ins=[d_x1_in.opt()], outs=[d_x1_out.opt()]), "s_cc", 1)
        P.op("gpsimd", lambda e, H: e.collective_compute(
            "AllReduce", OP.add, replica_groups=[list(range(NCORE))],
            ins=[d_bar_in.opt()], outs=[d_bar_out.opt()]), "s_cc", 1)
        cc_x1 = P.n("s_cc")
        if dbg:
            P.wait("scalar", "s_adma", x1w)
            for j in range(NT):
                P.op("scalar", lambda e, H, j=j: e.dma_start(
                    out=dbg_t["x1tm"][j * 128:(j + 1) * 128, :],
                    in_=x1tm[:, j * 1024:(j + 1) * 1024]), "s_st", 16)

        P.wait("gpsimd", "s_cc", cc_probs)
        cc_delay(P, X)
        P.op("gpsimd", lambda e, H: e.indirect_dma_start(
            out=prsb[:, 0:seq], out_offset=None, in_=d_probs_out[:, :],
            in_offset=IndirectOffsetOnAxis(ap=X["rsel"][:], axis=0),
            bounds_check=E * NCORE - 1, oob_is_err=False), "s_gdma", 16)
        eg = P.n("s_gdma")
        P.wait("gpsimd", "s_gdma", eg)
        P.op("gpsimd", lambda e, H: e.dma_start(
            out=d_escore.rearrange("(r t) o -> r (t o)", r=E),
            in_=prsb[:, 0:seq]), "s_gdma", 16)
        ew = P.n("s_gdma")
        P.wait("sync", "s_gdma", ew)
        P.op("sync", lambda e, H: e.dma_start(
            out=X["sc64"][:],
            in_=d_escore.rearrange("(p f) o -> p (f o)", p=64)), "s_in", 16)
        P.op("sync", lambda e, H: e.dma_start(
            out=sc128[:],
            in_=d_escore.rearrange("(p f) o -> p (f o)", p=128)), "s_in", 16)
        scl = P.n("s_in")

        P.wait("vector", "s_in", scl)
        P.op("vector", lambda e, H: e.memset(X["lo"][:], 0.0))
        P.op("vector", lambda e, H: e.memset(X["hi"][:], 1.0), "s_dve")
        P.wait("tensor", "s_dve", P.n("s_dve"))
        for it in range(40):
            P.wait("vector", "s_dve", P.n("s_dve"))
            P.op("vector", lambda e, H: e.tensor_tensor(
                out=X["midv"][:], in0=X["lo"][:], in1=X["hi"][:], op=OP.add),
                "s_cw")
            cw()
            P.op("vector", lambda e, H: e.tensor_scalar(
                out=X["midv"][:], in0=X["midv"][:], scalar1=0.5, scalar2=None,
                op0=OP.mult), "s_cw")
            cw()
            P.op("vector", lambda e, H: e.tensor_scalar(
                out=X["j64"][:], in0=X["sc64"][:], scalar1=X["midv"][:],
                scalar2=0.0, op0=OP.is_gt, op1=OP.add, accum_out=X["cnt"][:]),
                "s_cw")
            cw()
            P.op("vector", lambda e, H: e.tensor_copy(
                out=X["cntb"][:], in_=X["cnt"][:]), "s_dve")
            P.wait("tensor", "s_dve", P.n("s_dve"))
            P.op("tensor", lambda e, H: e.matmul(
                psSb(0)[0:64, 0:1], X["ones64"][:], X["cntb"][:], start=True,
                stop=True), "s_pe")
            P.wait("vector", "s_pe", P.n("s_pe"))
            P.op("vector", lambda e, H: e.tensor_scalar(
                out=X["bmask"][:], in0=psSb(0)[0:64, 0:1],
                scalar1=float(KSEL), scalar2=None, op0=OP.is_ge))
            P.op("vector", lambda e, H: e.tensor_scalar(
                out=X["nbmask"][:], in0=psSb(0)[0:64, 0:1],
                scalar1=float(KSEL), scalar2=None, op0=OP.is_lt), "s_cw")
            cw()
            P.op("vector", lambda e, H: e.select(
                out=X["lo"][:], mask=X["bmask"][:], on_true=X["midv"][:],
                on_false=X["lo"][:]))
            P.op("vector", lambda e, H: e.select(
                out=X["hi"][:], mask=X["nbmask"][:], on_true=X["midv"][:],
                on_false=X["hi"][:]), "s_dve")
            P.wait("tensor", "s_dve", P.n("s_dve"))
        P.wait("vector", "s_dve", P.n("s_dve"))
        P.op("vector", lambda e, H: e.tensor_scalar(
            out=X["j64"][:], in0=X["sc64"][:], scalar1=X["hi"][:], scalar2=0.0,
            op0=OP.is_gt, op1=OP.add, accum_out=X["cnt"][:]), "s_cw")
        cw()
        P.op("vector", lambda e, H: e.tensor_copy(
            out=X["cntb"][:], in_=X["cnt"][:]), "s_dve")
        P.wait("tensor", "s_dve", P.n("s_dve"))
        P.op("tensor", lambda e, H: e.matmul(
            psSb(0)[0:64, 0:1], X["ones64"][:], X["cntb"][:], start=True,
            stop=True), "s_pe")
        P.wait("vector", "s_pe", P.n("s_pe"))
        P.op("vector", lambda e, H: e.tensor_copy(
            out=X["mtot"][:], in_=psSb(0)[0:64, 0:1]), "s_cw")
        cw()
        P.op("vector", lambda e, H: e.tensor_scalar(
            out=X["rneed"][:], in0=X["mtot"][:], scalar1=-1.0,
            scalar2=float(KSEL), op0=OP.mult, op1=OP.add), "s_cw")
        cw()
        P.op("vector", lambda e, H: e.tensor_copy(out=X["tau128"][0:64, :],
                                                  in_=X["hi"][:]))
        P.op("vector", lambda e, H: e.tensor_copy(out=X["tau128"][64:128, :],
                                                  in_=X["hi"][:]))
        P.op("vector", lambda e, H: e.tensor_copy(out=X["r128"][0:64, :],
                                                  in_=X["rneed"][:]))
        P.op("vector", lambda e, H: e.tensor_copy(out=X["r128"][64:128, :],
                                                  in_=X["rneed"][:]), "s_dve")
        bis_done = P.n("s_dve")
        P.wait("vector", "s_dve", bis_done)
        if dbg:
            P.wait("scalar", "s_dve", bis_done)
            P.op("scalar", lambda e, H: e.dma_start(
                out=dbg_t["tau"][:, 0:1], allow_slow_non_contiguous=True, in_=X["hi"][:]), "s_st", 16)
            P.op("scalar", lambda e, H: e.dma_start(
                out=dbg_t["tau"][:, 1:2], allow_slow_non_contiguous=True, in_=X["mtot"][:]), "s_st", 16)
            P.op("scalar", lambda e, H: e.dma_start(
                out=dbg_t["tau"][:, 2:3], allow_slow_non_contiguous=True, in_=X["rneed"][:]), "s_st", 16)

        P.op("vector", lambda e, H: e.tensor_scalar(
            out=mgt[:], in0=sc128[:], scalar1=X["tau128"][:], scalar2=None,
            op0=OP.is_gt))
        P.op("vector", lambda e, H: e.tensor_scalar(
            out=meq[:], in0=sc128[:], scalar1=X["tau128"][:], scalar2=None,
            op0=OP.is_equal), "s_cw")
        cw()
        P.op("vector", lambda e, H: e.tensor_tensor_scan(
            out=tcum[:], data0=meq[:],
            data1=X["zero1"][:].to_broadcast([128, NPQ]), initial=0.0,
            op0=OP.add, op1=OP.add), "s_cw")
        cw()
        P.op("vector", lambda e, H: e.tensor_copy(
            out=X["mppb"][:], in_=tcum[:, NPQ - 1:NPQ]), "s_dve")
        P.wait("tensor", "s_dve", P.n("s_dve"))
        P.op("tensor", lambda e, H: e.matmul(
            psSb(0)[:, 0:1], X["trilb"][:], X["mppb"][:], start=True,
            stop=True), "s_pe")
        P.wait("vector", "s_pe", P.n("s_pe"))
        P.op("vector", lambda e, H: e.tensor_copy(
            out=X["t1c"][:], in_=psSb(0)[:, 0:1]), "s_cw")
        cw()
        P.op("vector", lambda e, H: e.tensor_tensor(
            out=tcum[:], in0=tcum[:], in1=meq[:], op=OP.subtract))
        P.op("vector", lambda e, H: e.tensor_scalar(
            out=tcum[:], in0=tcum[:], scalar1=X["t1c"][:], scalar2=None,
            op0=OP.add), "s_cw")
        cw()
        P.op("vector", lambda e, H: e.tensor_scalar(
            out=scrm[:], in0=tcum[:], scalar1=X["r128"][:], scalar2=None,
            op0=OP.is_lt), "s_cw")
        cw()
        P.op("vector", lambda e, H: e.tensor_tensor(
            out=scrm[:], in0=scrm[:], in1=meq[:], op=OP.mult), "s_cw")
        cw()
        P.op("vector", lambda e, H: e.tensor_tensor(
            out=msel[:], in0=mgt[:], in1=scrm[:], op=OP.add), "s_dve")
        msel_done = P.n("s_dve")
        P.wait("vector", "s_dve", msel_done)

        P.op("vector", lambda e, H: e.tensor_scalar(
            out=scrm[:], in0=msel[:], scalar1=-1.0, scalar2=1.0, op0=OP.mult,
            op1=OP.add), "s_cw")
        cw()
        P.op("vector", lambda e, H: e.tensor_tensor_scan(
            out=tcum[:], data0=scrm[:],
            data1=X["zero1"][:].to_broadcast([128, NPQ]), initial=0.0,
            op0=OP.add, op1=OP.add), "s_cw")
        cw()
        P.op("vector", lambda e, H: e.tensor_tensor(
            out=tcum[:], in0=tcum[:], in1=scrm[:], op=OP.subtract), "s_cw")
        cw()
        P.op("vector", lambda e, H: e.memset(fpM[:], 0.0))
        P.op("vector", lambda e, H: e.memset(fpV[:], 0.0))
        for k in range(NBITS):
            P.op("vector", lambda e, H, k=k: e.memset(fpB[k][:], 0.0))
        P.op("vector", lambda e, H: e.tensor_copy(
            out=fpV[:, 0:NPQ], in_=X["iota128f"][:, 0:NPQ]))
        P.op("vector", lambda e, H: e.tensor_copy(
            out=fpM[:, 0:NPQ], in_=msel[:]))
        P.op("vector", lambda e, H: e.tensor_tensor(
            out=fpmv[:, 0:NPQ], in0=tcum[:], in1=msel[:], op=OP.mult), "s_cw")
        cw()
        for k in range(NBITS - 1, -1, -1):
            P.op("vector", lambda e, H, k=k: e.tensor_scalar(
                out=fpB[k][:, 0:NPQ], in0=fpmv[:, 0:NPQ],
                scalar1=float(1 << k), scalar2=None, op0=OP.is_ge), "s_cw")
            cw()
            P.op("vector", lambda e, H, k=k: e.scalar_tensor_tensor(
                out=fpmv[:, 0:NPQ], in0=fpB[k][:, 0:NPQ],
                scalar=float(-(1 << k)), in1=fpmv[:, 0:NPQ], op0=OP.mult,
                op1=OP.add), "s_cw")
            cw()
        P.op("vector", lambda e, H: e.engine_nop() if False else
             e.memset(fpnm[:], 0.0), "s_dve")
        P.wait("vector", "s_dve", P.n("s_dve"))
        for k in range(NBITS):
            s = 1 << k
            P.op("vector", lambda e, H, k=k: e.tensor_tensor(
                out=fpmv[:], in0=fpM[:], in1=fpB[k][:], op=OP.mult), "s_cw")
            cw()
            P.op("vector", lambda e, H: e.tensor_copy(
                out=fpmvi[:], in_=fpmv[:]), "s_cw")
            cw()
            P.op("vector", lambda e, H, s=s: e.select(
                out=fpV[:, 0:PACKW - s], mask=fpmvi[:, s:PACKW],
                on_true=fpV[:, s:PACKW], on_false=fpV[:, 0:PACKW - s]))
            for j in range(k + 1, NBITS):
                P.op("vector", lambda e, H, s=s, j=j: e.select(
                    out=fpB[j][:, 0:PACKW - s], mask=fpmvi[:, s:PACKW],
                    on_true=fpB[j][:, s:PACKW],
                    on_false=fpB[j][:, 0:PACKW - s]))
            P.op("vector", lambda e, H: e.tensor_scalar(
                out=fpnm[:], in0=fpmv[:], scalar1=-1.0, scalar2=1.0,
                op0=OP.mult, op1=OP.add), "s_cw")
            cw()
            P.op("vector", lambda e, H: e.tensor_tensor(
                out=fpM[:], in0=fpM[:], in1=fpnm[:], op=OP.mult), "s_cw")
            cw()
            P.op("vector", lambda e, H, s=s: e.tensor_tensor(
                out=fpM[:, 0:PACKW - s], in0=fpM[:, 0:PACKW - s],
                in1=fpmv[:, s:PACKW], op=OP.max), "s_dve")
            P.wait("vector", "s_dve", P.n("s_dve"))
        pack_done = P.n("s_dve")

        P.op("vector", lambda e, H: e.tensor_scalar(
            out=scrm[:], in0=msel[:], scalar1=1.0, scalar2=0.0, op0=OP.mult,
            op1=OP.add, accum_out=X["npp"][:]), "s_cw")
        cw()
        P.op("vector", lambda e, H: e.tensor_copy(
            out=X["mppb"][:], in_=X["npp"][:]), "s_dve")
        P.wait("tensor", "s_dve", P.n("s_dve"))
        P.op("tensor", lambda e, H: e.matmul(
            psSb(0)[:, 0:1], X["trilb"][:], X["mppb"][:], start=True,
            stop=True), "s_pe")
        P.wait("vector", "s_pe", P.n("s_pe"))
        P.op("vector", lambda e, H: e.tensor_copy(
            out=X["mpp"][:], in_=psSb(0)[:, 0:1]), "s_dve")
        P.op("gpsimd", lambda e, H: e.iota(
            iotaf2.bitcast(I32)[:], pattern=[[1, KSEL]], base=0,
            channel_multiplier=0), "s_gp")
        P.wait("vector", "s_gp", P.n("s_gp"))
        P.wait("vector", "s_dve", P.n("s_dve"))
        P.op("vector", lambda e, H: e.tensor_copy(
            out=iotaf2[:], in_=iotaf2.bitcast(I32)[:]), "s_cw")
        cw()
        P.op("vector", lambda e, H: e.tensor_scalar(
            out=m12[:], in0=iotaf2[:], scalar1=X["mpp"][:], scalar2=None,
            op0=OP.is_ge), "s_dve")
        P.wait("tensor", "s_dve", P.n("s_dve"))
        for q in range(NKQ):
            P.op("tensor", lambda e, H, q=q: e.matmul(
                psSb(0)[0:1, :], X["onesb"][:, 0:1],
                m12[:, q * 512:(q + 1) * 512], start=True, stop=True), "s_pe")
            P.wait("vector", "s_pe", P.n("s_pe"))
            P.op("vector", lambda e, H, q=q: e.tensor_scalar(
                out=psrow[:, q * 512:(q + 1) * 512], in0=psSb(0)[0:1, :],
                scalar1=-1.0, scalar2=None, op0=OP.add), "s_dve")
            P.wait("tensor", "s_dve", P.n("s_dve"))
        P.wait("vector", "s_dve", P.n("s_dve"))
        P.op("vector", lambda e, H: e.tensor_copy(
            out=psrowb[:], in_=psrow[:]))
        P.op("vector", lambda e, H: e.tensor_copy(out=X["t1c"][:],
                                                  in_=X["mpp"][:]))
        P.op("vector", lambda e, H: e.memset(X["t2c"][:], 0.0), "s_cw")
        cw()
        for jb in range(11, 5, -1):
            P.op("vector", lambda e, H, jb=jb: e.tensor_scalar(
                out=X["lnr"][:], in0=X["t1c"][:], scalar1=float(1 << jb),
                scalar2=None, op0=OP.is_ge), "s_cw")
            cw()
            P.op("vector", lambda e, H, jb=jb: e.scalar_tensor_tensor(
                out=X["t1c"][:], in0=X["lnr"][:], scalar=float(-(1 << jb)),
                in1=X["t1c"][:], op0=OP.mult, op1=OP.add), "s_cw")
            cw()
            P.op("vector", lambda e, H, jb=jb: e.scalar_tensor_tensor(
                out=X["t2c"][:], in0=X["lnr"][:], scalar=float(1 << (jb - 6)),
                in1=X["t2c"][:], op0=OP.mult, op1=OP.add), "s_cw")
            cw()
        P.op("vector", lambda e, H: e.tensor_copy(out=X["mhib"][:],
                                                  in_=X["t2c"][:]))
        P.op("vector", lambda e, H: e.tensor_copy(out=X["mlob"][:],
                                                  in_=X["t1c"][:]), "s_dve")
        P.wait("tensor", "s_dve", P.n("s_dve"))
        for q in range(NKQ):
            P.op("tensor", lambda e, H, q=q: e.matmul(
                psSb(0)[:, :], X["onesb"][0:1, :],
                psrowb[:, q * 512:(q + 1) * 512], start=True, stop=True),
                "s_pe")
            P.wait("vector", "s_pe", P.n("s_pe"))
            P.op("vector", lambda e, H, q=q: e.tensor_scalar(
                out=m12[:, q * 512:(q + 1) * 512], in0=psSb(0)[:, :],
                scalar1=X["iotapf"][:], scalar2=None, op0=OP.is_equal),
                "s_dve")
            P.wait("tensor", "s_dve", P.n("s_dve"))
            P.op("tensor", lambda e, H, q=q: e.matmul(
                psSb(1)[0:1, :], X["mhib"][:], m12[:, q * 512:(q + 1) * 512],
                start=True, stop=True), "s_pe")
            P.wait("vector", "s_pe", P.n("s_pe"))
            P.op("vector", lambda e, H, q=q: e.tensor_copy(
                out=workA[:, q * 512:(q + 1) * 512], in_=psSb(1)[0:1, :]),
                "s_dve")
            P.wait("tensor", "s_dve", P.n("s_dve"))
            P.op("tensor", lambda e, H, q=q: e.matmul(
                psSb(1)[0:1, :], X["mlob"][:], m12[:, q * 512:(q + 1) * 512],
                start=True, stop=True), "s_pe")
            P.wait("vector", "s_pe", P.n("s_pe"))
            P.op("vector", lambda e, H, q=q: e.tensor_copy(
                out=workB[:, q * 512:(q + 1) * 512], in_=psSb(1)[0:1, :]),
                "s_dve")
        P.wait("vector", "s_dve", P.n("s_dve"))
        P.op("vector", lambda e, H: e.scalar_tensor_tensor(
            out=workB[:], in0=workA[:], scalar=64.0, in1=workB[:],
            op0=OP.mult, op1=OP.add), "s_cw")
        cw()
        P.op("vector", lambda e, H: e.tensor_tensor(
            out=workB[:], in0=iotaf2[0:1, :], in1=workB[:], op=OP.subtract),
            "s_cw")
        cw()
        P.op("vector", lambda e, H: e.tensor_copy(
            out=jrowb[:], in_=workB[:]))
        P.op("vector", lambda e, H: e.tensor_copy(
            out=X["tpackb"][:], in_=fpV[:, 0:128]), "s_dve")
        P.wait("tensor", "s_dve", P.n("s_dve"))
        for q in range(NKQ):
            P.op("tensor", lambda e, H, q=q: e.matmul(
                psSb(0)[:, :], X["onesb"][0:1, :],
                jrowb[:, q * 512:(q + 1) * 512], start=True, stop=True),
                "s_pe")
            P.wait("vector", "s_pe", P.n("s_pe"))
            P.op("vector", lambda e, H, q=q: e.tensor_scalar(
                out=jhT[:], in0=psSb(0)[:, :], scalar1=X["iotapf"][:],
                scalar2=None, op0=OP.is_equal), "s_dve")
            P.wait("tensor", "s_dve", P.n("s_dve"))
            P.op("tensor", lambda e, H, q=q: e.matmul(
                psSb(1)[:, :], X["tpackb"][:], m12[:, q * 512:(q + 1) * 512],
                start=True, stop=True), "s_pe")
            P.wait("vector", "s_pe", P.n("s_pe"))
            P.op("vector", lambda e, H, q=q: e.tensor_tensor(
                out=r2T[:], in0=jhT[:], in1=psSb(1)[:, :], op=OP.mult),
                "s_dve")
            P.wait("tensor", "s_dve", P.n("s_dve"))
            P.op("tensor", lambda e, H, q=q: e.matmul(
                psSb(1)[0:1, :], X["onesb"][:, 0:1], r2T[:], start=True,
                stop=True), "s_pe")
            P.wait("vector", "s_pe", P.n("s_pe"))
            P.op("vector", lambda e, H, q=q: e.scalar_tensor_tensor(
                out=workB[:, q * 512:(q + 1) * 512],
                in0=psrow[:, q * 512:(q + 1) * 512], scalar=128.0,
                in1=psSb(1)[0:1, :], op0=OP.mult, op1=OP.add), "s_dve")
            P.wait("tensor", "s_dve", P.n("s_dve"))
        P.wait("vector", "s_dve", P.n("s_dve"))
        P.op("vector", lambda e, H: e.tensor_copy(
            out=idrowu[:], in_=workB[:]), "s_dve")
        id_done = P.n("s_dve")
        P.wait("scalar", "s_dve", id_done)
        P.op("scalar", lambda e, H: e.dma_start(
            out=d_ids.rearrange("(o k) c -> o (k c)", o=1), in_=idrowu[:]),
            "s_adma", 16)
        idw = P.n("s_adma")
        P.wait("sync", "s_adma", idw)
        for j in range(NSL):
            P.op("sync", lambda e, H, j=j: e.dma_start(
                out=X["idtiles"][:, j:j + 1],
                in_=d_ids[j * 128:(j + 1) * 128, :]), "s_in", 16)
        idt = P.n("s_in")
        if dbg:
            P.wait("scalar", "s_adma", idw)
            P.op("scalar", lambda e, H: e.dma_start(
                out=dbg_t["ids"][:, :], in_=d_ids[:, :]), "s_st", 16)

        P.wait("gpsimd", "s_in", idt)
        P.wait("gpsimd", "s_cc", cc_x1)
        cc_delay(P, X)
        gtoks = ptl
        P.wait("gpsimd", "s_dve", x1_done)
        for j in range(NSL):
            P.op("gpsimd", lambda e, H, j=j: e.indirect_dma_start(
                out=gtoks[:, j * D:(j + 1) * D], out_offset=None,
                in_=d_x1_out[:, :],
                in_offset=IndirectOffsetOnAxis(ap=X["idtiles"][:, j:j + 1],
                                               axis=0),
                bounds_check=NTOK - 1, oob_is_err=False), "s_gdma", 16)
        tg = P.n("s_gdma")
        if dbg:
            P.wait("gpsimd", "s_gdma", tg)
            for j in range(NSL):
                P.op("gpsimd", lambda e, H, j=j: e.dma_start(
                    out=dbg_t["toks"][j * 128:(j + 1) * 128, :],
                    in_=gtoks[:, j * D:(j + 1) * D]), "s_gdma", 16)
        P.wait("tensor", "s_gdma", tg)
        P.wait("tensor", "s_dve", id_done)
        tokens_fm = h1a
        for j in range(NSL):
            tb = j % 2
            if j >= 2:
                P.wait("tensor", "s_dve", P.n("s_dve"))
            for d in range(ND):
                P.op("tensor", lambda e, H, j=j, d=d, tb=tb: e.transpose(
                    psTb(tb)[:, d * 128:(d + 1) * 128],
                    gtoks[:, j * D + d * 128:j * D + (d + 1) * 128],
                    X["identb"][:]), sem="s_pe" if d == ND - 1 else None)
            tp2 = P.n("s_pe")
            P.wait("vector", "s_pe", tp2)
            for d in range(ND):
                P.op("vector", lambda e, H, j=j, d=d, tb=tb: e.tensor_copy(
                    out=tokens_fm[:, d * KSEL + j * 128:
                                  d * KSEL + (j + 1) * 128],
                    in_=psTb(tb)[:, d * 128:(d + 1) * 128]),
                    sem="s_dve" if d == ND - 1 else None)
        tfm_done = P.n("s_dve")

        P.wait("tensor", "s_dve", tfm_done)
        for q in range(NKQ):
            if q >= 1:
                P.wait("tensor", "s_act", P.n("s_act"))
            for d in range(ND):
                P.op("tensor", lambda e, H, q=q, d=d: e.matmul(
                    psSb(0)[0:E, :], X["wrb"][:, d * E:(d + 1) * E],
                    tokens_fm[:, d * KSEL + q * 512:d * KSEL + (q + 1) * 512],
                    start=(d == 0), stop=(d == ND - 1)),
                    sem="s_pe" if d == ND - 1 else None)
            gq = P.n("s_pe")
            P.wait("scalar", "s_pe", gq)
            P.op("scalar", lambda e, H, q=q: e.activation(
                out=gexp[:, q * 512:(q + 1) * 512], in_=psSb(0)[0:E, :],
                func=AF.Exp, bias=X["brt"][:], scale=1.0), "s_act")
        ge_done = P.n("s_act")
        P.wait("tensor", "s_act", ge_done)
        for q in range(NKQ):
            P.op("tensor", lambda e, H, q=q: e.matmul(
                psSb(1)[0:1, :], X["ones8f"][:, 0:1],
                gexp[:, q * 512:(q + 1) * 512], start=True, stop=True))
            P.op("tensor", lambda e, H, q=q: e.matmul(
                psSb(0)[0:1, :], X["oneh"][:],
                gexp[:, q * 512:(q + 1) * 512], start=True, stop=True), "s_pe")
            g2 = P.n("s_pe")
            P.wait("vector", "s_pe", g2)
            P.op("vector", lambda e, H, q=q: e.reciprocal(
                out=grecip[:, q * 512:(q + 1) * 512], in_=psSb(1)[0:1, :]),
                "s_cw")
            cw()
            P.op("vector", lambda e, H, q=q: e.tensor_tensor(
                out=gslotr[:, q * 512:(q + 1) * 512],
                in0=grecip[:, q * 512:(q + 1) * 512], in1=psSb(0)[0:1, :],
                op=OP.mult), "s_dve")
            P.wait("tensor", "s_dve", P.n("s_dve"))
        g_done = P.n("s_dve")
        P.wait("scalar", "s_dve", g_done)
        P.op("scalar", lambda e, H: e.dma_start(
            out=d_grow[:, :], in_=gslotr[:]), "s_adma", 16)
        gw = P.n("s_adma")
        P.wait("sync", "s_adma", gw)
        for j in range(NSL):
            P.op("sync", lambda e, H, j=j: e.dma_start(
                out=X["gslot_tm"][:, j:j + 1],
                in_=d_grow.rearrange("o (a b) -> (o a) b", b=1)[
                    j * 128:(j + 1) * 128, :]), "s_in", 16)
        gtm = P.n("s_in")
        if dbg:
            P.wait("scalar", "s_adma", gw)
            P.op("scalar", lambda e, H: e.dma_start(
                out=dbg_t["gslot"][:, :], in_=d_grow[:, :]), "s_st", 16)

        P.wait("vector", "s_in", gtm)
        P.wait("scalar", "s_dve", g_done)
        P.wait("gpsimd", "s_zero", P.n("s_zero"))
        dbank = [lambda i=i: psKb(i) for i in range(2)] + \
                [lambda i=i: psPb(i) for i in range(2)] + \
                [lambda i=i: psSb(i) for i in range(2)] + \
                [lambda i=i: psTfb(i) for i in range(2)]

        for tq in range(NKQ):
            if tq >= 1:
                P.wait("tensor", "s_dve", P.n("s_dve"))
            cast_marks = []
            for fc in range(NF):
                rows = FCH[fc]
                hb = fc % 2
                if fc >= 2:
                    P.wait("sync", "s_gp", cast_marks[fc - 2])
                P.op("sync", lambda e, H, fc=fc, hb=hb, rows=rows:
                     e.dma_start(
                         out=X["wstg"][:, hb * 1024:hb * 1024 + ND * rows],
                         in_=wu.rearrange("(kc p) f -> p kc f", p=128)[
                             :, :, fc * 128:fc * 128 + rows]), "s_in", 16)
                wl = P.n("s_in")
                P.wait("gpsimd", "s_in", wl)
                P.wait("gpsimd", "s_pe", P.n("s_pe"))
                P.op("gpsimd", lambda e, H, fc=fc, hb=hb, rows=rows:
                     e.tensor_copy(
                         out=X["wub2"][:, hb * 1024:hb * 1024 + ND * rows],
                         in_=X["wstg"][:, hb * 1024:hb * 1024 + ND * rows]),
                     "s_gp")
                wc = P.n("s_gp")
                cast_marks.append(wc)
                P.wait("tensor", "s_gp", wc)
                if fc >= 2:
                    P.wait("tensor", "s_act", P.n("s_act"))
                for d in range(ND):
                    P.op("tensor", lambda e, H, tq=tq, fc=fc, d=d, hb=hb,
                         rows=rows: e.matmul(
                             psKb(fc % 2)[0:rows, :],
                             X["wub2"][:, hb * 1024 + d * rows:
                                       hb * 1024 + (d + 1) * rows],
                             tokens_fm[:, d * KSEL + tq * 512:
                                       d * KSEL + (tq + 1) * 512],
                             start=(d == 0), stop=(d == ND - 1)),
                         sem="s_pe" if d == ND - 1 else None)
                um = P.n("s_pe")
                P.wait("scalar", "s_pe", um)
                P.op("scalar", lambda e, H, fc=fc, rows=rows: e.activation(
                    out=mid[0:rows, fc * 512:(fc + 1) * 512],
                    in_=psKb(fc % 2)[0:rows, :], func=AF.Silu,
                    bias=X["but"][0:rows, fc:fc + 1], scale=1.0), "s_act")
            silu_done = P.n("s_act")
            if dbg and tq == 0:
                P.op("scalar", lambda e, H: e.dma_start(
                    out=dbg_t["mid0"][:, :], in_=mid[:, :]), "s_adma", 16)

            P.wait("tensor", "s_act", silu_done)
            for fc in range(NF):
                rows = FCH[fc]
                hb = fc % 2
                P.wait("sync", "s_gp", P.n("s_gp"))
                P.op("sync", lambda e, H, fc=fc, hb=hb, rows=rows:
                     e.dma_start(
                         out=X["wstg"][:, hb * 1024:hb * 1024 + 1024][
                             0:rows, :],
                         in_=wd[fc * 128:fc * 128 + rows, :]), "s_in", 16)
                wl = P.n("s_in")
                P.wait("gpsimd", "s_in", wl)
                P.wait("gpsimd", "s_pe", P.n("s_pe"))
                P.op("gpsimd", lambda e, H, fc=fc, hb=hb, rows=rows:
                     e.tensor_copy(
                         out=X["wdb2"][0:rows, hb * 1024:hb * 1024 + 1024],
                         in_=X["wstg"][:, hb * 1024:hb * 1024 + 1024][
                             0:rows, :]), "s_gp")
                wc2 = P.n("s_gp")
                P.wait("tensor", "s_gp", wc2)
                for st4 in range(4):
                    for hh in range(2):
                        P.op("tensor", lambda e, H, tq=tq, fc=fc, st4=st4,
                             hh=hh, rows=rows, hb=hb: e.matmul(
                                 dbank[st4 * 2 + hh](),
                                 mid[0:rows, fc * 512 + st4 * 128:
                                     fc * 512 + (st4 + 1) * 128],
                                 X["wdb2"][0:rows, hb * 1024 + hh * 512:
                                           hb * 1024 + (hh + 1) * 512],
                                 start=(fc == 0), stop=(fc == NF - 1)),
                             sem="s_pe" if (st4 == 3 and hh == 1) else None)
            dn = P.n("s_pe")
            P.wait("vector", "s_pe", dn)
            for st4 in range(4):
                j = tq * 4 + st4
                for hh in range(2):
                    P.op("vector", lambda e, H, st4=st4, hh=hh: e.tensor_tensor(
                        out=X["stsc"][:, hh * 512:(hh + 1) * 512],
                        in0=dbank[st4 * 2 + hh](),
                        in1=X["bdt"][:, hh * 512:(hh + 1) * 512], op=OP.add),
                        "s_cw")
                    cw()
                    P.op("vector", lambda e, H, j=j, hh=hh: e.tensor_scalar(
                        out=wrow[:, hh * 512:(hh + 1) * 512],
                        in0=X["stsc"][:, hh * 512:(hh + 1) * 512],
                        scalar1=X["gslot_tm"][:, j:j + 1], scalar2=None,
                        op0=OP.mult), "s_dve")
                wv = P.n("s_dve")
                P.wait("gpsimd", "s_dve", wv)
                P.op("gpsimd", lambda e, H, j=j: e.indirect_dma_start(
                    out=d_y_in[:, :],
                    out_offset=IndirectOffsetOnAxis(
                        ap=X["idtiles"][:, j:j + 1], axis=0),
                    in_=wrow[:, 0:D], in_offset=None,
                    bounds_check=NTOK - 1, oob_is_err=False), "s_gdma", 16)
                P.wait("vector", "s_gdma", P.n("s_gdma"))
        ffn_done = P.n("s_gdma")

        P.wait("gpsimd", "s_gdma", ffn_done)
        if dbg:
            P.op("gpsimd", lambda e, H: e.dma_start(
                out=dbg_t["yin"][:, :], in_=d_y_in[:, :]), "s_gdma", 16)
            P.wait("gpsimd", "s_gdma", P.n("s_gdma"))
        P.op("gpsimd", lambda e, H: e.collective_compute(
            "ReduceScatter", OP.add, replica_groups=[list(range(NCORE))],
            ins=[d_y_in.opt()], outs=[d_y_out.opt()]), "s_cc", 1)
        P.op("gpsimd", lambda e, H: e.collective_compute(
            "AllReduce", OP.add, replica_groups=[list(range(NCORE))],
            ins=[d_bar_in.opt()], outs=[d_bar_out.opt()]), "s_cc", 1)
        cc_y = P.n("s_cc")
        P.wait("gpsimd", "s_cc", cc_y)
        cc_delay(P, X)
        P.op("gpsimd", lambda e, H: e.memset(X["zrow"][:, 0:512], 0.0), "s_gp")
        ccy_rdy = P.n("s_gp")
        if dbg:
            P.wait("gpsimd", "s_cc", cc_y)
            P.op("gpsimd", lambda e, H: e.dma_start(
                out=dbg_t["y"][:, :], in_=d_y_out[:, :]), "s_gdma", 16)
        P.wait("sync", "s_cc", cc_y)
        P.wait("sync", "s_gp", ccy_rdy)
        for j in range(NT):
            P.wait("sync", "s_dve", P.n("s_dve"))
            P.op("sync", lambda e, H, j=j: e.dma_start(
                out=x1re[:, 0:D], in_=d_x1_in[j * 128:(j + 1) * 128, :]),
                "s_in", 16)
            P.op("sync", lambda e, H, j=j: e.dma_start(
                out=yre[:, 0:D], in_=d_y_out[j * 128:(j + 1) * 128, :]),
                "s_in", 16)
            l2 = P.n("s_in")
            P.wait("vector", "s_in", l2)
            P.wait("vector", "s_st", P.n("s_st"))
            P.op("vector", lambda e, H: e.tensor_tensor(
                out=accq[:, 0:D], in0=x1re[:, 0:D], in1=yre[:, 0:D],
                op=OP.add), "s_cw")
            cw()
            P.op("vector", lambda e, H: e.tensor_scalar(
                out=X["stsc"][:], in0=accq[:, 0:D], scalar1=1.0, scalar2=0.0,
                op0=OP.mult, op1=OP.add, accum_out=X["lnm"][:]), "s_dve")
            aq = P.n("s_dve")
            P.wait("scalar", "s_dve", aq)
            P.op("scalar", lambda e, H: e.activation(
                out=X["stsc"][:], in_=accq[:, 0:D], func=AF.Square, bias=0.0,
                scale=1.0), "s_act")
            P.wait("vector", "s_act", P.n("s_act"))
            P.op("vector", lambda e, H: e.tensor_scalar(
                out=X["stsc"][:], in0=X["stsc"][:], scalar1=1.0, scalar2=0.0,
                op0=OP.mult, op1=OP.add, accum_out=X["lnv"][:]), "s_cw")
            cw()
            P.op("vector", lambda e, H: e.tensor_scalar(
                out=X["lnm"][:], in0=X["lnm"][:], scalar1=1.0 / D,
                scalar2=None, op0=OP.mult), "s_cw")
            cw()
            P.op("vector", lambda e, H: e.tensor_tensor(
                out=X["lnr"][:], in0=X["lnm"][:], in1=X["lnm"][:],
                op=OP.mult), "s_cw")
            cw()
            P.op("vector", lambda e, H: e.scalar_tensor_tensor(
                out=X["lnv"][:], in0=X["lnv"][:], scalar=1.0 / D,
                in1=X["lnr"][:], op0=OP.mult, op1=OP.subtract), "s_dve")
            v2 = P.n("s_dve")
            P.wait("scalar", "s_dve", v2)
            P.op("scalar", lambda e, H: e.activation(
                out=X["lnv"][:], in_=X["lnv"][:], func=AF.Ln, bias=X["epsc"][:],
                scale=1.0), "s_cs")
            cs()
            P.op("scalar", lambda e, H: e.activation(
                out=X["lnv"][:], in_=X["lnv"][:], func=AF.Exp, bias=0.0,
                scale=-0.5), "s_act")
            a2 = P.n("s_act")
            P.wait("vector", "s_act", a2)
            P.op("vector", lambda e, H: e.tensor_scalar(
                out=osta[:, 0:D], in0=accq[:, 0:D], scalar1=X["lnm"][:],
                scalar2=X["lnv"][:], op0=OP.subtract, op1=OP.mult), "s_dve")
            o2 = P.n("s_dve")
            P.wait("scalar", "s_dve", o2)
            P.op("scalar", lambda e, H, j=j: e.dma_start(
                out=out[j * 128:(j + 1) * 128, :], in_=osta[:, 0:D]),
                "s_st", 16)
        P.wait("scalar", "s_st", P.n("s_st"))
        P.wait("gpsimd", "s_st", P.n("s_st"))
        P.wait("gpsimd", "s_gdma", P.n("s_gdma"))
        P.wait("vector", "s_st", P.n("s_st"))

        with PSE:
            with contextlib.ExitStack() as semctx:
                handles = {}
                for name in P.sems:
                    handles[name] = semctx.enter_context(nc.semaphore(name))
                block = semctx.enter_context(nc.Block())
                eng_items = {k: [] for k in
                             ["sync", "tensor", "scalar", "vector", "gpsimd"]}
                for eng, fn in P.items:
                    eng_items[eng].append(fn)

                @block.sync
                def _(eng):
                    for fn in eng_items["sync"]:
                        fn(eng, handles)

                @block.tensor
                def _(eng):
                    for fn in eng_items["tensor"]:
                        fn(eng, handles)

                @block.scalar
                def _(eng):
                    for fn in eng_items["scalar"]:
                        fn(eng, handles)

                @block.vector
                def _(eng):
                    for fn in eng_items["vector"]:
                        fn(eng, handles)

                @block.gpsimd
                def _(eng):
                    for fn in eng_items["gpsimd"]:
                        fn(eng, handles)

    return nc


def _prep_inputs(inputs, seq):
    x = np.asarray(inputs["L0_attn_x"])
    xt = np.ascontiguousarray(x.transpose(0, 2, 1))
    in_maps = []
    for c in range(NCORE):
        b, q = c // 4, c % 4
        m = {"xT": np.ascontiguousarray(xt[b][:, q * seq:(q + 1) * seq])}
        names = [("L0_attn_Wz0", "L0_attn_Wh0", "L0_attn_bz0", "L0_attn_bh0"),
                 ("L0_attn_Wz1", "L0_attn_Wh1", "L0_attn_bz1", "L0_attn_bh1")]
        bz2 = np.zeros((128, 2 * ND), np.float32)
        bh2 = np.zeros((128, 2 * ND), np.float32)
        for l, (wz_, wh_, bz_, bh_) in enumerate(names):
            m[f"wzT{l}"] = np.ascontiguousarray(np.asarray(inputs[wz_]).T)
            m[f"whT{l}"] = np.ascontiguousarray(np.asarray(inputs[wh_]).T)
            bz2[:, l * ND:(l + 1) * ND] = np.asarray(
                inputs[bz_]).reshape(ND, 128).T
            bh2[:, l * ND:(l + 1) * ND] = np.asarray(
                inputs[bh_]).reshape(ND, 128).T
        m["bzin"] = bz2
        m["bhin"] = bh2
        m["wr"] = np.ascontiguousarray(np.asarray(inputs["L0_ffn_Wr"]))
        m["br"] = np.ascontiguousarray(
            np.asarray(inputs["L0_ffn_br"]).reshape(E, 1))
        m["wu"] = np.ascontiguousarray(np.asarray(inputs["L0_ffn_W_up"])[c // 2])
        bu_ = np.zeros((128, NF), np.float32)
        buv = np.asarray(inputs["L0_ffn_b_up"])[c // 2]
        for fc in range(NF):
            bu_[0:FCH[fc], fc] = buv[fc * 128:fc * 128 + FCH[fc]]
        m["bu"] = bu_
        m["wd"] = np.ascontiguousarray(np.asarray(inputs["L0_ffn_W_down"])[c])
        m["bd"] = np.ascontiguousarray(
            np.asarray(inputs["L0_ffn_b_down"])[c].reshape(1, D))
        cm = np.zeros((128, NCORE), np.float32)
        for r in range(4 * b, 4 * b + q):
            cm[:, r] = 1.0
        m["cmask"] = cm
        m["rowsel"] = (np.arange(NCORE, dtype=np.uint32) * E + c).reshape(E, 1)
        oh = np.zeros((E, 1), np.float32)
        oh[c, 0] = 1.0
        m["onehot"] = oh
        in_maps.append(m)
    return in_maps


def kernel(**inputs):
    seq = 2048
    nc = build_program(seq=seq, dbg=False)
    in_maps = _prep_inputs(inputs, seq)
    res = run_bass_kernel_spmd(nc, in_maps, core_ids=list(range(NCORE)))
    full = np.concatenate([r["out"] for r in res.results], axis=0)
    return full.reshape(2, 8192, 1024)


# version 03
# speedup vs baseline: 1.2412x; 1.2412x over previous
import contextlib
import numpy as np
import concourse.bass as bass
import concourse.mybir as mybir
from concourse.bass import IndirectOffsetOnAxis
from concourse.bass_utils import run_bass_kernel_spmd
from concourse.alu_op_type import AluOpType as OP

F32 = mybir.dt.float32
BF16 = mybir.dt.bfloat16
U32 = mybir.dt.uint32
I32 = mybir.dt.int32
AF = mybir.ActivationFunctionType

D = 1024
ND = 8
E = 8
NCORE = 8
F_UP = 2730
NF = 22
FCH = [128] * 21 + [42]
EPS = 1e-5


class Prog:
    def __init__(self):
        self.items = []
        self.sems = {}

    def sem(self, name):
        self.sems.setdefault(name, 0)

    def n(self, name):
        return self.sems[name]

    def op(self, eng, emitfn, sem=None, k=1, emit_k=None):
        if sem is not None:
            self.sems[sem] = self.sems.get(sem, 0) + k
            tgt = self.sems[sem]
            ek = k if emit_k is None else emit_k

            def fn(e, H, emitfn=emitfn, sem=sem, ek=ek):
                emitfn(e, H).then_inc(H[sem], ek)
            self.items.append((eng, fn))
            return tgt
        self.items.append((eng, lambda e, H, emitfn=emitfn: emitfn(e, H)))
        return None

    def wait(self, eng, sem, val=None):
        if val is None:
            val = self.sems.get(sem, 0)
        if val > 0:
            self.items.append(
                (eng, lambda e, H, sem=sem, val=val: e.wait_ge(H[sem], val)))
        return val


def build_program(seq=2048, dbg=False):
    NT = seq // 128
    NQ = seq // 512
    NTOK = NCORE * seq
    KSEL = NTOK // E
    NSL = KSEL // 128
    NPQ = NTOK // 128
    BIS_F = NTOK // 64
    NKQ = KSEL // 512
    PACKW = 2 * NPQ
    NBITS = max(1, (NPQ - 1).bit_length())
    MIDW = NF * 512

    nc = bass.Bass("TRN2", target_bir_lowering=False, debug=False,
                   detect_race_conditions=False)

    xT = nc.dram_tensor("xT", [D, seq], F32, kind="ExternalInput").ap()
    wzT = [nc.dram_tensor(f"wzT{l}", [128, ND * D], F32,
                          kind="ExternalInput").ap() for l in range(2)]
    whT = [nc.dram_tensor(f"whT{l}", [128, ND * D], F32,
                          kind="ExternalInput").ap() for l in range(2)]
    bzin = nc.dram_tensor("bzin", [128, 2 * ND], F32, kind="ExternalInput").ap()
    bhin = nc.dram_tensor("bhin", [128, 2 * ND], F32, kind="ExternalInput").ap()
    wr = nc.dram_tensor("wr", [D, E], F32, kind="ExternalInput").ap()
    br = nc.dram_tensor("br", [E, 1], F32, kind="ExternalInput").ap()
    wu = nc.dram_tensor("wu", [128, NF * ND * 128], F32,
                        kind="ExternalInput").ap()
    bu = nc.dram_tensor("bu", [128, NF], F32, kind="ExternalInput").ap()
    wd = nc.dram_tensor("wd", [F_UP, D], F32, kind="ExternalInput").ap()
    bd = nc.dram_tensor("bd", [1, D], F32, kind="ExternalInput").ap()
    cmask = nc.dram_tensor("cmask", [128, NCORE], F32, kind="ExternalInput").ap()
    rowsel = nc.dram_tensor("rowsel", [E, 1], U32, kind="ExternalInput").ap()
    onehot = nc.dram_tensor("onehot", [E, 1], F32, kind="ExternalInput").ap()
    out = nc.dram_tensor("out", [seq, D], F32, kind="ExternalOutput").ap()
    dbg_t = {}
    if dbg:
        dbg_t["x1tm"] = nc.dram_tensor("dbg_x1tm", [seq, D], BF16,
                                       kind="ExternalOutput").ap()
        dbg_t["probs"] = nc.dram_tensor("dbg_probs", [E, seq], F32,
                                        kind="ExternalOutput").ap()
        dbg_t["tau"] = nc.dram_tensor("dbg_tau", [64, 4], F32,
                                      kind="ExternalOutput").ap()
        dbg_t["ids"] = nc.dram_tensor("dbg_ids", [KSEL, 1], U32,
                                      kind="ExternalOutput").ap()
        dbg_t["gslot"] = nc.dram_tensor("dbg_gslot", [1, KSEL], F32,
                                        kind="ExternalOutput").ap()
        dbg_t["y"] = nc.dram_tensor("dbg_y", [seq, D], BF16,
                                    kind="ExternalOutput").ap()
        dbg_t["toks"] = nc.dram_tensor("dbg_toks", [KSEL, D], BF16,
                                       kind="ExternalOutput").ap()
        dbg_t["mid0"] = nc.dram_tensor("dbg_mid0", [128, NF * 512], BF16,
                                       kind="ExternalOutput").ap()
        dbg_t["yin"] = nc.dram_tensor("dbg_yin", [NTOK, D], BF16,
                                      kind="ExternalOutput").ap()
        dbg_t["xbf"] = nc.dram_tensor("dbg_xbf", [128, ND * seq], BF16,
                                      kind="ExternalOutput").ap()
        dbg_t["h1"] = nc.dram_tensor("dbg_h1", [128, ND * seq], BF16,
                                     kind="ExternalOutput").ap()
        dbg_t["car"] = nc.dram_tensor("dbg_car", [128, NCORE * 32], F32,
                                      kind="ExternalOutput").ap()
        dbg_t["y1"] = nc.dram_tensor("dbg_y1", [128, ND * seq], BF16,
                                     kind="ExternalOutput").ap()
        dbg_t["hst"] = nc.dram_tensor("dbg_hst", [128, 2 * ND], F32,
                                      kind="ExternalOutput").ap()
        dbg_t["car0"] = nc.dram_tensor("dbg_car0", [128, NCORE * 32], F32,
                                       kind="ExternalOutput").ap()
        dbg_t["cmsk"] = nc.dram_tensor("dbg_cmsk", [128, NCORE], F32,
                                       kind="ExternalOutput").ap()

    d_car_in = nc.dram_tensor("d_car_in", [128, 32], F32).ap()
    d_car_out = nc.dram_tensor("d_car_out", [128 * NCORE, 32], F32).ap()
    d_probs_in = nc.dram_tensor("d_probs_in", [E, seq], F32).ap()
    d_probs_out = nc.dram_tensor("d_probs_out", [E * NCORE, seq], F32).ap()
    d_x1_in = nc.dram_tensor("d_x1_in", [seq, D], BF16).ap()
    d_x1_out = nc.dram_tensor("d_x1_out", [NTOK, D], BF16).ap()
    d_y_in = nc.dram_tensor("d_y_in", [NTOK, D], BF16).ap()
    d_y_out = nc.dram_tensor("d_y_out", [seq, D], BF16).ap()
    d_escore = nc.dram_tensor("d_escore", [NTOK, 1], F32).ap()
    d_ids = nc.dram_tensor("d_ids", [KSEL, 1], U32).ap()
    d_rows = nc.dram_tensor("d_rows", [2, NTOK], F32).ap()
    d_grow = nc.dram_tensor("d_grow", [1, KSEL], F32).ap()
    d_bar_in = nc.dram_tensor("d_bar_in", [8, 4], F32).ap()
    d_bar_out = nc.dram_tensor("d_bar_out", [8, 4], F32).ap()

    P = Prog()
    for s in ["s_in", "s_pe", "s_act", "s_dve", "s_gp", "s_gdma", "s_adma",
              "s_zero", "s_cc", "s_st", "s_cw", "s_cs"]:
        P.sem(s)

    def cc_delay(P, X, n=150):
        for _ in range(n):
            P.op("gpsimd", lambda e, H: e.memset(X["zrow"][:, 0:512], 0.0))

    def cw():
        P.wait("vector", "s_cw", P.n("s_cw"))

    def cs():
        P.wait("scalar", "s_cs", P.n("s_cs"))

    ctx = contextlib.ExitStack()
    X = {}

    def T(name, shape, dt):
        X[name] = ctx.enter_context(nc.sbuf_tensor(name, shape, dt))
        return X[name]

    T("xbf", [128, ND * seq], BF16)
    T("h1a", [128, ND * seq], BF16)
    T("hlc", [128, ND * seq], BF16)
    T("ptl", [128, ND * seq], BF16)
    T("cbuf", [128, seq], BF16)
    T("vbuf", [128, seq], BF16)
    T("spb", [128, 1024], BF16)
    T("sigk", [128, 512], BF16)
    T("wstg", [128, 2048], F32)
    T("wzb", [128, D], BF16)
    T("whb", [128, D], BF16)
    T("wub2", [128, 2048], BF16)
    T("wdb2", [128, 2048], BF16)
    T("iota128f", [128, 128], F32)
    T("iotapf", [128, 1], F32)
    T("iotapi", [128, 1], I32)
    T("identb", [128, 128], BF16)
    T("trilb", [128, 128], BF16)
    T("onesb", [128, 128], BF16)
    T("ones64", [64, 64], BF16)
    T("ones8f", [E, 1], F32)
    T("bzt", [128, 2 * ND], F32)
    T("bht", [128, 2 * ND], F32)
    T("bzn", [128, 2 * ND], F32)
    T("bh05", [128, 2 * ND], F32)
    T("wrb", [128, ND * E], BF16)
    T("brt", [E, 1], F32)
    T("oneh", [E, 1], F32)
    T("rsel", [E, 1], U32)
    T("cmsk", [128, NCORE], F32)
    T("carst", [128, 32], F32)
    T("carall", [128, NCORE * 32 + 64], F32)
    T("hstart", [128, 128], F32)
    T("tmpc", [128, 64], F32)
    T("hsB", [128, 128], F32)
    T("mstat", [128, NT], F32)
    T("rstdv", [128, NT], F32)
    T("mrv", [128, NT], F32)
    T("stsc", [128, 1024], F32)
    T("zrow", [128, 1024], BF16)
    T("zero1", [128, 1], F32)
    T("zero1b", [128, 1], BF16)
    T("sc64", [64, BIS_F], F32)
    T("j64", [64, BIS_F], BF16)
    T("lo", [64, 1], F32)
    T("hi", [64, 1], F32)
    T("midv", [64, 1], F32)
    T("cnt", [64, 1], F32)
    T("cntb", [64, 1], BF16)
    T("bmask", [64, 1], I32)
    T("nbmask", [64, 1], I32)
    T("mtot", [64, 1], F32)
    T("rneed", [64, 1], F32)
    T("tau128", [128, 1], F32)
    T("r128", [128, 1], F32)
    T("npp", [128, 1], F32)
    T("mpp", [128, 1], F32)
    T("mppb", [128, 1], BF16)
    T("mhib", [128, 1], BF16)
    T("mlob", [128, 1], BF16)
    T("t1c", [128, 1], F32)
    T("t2c", [128, 1], F32)
    T("tpackb", [128, 128], BF16)
    T("idtiles", [128, NSL], U32)
    T("gslot_tm", [128, NSL], F32)
    T("but", [128, NF], F32)
    T("bdt", [128, D], F32)
    T("lnm", [128, 1], F32)
    T("lnv", [128, 1], F32)
    T("lnr", [128, 1], F32)
    T("epsc", [128, 1], F32)

    with ctx:
        xbf, h1a, hlc, ptl = X["xbf"], X["h1a"], X["hlc"], X["ptl"]

        def bh2(n):
            return n // 2
        rawsb = xbf[:, bh2(0):bh2(8192)].bitcast(F32)[0:E, :]
        rows8 = xbf[:, bh2(8192):bh2(16384)].bitcast(F32)[0:E, :]
        prsb = xbf[:, bh2(16384):bh2(24576)].bitcast(F32)[0:E, :]
        iotaf2 = xbf[:, bh2(24576):bh2(24576) + 2048].bitcast(F32)
        iotaf2 = xbf[:, bh2(24576):bh2(32768)].bitcast(F32)
        psrow = xbf[:, bh2(0):bh2(8192)].bitcast(F32)[0:1, :]
        workA = xbf[:, bh2(8192):bh2(16384)].bitcast(F32)[0:1, :]
        workB = xbf[:, bh2(16384):bh2(24576)].bitcast(F32)[0:1, :]
        idrowu = xbf[:, bh2(8192):bh2(16384)].bitcast(U32)[0:1, :]
        gexp = xbf[:, bh2(0):bh2(8192)].bitcast(F32)[0:E, :]
        grecip = xbf[:, bh2(16384):bh2(24576)].bitcast(F32)[0:1, :]
        gslotr = xbf[:, bh2(24576):bh2(32768)].bitcast(F32)[0:1, :]
        mid = xbf[:, 0:MIDW]
        _cur = [0]

        def _cv(nbf):
            a, b = _cur[0], _cur[0] + nbf
            _cur[0] = b
            return h1a[:, a:b]
        m12 = _cv(KSEL)
        jhT = _cv(512)
        r2T = _cv(512)
        psrowb = _cv(KSEL)[0:1, :]
        jrowb = _cv(KSEL)[0:1, :]
        mgt = _cv(2 * NPQ).bitcast(F32)
        meq = _cv(2 * NPQ).bitcast(F32)
        msel = _cv(2 * NPQ).bitcast(F32)
        tcum = _cv(2 * NPQ).bitcast(F32)
        scrm = _cv(2 * NPQ).bitcast(F32)
        sc128 = _cv(2 * NPQ).bitcast(F32)
        fpV = _cv(2 * PACKW).bitcast(F32)
        fpM = _cv(2 * PACKW).bitcast(F32)
        fpmv = _cv(2 * PACKW).bitcast(F32)
        fpnm = _cv(2 * PACKW).bitcast(F32)
        fpB = [_cv(2 * PACKW).bitcast(F32) for _ in range(NBITS)]
        fpmvi = _cv(PACKW).bitcast(mybir.dt.uint8)[:, 0:PACKW]
        assert _cur[0] <= ND * seq, _cur[0]
        wrow = ptl[:, 0:1024]
        accq = ptl[:, 1024:1024 + 2048].bitcast(F32)
        x1re = ptl[:, 3072:4096]
        yre = ptl[:, 4096:5120]
        osta = ptl[:, 5120:7168].bitcast(F32)

        PSE = contextlib.ExitStack()
        psK = PSE.enter_context(nc.psum_tensor("psK", [128, 1024], F32))
        psP = PSE.enter_context(nc.psum_tensor("psP", [128, 1024], F32))
        psT = PSE.enter_context(nc.psum_tensor("psT", [128, 2048], BF16))
        psS = PSE.enter_context(nc.psum_tensor("psS", [128, 1024], F32))
        psTf = psT.ap().bitcast(F32)

        def psKb(i):
            return psK[:, i * 512:(i + 1) * 512]

        def psPb(i):
            return psP[:, i * 512:(i + 1) * 512]

        def psSb(i):
            return psS[:, i * 512:(i + 1) * 512]

        def psTb(i):
            return psT[:, i * 1024:(i + 1) * 1024]

        def psTfb(i):
            return psTf[:, i * 512:(i + 1) * 512]

        def dt_(t, d):
            return t[:, d * seq:(d + 1) * seq]

        P.op("sync", lambda e, H: e.dma_start(out=X["bzt"][:], in_=bzin[:, :]),
             "s_in", 16)
        P.op("sync", lambda e, H: e.dma_start(out=X["bht"][:], in_=bhin[:, :]),
             "s_in", 16)
        P.op("sync", lambda e, H: e.dma_start(
            out=X["wstg"][:, 0:ND * E],
            in_=wr.rearrange("(c p) e -> p c e", p=128)), "s_in", 16)
        P.op("sync", lambda e, H: e.dma_start(out=X["brt"][:], in_=br[:, :]),
             "s_in", 16)
        P.op("sync", lambda e, H: e.dma_start(out=X["cmsk"][:], in_=cmask[:, :]),
             "s_in", 16)
        P.op("sync", lambda e, H: e.dma_start(out=X["oneh"][:], in_=onehot[:, :]),
             "s_in", 16)
        P.op("sync", lambda e, H: e.dma_start(out=X["rsel"][:], in_=rowsel[:, :]),
             "s_in", 16)
        P.op("sync", lambda e, H: e.dma_start(out=X["but"][:], in_=bu[:, :]),
             "s_in", 16)
        P.op("sync", lambda e, H: e.dma_start(
            out=X["bdt"][:], in_=bd[0:1, :].partition_broadcast(128)),
            "s_in", 16)
        in0 = P.n("s_in")

        P.op("gpsimd", lambda e, H: e.iota(
            X["iotapi"][:], pattern=[[0, 1]], base=0, channel_multiplier=1))
        P.op("gpsimd", lambda e, H: e.iota(
            X["iota128f"].bitcast(I32)[:], pattern=[[1, 128]], base=0,
            channel_multiplier=0), "s_gp")
        gp0 = P.n("s_gp")

        P.wait("vector", "s_gp", gp0)
        P.op("vector", lambda e, H: e.tensor_copy(out=X["iotapf"][:],
                                                  in_=X["iotapi"][:]))
        P.op("vector", lambda e, H: e.tensor_copy(
            out=X["iota128f"][:], in_=X["iota128f"].bitcast(I32)[:]), "s_cw")
        cw()
        P.op("vector", lambda e, H: e.tensor_tensor(
            out=X["identb"][:], in0=X["iota128f"][:],
            in1=X["iotapf"][:].to_broadcast([128, 128]), op=OP.is_equal))
        P.op("vector", lambda e, H: e.tensor_tensor(
            out=X["trilb"][:], in0=X["iota128f"][:],
            in1=X["iotapf"][:].to_broadcast([128, 128]), op=OP.is_gt))
        P.op("vector", lambda e, H: e.memset(X["onesb"][:], 1.0))
        P.op("vector", lambda e, H: e.memset(X["ones64"][:], 1.0))
        P.op("vector", lambda e, H: e.memset(X["ones8f"][:], 1.0))
        P.op("vector", lambda e, H: e.memset(X["zero1"][:], 0.0))
        P.op("vector", lambda e, H: e.memset(X["zero1b"][:], 0.0))
        P.op("vector", lambda e, H: e.memset(X["epsc"][:], EPS))
        P.op("vector", lambda e, H: e.memset(X["zrow"][:], 0.0))
        P.op("vector", lambda e, H: e.memset(X["carst"][:], 0.0))
        P.op("vector", lambda e, H: e.memset(X["hstart"][:], 0.0))
        P.op("vector", lambda e, H: e.memset(X["hsB"][:], 0.0))
        P.op("vector", lambda e, H: e.memset(
            X["carall"][:, NCORE * 32:NCORE * 32 + 64], 0.0))
        P.wait("vector", "s_in", in0)
        P.op("vector", lambda e, H: e.tensor_scalar(
            out=X["bzn"][:], in0=X["bzt"][:], scalar1=-1.0, scalar2=None,
            op0=OP.mult))
        P.op("vector", lambda e, H: e.tensor_scalar(
            out=X["bh05"][:], in0=X["bht"][:], scalar1=0.5, scalar2=None,
            op0=OP.add))
        P.op("vector", lambda e, H: e.tensor_copy(
            out=X["wrb"][:], in_=X["wstg"][:, 0:ND * E]), "s_dve")
        v0 = P.n("s_dve")

        P.wait("scalar", "s_dve", v0)
        P.op("scalar", lambda e, H: e.dma_start(
            out=d_bar_in[:, :], in_=X["zrow"].bitcast(F32)[0:8, 0:4]),
            "s_zero", 16)
        bar_z = P.n("s_zero")
        for i in range(NTOK // 128):
            P.op("scalar", lambda e, H, i=i: e.dma_start(
                out=d_y_in[i * 128:(i + 1) * 128, :], in_=X["zrow"][:, 0:D]),
                "s_zero", 16)

        P.wait("sync", "s_dve", v0)
        for d in range(ND):
            P.wait("sync", "s_gp", P.n("s_gp"))
            P.op("sync", lambda e, H, d=d: e.dma_start(
                out=X["wstg"][:, 0:seq], in_=xT[d * 128:(d + 1) * 128, :]),
                "s_in", 16)
            xl = P.n("s_in")
            P.wait("gpsimd", "s_in", xl)
            P.op("gpsimd", lambda e, H, d=d: e.tensor_copy(
                out=dt_(xbf, d), in_=X["wstg"][:, 0:seq]), "s_gp")
        xld = P.n("s_gp")
        if dbg:
            P.wait("scalar", "s_gp", xld)
            P.op("scalar", lambda e, H: e.dma_start(
                out=dbg_t["xbf"][:, :], in_=xbf[:]), "s_adma", 16)

        def mingru_layer(l, src, dst, ccol):
            for d in range(ND):
                P.wait("sync", "s_gp", P.n("s_gp"))
                P.op("sync", lambda e, H, d=d, l=l: e.dma_start(
                    out=X["wstg"][:, 0:D],
                    in_=wzT[l][:, d * D:(d + 1) * D]), "s_in", 16)
                wl1 = P.n("s_in")
                P.wait("gpsimd", "s_in", wl1)
                P.wait("gpsimd", "s_pe", P.n("s_pe"))
                P.op("gpsimd", lambda e, H: e.tensor_copy(
                    out=X["wzb"][:], in_=X["wstg"][:, 0:D]), "s_gp")
                P.wait("sync", "s_gp", P.n("s_gp"))
                P.op("sync", lambda e, H, d=d, l=l: e.dma_start(
                    out=X["wstg"][:, D:2 * D],
                    in_=whT[l][:, d * D:(d + 1) * D]), "s_in", 16)
                wl2 = P.n("s_in")
                P.wait("gpsimd", "s_in", wl2)
                P.op("gpsimd", lambda e, H: e.tensor_copy(
                    out=X["whb"][:], in_=X["wstg"][:, D:2 * D]), "s_gp")
                wcast = P.n("s_gp")

                P.wait("tensor", "s_gp", wcast)
                for q in range(NQ):
                    qb = q % 2
                    if not (l == 0 and d == 0 and q < 2):
                        P.wait("tensor", "s_act", P.n("s_act"))
                        P.wait("tensor", "s_dve", P.n("s_dve"))
                    for kc in range(ND):
                        P.op("tensor", lambda e, H, q=q, kc=kc, qb=qb,
                             src=src: e.matmul(
                                 psKb(qb), X["wzb"][:, kc * 128:(kc + 1) * 128],
                                 dt_(src, kc)[:, q * 512:(q + 1) * 512],
                                 start=(kc == 0), stop=(kc == ND - 1)))
                    for kc in range(ND):
                        P.op("tensor", lambda e, H, q=q, kc=kc, qb=qb,
                             src=src: e.matmul(
                                 psPb(qb), X["whb"][:, kc * 128:(kc + 1) * 128],
                                 dt_(src, kc)[:, q * 512:(q + 1) * 512],
                                 start=(kc == 0), stop=(kc == ND - 1)),
                             sem="s_pe" if kc == ND - 1 else None)
                    mmd = P.n("s_pe")
                    P.wait("scalar", "s_pe", mmd)
                    P.op("scalar", lambda e, H, q=q, qb=qb, d=d, l=l:
                         e.activation(
                             out=X["cbuf"][:, q * 512:(q + 1) * 512],
                             in_=psKb(qb), func=AF.Sigmoid,
                             bias=X["bzn"][:, l * ND + d:l * ND + d + 1],
                             scale=-1.0))
                    P.op("scalar", lambda e, H, q=q, qb=qb, d=d, l=l:
                         e.activation(
                             out=X["spb"][:, qb * 512:(qb + 1) * 512],
                             in_=psPb(qb), func=AF.Sigmoid,
                             bias=X["bht"][:, l * ND + d:l * ND + d + 1],
                             scale=1.0), "s_act")
                    acd = P.n("s_act")
                    P.wait("vector", "s_act", acd)
                    P.op("vector", lambda e, H, q=q, qb=qb, d=d, l=l:
                         e.scalar_tensor_tensor(
                             out=X["vbuf"][:, q * 512:(q + 1) * 512],
                             in0=psPb(qb),
                             scalar=X["bh05"][:, l * ND + d:l * ND + d + 1],
                             in1=X["spb"][:, qb * 512:(qb + 1) * 512],
                             op0=OP.add, op1=OP.max))
                    P.op("vector", lambda e, H, q=q: e.tensor_scalar(
                        out=X["sigk"][:],
                        in0=X["cbuf"][:, q * 512:(q + 1) * 512],
                        scalar1=-1.0, scalar2=1.0, op0=OP.mult, op1=OP.add))
                    P.op("vector", lambda e, H, q=q: e.tensor_tensor(
                        out=X["vbuf"][:, q * 512:(q + 1) * 512],
                        in0=X["vbuf"][:, q * 512:(q + 1) * 512],
                        in1=X["sigk"][:], op=OP.mult), "s_dve")
                P.op("vector", lambda e, H, d=d, dst=dst: e.tensor_tensor_scan(
                    out=dt_(dst, d), data0=X["cbuf"][:], data1=X["vbuf"][:],
                    initial=0.0, op0=OP.mult, op1=OP.add))
                P.op("vector", lambda e, H, d=d: e.tensor_tensor_scan(
                    out=dt_(ptl, d), data0=X["cbuf"][:],
                    data1=X["zero1b"][:].to_broadcast([128, seq]), initial=1.0,
                    op0=OP.mult, op1=OP.add))
                P.op("vector", lambda e, H, d=d, cc=ccol, dst=dst:
                     e.tensor_copy(out=X["carst"][:, cc + d:cc + d + 1],
                                   in_=dt_(dst, d)[:, seq - 1:seq]))
                P.op("vector", lambda e, H, d=d, cc=ccol: e.tensor_copy(
                    out=X["carst"][:, cc + 8 + d:cc + 8 + d + 1],
                    in_=dt_(ptl, d)[:, seq - 1:seq]), "s_dve")
            sc_done = P.n("s_dve")
            P.wait("gpsimd", "s_dve", sc_done)
            P.wait("gpsimd", "s_zero", bar_z)
            P.op("gpsimd", lambda e, H: e.dma_start(
                out=d_car_in[:, :], in_=X["carst"][:]), "s_gdma", 16)
            P.wait("gpsimd", "s_gdma")
            P.op("gpsimd", lambda e, H: e.collective_compute(
                "AllGather", OP.bypass, replica_groups=[list(range(NCORE))],
                ins=[d_car_in.opt()], outs=[d_car_out.opt()]), "s_cc", 1)
            P.op("gpsimd", lambda e, H: e.collective_compute(
                "AllReduce", OP.add, replica_groups=[list(range(NCORE))],
                ins=[d_bar_in.opt()], outs=[d_bar_out.opt()]), "s_cc", 1)
            P.wait("gpsimd", "s_cc")
            pass
            P.op("gpsimd", lambda e, H: e.dma_start(
                out=X["carall"][:, 0:NCORE * 32],
                in_=d_car_out.rearrange("(r p) c -> p r c", p=128)),
                "s_gdma", 16)
            if dbg and ccol == 0:
                P.op("gpsimd", lambda e, H: e.dma_start(
                    out=dbg_t["car0"][:, :], in_=X["carall"][:, 0:NCORE * 32]),
                    "s_gdma", 16)
                P.op("gpsimd", lambda e, H: e.dma_start(
                    out=dbg_t["cmsk"][:, :], in_=X["cmsk"][:]), "s_gdma", 16)
            cl = P.n("s_gdma")
            P.wait("vector", "s_gdma", cl)
            hso = l * 64
            bufs = ["hstart", "hsB"]
            P.op("vector", lambda e, H, hso=hso: e.memset(
                X["hstart"][:, hso:hso + 64], 0.5), "s_cw")
            cw()
            for r in range(NCORE):
                hcol = r * 32 + ccol
                pcol = r * 32 + ccol + 8
                sa = bufs[r % 2]
                da = bufs[(r + 1) % 2]
                P.op("vector", lambda e, H, pcol=pcol, sa=sa, hso=hso:
                     e.tensor_tensor(
                         out=X["tmpc"][:], in0=X["carall"][:, pcol:pcol + 64],
                         in1=X[sa][:, hso:hso + 64], op=OP.mult))
                P.op("vector", lambda e, H, hcol=hcol: e.tensor_tensor(
                    out=X["tmpc"][:], in0=X["tmpc"][:],
                    in1=X["carall"][:, hcol:hcol + 64], op=OP.add))
                P.op("vector", lambda e, H, r=r, sa=sa, da=da, hso=hso:
                     e.scalar_tensor_tensor(
                         out=X[da][:, hso:hso + 64], in0=X["tmpc"][:],
                         scalar=X["cmsk"][:, r:r + 1],
                         in1=X[sa][:, hso:hso + 64], op0=OP.mult, op1=OP.add))
                P.op("vector", lambda e, H, r=r, sa=sa, hso=hso:
                     e.tensor_scalar(
                         out=X["tmpc"][:], in0=X[sa][:, hso:hso + 64],
                         scalar1=X["cmsk"][:, r:r + 1], scalar2=None,
                         op0=OP.mult))
                P.op("vector", lambda e, H, da=da, hso=hso: e.tensor_tensor(
                    out=X[da][:, hso:hso + 64], in0=X[da][:, hso:hso + 64],
                    in1=X["tmpc"][:], op=OP.subtract), "s_cw")
                cw()
            for d in range(ND):
                P.op("vector", lambda e, H, d=d, hso=hso, dst=dst:
                     e.scalar_tensor_tensor(
                         out=dt_(dst, d), in0=dt_(ptl, d),
                         scalar=X["hstart"][:, hso + d:hso + d + 1],
                         in1=dt_(dst, d), op0=OP.mult, op1=OP.add),
                     sem="s_dve" if d == ND - 1 else None)
            return P.n("s_dve")

        P.wait("tensor", "s_gp", xld)
        P.wait("vector", "s_gp", xld)
        l0_done = mingru_layer(0, xbf, h1a, 0)
        if dbg:
            P.wait("scalar", "s_dve", l0_done)
            P.op("scalar", lambda e, H: e.dma_start(
                out=dbg_t["hst"][:, :], in_=X["hstart"][:, 0:16]),
                "s_adma", 16)
        if dbg:
            P.wait("scalar", "s_dve", l0_done)
            P.op("scalar", lambda e, H: e.dma_start(
                out=dbg_t["h1"][:, :], in_=h1a[:]), "s_adma", 16)
            P.op("scalar", lambda e, H: e.dma_start(
                out=dbg_t["car"][:, :], in_=X["carall"][:, 0:NCORE * 32]),
                "s_adma", 16)
            P.wait("vector", "s_adma", P.n("s_adma"))
        mingru_layer(1, h1a, hlc, 16)

        for d in range(ND):
            P.op("vector", lambda e, H, d=d: e.tensor_tensor(
                out=dt_(hlc, d), in0=dt_(hlc, d), in1=dt_(xbf, d), op=OP.add),
                sem="s_dve" if d == ND - 1 else None)
        y1_done = P.n("s_dve")
        if dbg:
            P.wait("scalar", "s_dve", y1_done)
            P.op("scalar", lambda e, H: e.dma_start(
                out=dbg_t["y1"][:, :], in_=hlc[:]), "s_adma", 16)
            P.wait("vector", "s_adma", P.n("s_adma"))

        P.wait("tensor", "s_dve", y1_done)
        for q in range(NQ):
            if q >= 2:
                P.wait("tensor", "s_act", P.n("s_act"))
            for d in range(ND):
                P.op("tensor", lambda e, H, q=q, d=d: e.matmul(
                    psSb(q % 2)[0:E, :], X["wrb"][:, d * E:(d + 1) * E],
                    dt_(hlc, d)[:, q * 512:(q + 1) * 512],
                    start=(d == 0), stop=(d == ND - 1)),
                    sem="s_pe" if d == ND - 1 else None)
            rq = P.n("s_pe")
            P.wait("scalar", "s_pe", rq)
            P.op("scalar", lambda e, H, q=q: e.copy(
                out=rawsb[:, q * 512:(q + 1) * 512], in_=psSb(q % 2)[0:E, :]),
                "s_act")
        raw_done = P.n("s_act")

        P.wait("tensor", "s_act", raw_done)
        for j in range(NT):
            tb = j % 2
            if j >= 2:
                P.wait("tensor", "s_dve", P.n("s_dve"))
            for d in range(ND):
                P.op("tensor", lambda e, H, j=j, d=d, tb=tb: e.transpose(
                    psTb(tb)[:, d * 128:(d + 1) * 128],
                    dt_(hlc, d)[:, j * 128:(j + 1) * 128], X["identb"][:]),
                    sem="s_pe" if d == ND - 1 else None)
            tp = P.n("s_pe")
            P.wait("vector", "s_pe", tp)
            P.op("vector", lambda e, H, j=j, tb=tb: e.tensor_copy(
                out=ptl[:, j * 1024:(j + 1) * 1024], in_=psTb(tb)[:]), "s_dve")
        y1tm_done = P.n("s_dve")
        y1tm = ptl

        for j in range(NT):
            P.op("vector", lambda e, H, j=j: e.tensor_scalar(
                out=X["stsc"][:], in0=y1tm[:, j * 1024:(j + 1) * 1024],
                scalar1=1.0, scalar2=0.0, op0=OP.mult, op1=OP.add,
                accum_out=X["mstat"][:, j:j + 1]), "s_dve")
            sqv = P.n("s_dve")
            P.wait("scalar", "s_dve", sqv)
            P.op("scalar", lambda e, H, j=j: e.activation(
                out=X["stsc"][:], in_=y1tm[:, j * 1024:(j + 1) * 1024],
                func=AF.Square, bias=0.0, scale=1.0), "s_act")
            P.wait("vector", "s_act", P.n("s_act"))
            P.op("vector", lambda e, H, j=j: e.tensor_scalar(
                out=X["stsc"][:], in0=X["stsc"][:], scalar1=1.0, scalar2=0.0,
                op0=OP.mult, op1=OP.add, accum_out=X["rstdv"][:, j:j + 1]),
                "s_dve")
        P.op("vector", lambda e, H: e.tensor_scalar(
            out=X["mstat"][:], in0=X["mstat"][:], scalar1=1.0 / D,
            scalar2=None, op0=OP.mult), "s_cw")
        cw()
        P.op("vector", lambda e, H: e.tensor_tensor(
            out=X["mrv"][:], in0=X["mstat"][:], in1=X["mstat"][:],
            op=OP.mult), "s_cw")
        cw()
        P.op("vector", lambda e, H: e.scalar_tensor_tensor(
            out=X["rstdv"][:], in0=X["rstdv"][:], scalar=1.0 / D,
            in1=X["mrv"][:], op0=OP.mult, op1=OP.subtract), "s_dve")
        var_done = P.n("s_dve")
        P.wait("scalar", "s_dve", var_done)
        P.op("scalar", lambda e, H: e.activation(
            out=X["rstdv"][:], in_=X["rstdv"][:], func=AF.Ln, bias=X["epsc"][:],
            scale=1.0), "s_cs")
        cs()
        P.op("scalar", lambda e, H: e.activation(
            out=X["rstdv"][:], in_=X["rstdv"][:], func=AF.Exp, bias=0.0,
            scale=-0.5), "s_act")
        rstd_done = P.n("s_act")
        P.wait("vector", "s_act", rstd_done)
        P.op("vector", lambda e, H: e.tensor_tensor(
            out=X["mrv"][:], in0=X["mstat"][:], in1=X["rstdv"][:], op=OP.mult),
            "s_dve")
        mr_done = P.n("s_dve")

        P.wait("scalar", "s_dve", mr_done)
        for j in range(NT):
            P.op("scalar", lambda e, H, j=j: e.dma_start(
                out=d_rows[0:1, j * 128:(j + 1) * 128].rearrange("o t -> t o"),
                in_=X["rstdv"][:, j:j + 1]), "s_adma", 16)
            P.op("scalar", lambda e, H, j=j: e.dma_start(
                out=d_rows[1:2, j * 128:(j + 1) * 128].rearrange("o t -> t o"),
                in_=X["mrv"][:, j:j + 1]), "s_adma", 16)
        srw = P.n("s_adma")
        P.wait("tensor", "s_act", raw_done)
        for d in range(ND):
            P.op("tensor", lambda e, H, d=d: e.matmul(
                psSb(0)[0:E, 0:1], X["wrb"][:, d * E:(d + 1) * E],
                X["onesb"][:, 0:1], start=(d == 0), stop=(d == ND - 1)),
                sem="s_pe" if d == ND - 1 else None)
        sw_done = P.n("s_pe")
        P.wait("vector", "s_pe", sw_done)
        P.op("vector", lambda e, H: e.tensor_scalar(
            out=X["cnt"][0:E, :], in0=psSb(0)[0:E, 0:1], scalar1=-1.0,
            scalar2=None, op0=OP.mult), "s_dve")
        negsw_done = P.n("s_dve")

        P.wait("sync", "s_adma", srw)
        P.op("sync", lambda e, H: e.dma_start(
            out=rows8[:, 0:seq], in_=d_rows[0:1, 0:seq].partition_broadcast(E)),
            "s_in", 16)
        r8a = P.n("s_in")
        P.wait("vector", "s_in", r8a)
        P.op("vector", lambda e, H: e.tensor_tensor(
            out=rawsb[:, 0:seq], in0=rawsb[:, 0:seq], in1=rows8[:, 0:seq],
            op=OP.mult), "s_dve")
        lg1 = P.n("s_dve")
        P.wait("sync", "s_dve", lg1)
        P.op("sync", lambda e, H: e.dma_start(
            out=rows8[:, 0:seq], in_=d_rows[1:2, 0:seq].partition_broadcast(E)),
            "s_in", 16)
        r8b = P.n("s_in")
        P.wait("vector", "s_in", r8b)
        P.wait("vector", "s_dve", negsw_done)
        P.op("vector", lambda e, H: e.scalar_tensor_tensor(
            out=rawsb[:, 0:seq], in0=rows8[:, 0:seq],
            scalar=X["cnt"][0:E, 0:1], in1=rawsb[:, 0:seq],
            op0=OP.mult, op1=OP.add), "s_dve")
        lg2 = P.n("s_dve")
        P.wait("scalar", "s_dve", lg2)
        P.op("scalar", lambda e, H: e.activation(
            out=prsb[:, 0:seq], in_=rawsb[:, 0:seq], func=AF.Exp,
            bias=X["brt"][:], scale=1.0), "s_act")
        exp_done = P.n("s_act")
        P.wait("tensor", "s_act", exp_done)
        for q in range(NQ):
            P.op("tensor", lambda e, H, q=q: e.matmul(
                psSb(1)[0:1, :], X["ones8f"][:, 0:1],
                prsb[:, q * 512:(q + 1) * 512], start=True, stop=True), "s_pe")
            dq = P.n("s_pe")
            P.wait("vector", "s_pe", dq)
            P.op("vector", lambda e, H, q=q: e.reciprocal(
                out=rawsb[0:1, q * 512:(q + 1) * 512], in_=psSb(1)[0:1, :]),
                "s_dve")
            P.wait("tensor", "s_dve", P.n("s_dve"))
        rec_done = P.n("s_dve")
        P.wait("scalar", "s_dve", rec_done)
        P.op("scalar", lambda e, H: e.dma_start(
            out=d_rows[0:1, 0:seq], in_=rawsb[0:1, 0:seq]), "s_adma", 16)
        P.wait("sync", "s_adma", P.n("s_adma"))
        P.op("sync", lambda e, H: e.dma_start(
            out=rows8[:, 0:seq], in_=d_rows[0:1, 0:seq].partition_broadcast(E)),
            "s_in", 16)
        r8c = P.n("s_in")
        P.wait("vector", "s_in", r8c)
        P.op("vector", lambda e, H: e.tensor_tensor(
            out=prsb[:, 0:seq], in0=prsb[:, 0:seq], in1=rows8[:, 0:seq],
            op=OP.mult), "s_dve")
        probs_done = P.n("s_dve")
        P.wait("scalar", "s_dve", probs_done)
        P.op("scalar", lambda e, H: e.dma_start(
            out=d_probs_in[:, :], in_=prsb[:, 0:seq]), "s_adma", 16)
        pw = P.n("s_adma")
        P.wait("gpsimd", "s_adma", pw)
        P.op("gpsimd", lambda e, H: e.collective_compute(
            "AllGather", OP.bypass, replica_groups=[list(range(NCORE))],
            ins=[d_probs_in.opt()], outs=[d_probs_out.opt()]), "s_cc", 1)
        P.op("gpsimd", lambda e, H: e.collective_compute(
            "AllReduce", OP.add, replica_groups=[list(range(NCORE))],
            ins=[d_bar_in.opt()], outs=[d_bar_out.opt()]), "s_cc", 1)
        cc_probs = P.n("s_cc")
        if dbg:
            P.wait("scalar", "s_dve", probs_done)
            P.op("scalar", lambda e, H: e.dma_start(
                out=dbg_t["probs"][:, :], in_=prsb[:, 0:seq]), "s_st", 16)

        P.wait("vector", "s_dve", probs_done)
        for j in range(NT):
            P.op("vector", lambda e, H, j=j: e.tensor_scalar(
                out=hlc[:, j * 1024:(j + 1) * 1024],
                in0=y1tm[:, j * 1024:(j + 1) * 1024],
                scalar1=X["mstat"][:, j:j + 1], scalar2=X["rstdv"][:, j:j + 1],
                op0=OP.subtract, op1=OP.mult),
                sem="s_dve" if j == NT - 1 else None)
        x1_done = P.n("s_dve")
        x1tm = hlc
        P.wait("scalar", "s_dve", x1_done)
        for j in range(NT):
            P.op("scalar", lambda e, H, j=j: e.dma_start(
                out=d_x1_in[j * 128:(j + 1) * 128, :],
                in_=x1tm[:, j * 1024:(j + 1) * 1024]), "s_adma", 16)
        x1w = P.n("s_adma")
        P.wait("gpsimd", "s_adma", x1w)
        P.op("gpsimd", lambda e, H: e.collective_compute(
            "AllGather", OP.bypass, replica_groups=[list(range(NCORE))],
            ins=[d_x1_in.opt()], outs=[d_x1_out.opt()]), "s_cc", 1)
        P.op("gpsimd", lambda e, H: e.collective_compute(
            "AllReduce", OP.add, replica_groups=[list(range(NCORE))],
            ins=[d_bar_in.opt()], outs=[d_bar_out.opt()]), "s_cc", 1)
        cc_x1 = P.n("s_cc")
        if dbg:
            P.wait("scalar", "s_adma", x1w)
            for j in range(NT):
                P.op("scalar", lambda e, H, j=j: e.dma_start(
                    out=dbg_t["x1tm"][j * 128:(j + 1) * 128, :],
                    in_=x1tm[:, j * 1024:(j + 1) * 1024]), "s_st", 16)

        P.wait("gpsimd", "s_cc", cc_probs)
        pass
        P.op("gpsimd", lambda e, H: e.indirect_dma_start(
            out=prsb[:, 0:seq], out_offset=None, in_=d_probs_out[:, :],
            in_offset=IndirectOffsetOnAxis(ap=X["rsel"][:], axis=0),
            bounds_check=E * NCORE - 1, oob_is_err=False), "s_gdma", 16)
        eg = P.n("s_gdma")
        P.wait("gpsimd", "s_gdma", eg)
        P.op("gpsimd", lambda e, H: e.dma_start(
            out=d_escore.rearrange("(r t) o -> r (t o)", r=E),
            in_=prsb[:, 0:seq]), "s_gdma", 16)
        ew = P.n("s_gdma")
        P.wait("sync", "s_gdma", ew)
        P.op("sync", lambda e, H: e.dma_start(
            out=X["sc64"][:],
            in_=d_escore.rearrange("(p f) o -> p (f o)", p=64)), "s_in", 16)
        P.op("sync", lambda e, H: e.dma_start(
            out=sc128[:],
            in_=d_escore.rearrange("(p f) o -> p (f o)", p=128)), "s_in", 16)
        scl = P.n("s_in")

        P.wait("vector", "s_in", scl)
        P.op("vector", lambda e, H: e.memset(X["lo"][:], 0.0))
        P.op("vector", lambda e, H: e.memset(X["hi"][:], 1.0), "s_dve")
        P.wait("tensor", "s_dve", P.n("s_dve"))
        for it in range(40):
            P.wait("vector", "s_dve", P.n("s_dve"))
            P.op("vector", lambda e, H: e.tensor_tensor(
                out=X["midv"][:], in0=X["lo"][:], in1=X["hi"][:], op=OP.add),
                "s_cw")
            cw()
            P.op("vector", lambda e, H: e.tensor_scalar(
                out=X["midv"][:], in0=X["midv"][:], scalar1=0.5, scalar2=None,
                op0=OP.mult), "s_cw")
            cw()
            P.op("vector", lambda e, H: e.tensor_scalar(
                out=X["j64"][:], in0=X["sc64"][:], scalar1=X["midv"][:],
                scalar2=0.0, op0=OP.is_gt, op1=OP.add, accum_out=X["cnt"][:]),
                "s_cw")
            cw()
            P.op("vector", lambda e, H: e.tensor_copy(
                out=X["cntb"][:], in_=X["cnt"][:]), "s_dve")
            P.wait("tensor", "s_dve", P.n("s_dve"))
            P.op("tensor", lambda e, H: e.matmul(
                psSb(0)[0:64, 0:1], X["ones64"][:], X["cntb"][:], start=True,
                stop=True), "s_pe")
            P.wait("vector", "s_pe", P.n("s_pe"))
            P.op("vector", lambda e, H: e.tensor_scalar(
                out=X["bmask"][:], in0=psSb(0)[0:64, 0:1],
                scalar1=float(KSEL), scalar2=None, op0=OP.is_ge))
            P.op("vector", lambda e, H: e.tensor_scalar(
                out=X["nbmask"][:], in0=psSb(0)[0:64, 0:1],
                scalar1=float(KSEL), scalar2=None, op0=OP.is_lt), "s_cw")
            cw()
            P.op("vector", lambda e, H: e.select(
                out=X["lo"][:], mask=X["bmask"][:], on_true=X["midv"][:],
                on_false=X["lo"][:]))
            P.op("vector", lambda e, H: e.select(
                out=X["hi"][:], mask=X["nbmask"][:], on_true=X["midv"][:],
                on_false=X["hi"][:]), "s_dve")
            P.wait("tensor", "s_dve", P.n("s_dve"))
        P.wait("vector", "s_dve", P.n("s_dve"))
        P.op("vector", lambda e, H: e.tensor_scalar(
            out=X["j64"][:], in0=X["sc64"][:], scalar1=X["hi"][:], scalar2=0.0,
            op0=OP.is_gt, op1=OP.add, accum_out=X["cnt"][:]), "s_cw")
        cw()
        P.op("vector", lambda e, H: e.tensor_copy(
            out=X["cntb"][:], in_=X["cnt"][:]), "s_dve")
        P.wait("tensor", "s_dve", P.n("s_dve"))
        P.op("tensor", lambda e, H: e.matmul(
            psSb(0)[0:64, 0:1], X["ones64"][:], X["cntb"][:], start=True,
            stop=True), "s_pe")
        P.wait("vector", "s_pe", P.n("s_pe"))
        P.op("vector", lambda e, H: e.tensor_copy(
            out=X["mtot"][:], in_=psSb(0)[0:64, 0:1]), "s_cw")
        cw()
        P.op("vector", lambda e, H: e.tensor_scalar(
            out=X["rneed"][:], in0=X["mtot"][:], scalar1=-1.0,
            scalar2=float(KSEL), op0=OP.mult, op1=OP.add), "s_cw")
        cw()
        P.op("vector", lambda e, H: e.tensor_copy(out=X["tau128"][0:64, :],
                                                  in_=X["hi"][:]))
        P.op("vector", lambda e, H: e.tensor_copy(out=X["tau128"][64:128, :],
                                                  in_=X["hi"][:]))
        P.op("vector", lambda e, H: e.tensor_copy(out=X["r128"][0:64, :],
                                                  in_=X["rneed"][:]))
        P.op("vector", lambda e, H: e.tensor_copy(out=X["r128"][64:128, :],
                                                  in_=X["rneed"][:]), "s_dve")
        bis_done = P.n("s_dve")
        P.wait("vector", "s_dve", bis_done)
        if dbg:
            P.wait("scalar", "s_dve", bis_done)
            P.op("scalar", lambda e, H: e.dma_start(
                out=dbg_t["tau"][:, 0:1], allow_slow_non_contiguous=True, in_=X["hi"][:]), "s_st", 16)
            P.op("scalar", lambda e, H: e.dma_start(
                out=dbg_t["tau"][:, 1:2], allow_slow_non_contiguous=True, in_=X["mtot"][:]), "s_st", 16)
            P.op("scalar", lambda e, H: e.dma_start(
                out=dbg_t["tau"][:, 2:3], allow_slow_non_contiguous=True, in_=X["rneed"][:]), "s_st", 16)

        P.op("vector", lambda e, H: e.tensor_scalar(
            out=mgt[:], in0=sc128[:], scalar1=X["tau128"][:], scalar2=None,
            op0=OP.is_gt))
        P.op("vector", lambda e, H: e.tensor_scalar(
            out=meq[:], in0=sc128[:], scalar1=X["tau128"][:], scalar2=None,
            op0=OP.is_equal), "s_cw")
        cw()
        P.op("vector", lambda e, H: e.tensor_tensor_scan(
            out=tcum[:], data0=meq[:],
            data1=X["zero1"][:].to_broadcast([128, NPQ]), initial=0.0,
            op0=OP.add, op1=OP.add), "s_cw")
        cw()
        P.op("vector", lambda e, H: e.tensor_copy(
            out=X["mppb"][:], in_=tcum[:, NPQ - 1:NPQ]), "s_dve")
        P.wait("tensor", "s_dve", P.n("s_dve"))
        P.op("tensor", lambda e, H: e.matmul(
            psSb(0)[:, 0:1], X["trilb"][:], X["mppb"][:], start=True,
            stop=True), "s_pe")
        P.wait("vector", "s_pe", P.n("s_pe"))
        P.op("vector", lambda e, H: e.tensor_copy(
            out=X["t1c"][:], in_=psSb(0)[:, 0:1]), "s_cw")
        cw()
        P.op("vector", lambda e, H: e.tensor_tensor(
            out=tcum[:], in0=tcum[:], in1=meq[:], op=OP.subtract))
        P.op("vector", lambda e, H: e.tensor_scalar(
            out=tcum[:], in0=tcum[:], scalar1=X["t1c"][:], scalar2=None,
            op0=OP.add), "s_cw")
        cw()
        P.op("vector", lambda e, H: e.tensor_scalar(
            out=scrm[:], in0=tcum[:], scalar1=X["r128"][:], scalar2=None,
            op0=OP.is_lt), "s_cw")
        cw()
        P.op("vector", lambda e, H: e.tensor_tensor(
            out=scrm[:], in0=scrm[:], in1=meq[:], op=OP.mult), "s_cw")
        cw()
        P.op("vector", lambda e, H: e.tensor_tensor(
            out=msel[:], in0=mgt[:], in1=scrm[:], op=OP.add), "s_dve")
        msel_done = P.n("s_dve")
        P.wait("vector", "s_dve", msel_done)

        P.op("vector", lambda e, H: e.tensor_scalar(
            out=scrm[:], in0=msel[:], scalar1=-1.0, scalar2=1.0, op0=OP.mult,
            op1=OP.add), "s_cw")
        cw()
        P.op("vector", lambda e, H: e.tensor_tensor_scan(
            out=tcum[:], data0=scrm[:],
            data1=X["zero1"][:].to_broadcast([128, NPQ]), initial=0.0,
            op0=OP.add, op1=OP.add), "s_cw")
        cw()
        P.op("vector", lambda e, H: e.tensor_tensor(
            out=tcum[:], in0=tcum[:], in1=scrm[:], op=OP.subtract), "s_cw")
        cw()
        P.op("vector", lambda e, H: e.memset(fpM[:], 0.0))
        P.op("vector", lambda e, H: e.memset(fpV[:], 0.0))
        for k in range(NBITS):
            P.op("vector", lambda e, H, k=k: e.memset(fpB[k][:], 0.0))
        P.op("vector", lambda e, H: e.tensor_copy(
            out=fpV[:, 0:NPQ], in_=X["iota128f"][:, 0:NPQ]))
        P.op("vector", lambda e, H: e.tensor_copy(
            out=fpM[:, 0:NPQ], in_=msel[:]))
        P.op("vector", lambda e, H: e.tensor_tensor(
            out=fpmv[:, 0:NPQ], in0=tcum[:], in1=msel[:], op=OP.mult), "s_cw")
        cw()
        for k in range(NBITS - 1, -1, -1):
            P.op("vector", lambda e, H, k=k: e.tensor_scalar(
                out=fpB[k][:, 0:NPQ], in0=fpmv[:, 0:NPQ],
                scalar1=float(1 << k), scalar2=None, op0=OP.is_ge), "s_cw")
            cw()
            P.op("vector", lambda e, H, k=k: e.scalar_tensor_tensor(
                out=fpmv[:, 0:NPQ], in0=fpB[k][:, 0:NPQ],
                scalar=float(-(1 << k)), in1=fpmv[:, 0:NPQ], op0=OP.mult,
                op1=OP.add), "s_cw")
            cw()
        P.op("vector", lambda e, H: e.engine_nop() if False else
             e.memset(fpnm[:], 0.0), "s_dve")
        P.wait("vector", "s_dve", P.n("s_dve"))
        for k in range(NBITS):
            s = 1 << k
            P.op("vector", lambda e, H, k=k: e.tensor_tensor(
                out=fpmv[:], in0=fpM[:], in1=fpB[k][:], op=OP.mult), "s_cw")
            cw()
            P.op("vector", lambda e, H: e.tensor_copy(
                out=fpmvi[:], in_=fpmv[:]), "s_cw")
            cw()
            P.op("vector", lambda e, H, s=s: e.select(
                out=fpV[:, 0:PACKW - s], mask=fpmvi[:, s:PACKW],
                on_true=fpV[:, s:PACKW], on_false=fpV[:, 0:PACKW - s]))
            for j in range(k + 1, NBITS):
                P.op("vector", lambda e, H, s=s, j=j: e.select(
                    out=fpB[j][:, 0:PACKW - s], mask=fpmvi[:, s:PACKW],
                    on_true=fpB[j][:, s:PACKW],
                    on_false=fpB[j][:, 0:PACKW - s]))
            P.op("vector", lambda e, H: e.tensor_scalar(
                out=fpnm[:], in0=fpmv[:], scalar1=-1.0, scalar2=1.0,
                op0=OP.mult, op1=OP.add), "s_cw")
            cw()
            P.op("vector", lambda e, H: e.tensor_tensor(
                out=fpM[:], in0=fpM[:], in1=fpnm[:], op=OP.mult), "s_cw")
            cw()
            P.op("vector", lambda e, H, s=s: e.tensor_tensor(
                out=fpM[:, 0:PACKW - s], in0=fpM[:, 0:PACKW - s],
                in1=fpmv[:, s:PACKW], op=OP.max), "s_dve")
            P.wait("vector", "s_dve", P.n("s_dve"))
        pack_done = P.n("s_dve")

        P.op("vector", lambda e, H: e.tensor_scalar(
            out=scrm[:], in0=msel[:], scalar1=1.0, scalar2=0.0, op0=OP.mult,
            op1=OP.add, accum_out=X["npp"][:]), "s_cw")
        cw()
        P.op("vector", lambda e, H: e.tensor_copy(
            out=X["mppb"][:], in_=X["npp"][:]), "s_dve")
        P.wait("tensor", "s_dve", P.n("s_dve"))
        P.op("tensor", lambda e, H: e.matmul(
            psSb(0)[:, 0:1], X["trilb"][:], X["mppb"][:], start=True,
            stop=True), "s_pe")
        P.wait("vector", "s_pe", P.n("s_pe"))
        P.op("vector", lambda e, H: e.tensor_copy(
            out=X["mpp"][:], in_=psSb(0)[:, 0:1]), "s_dve")
        P.op("gpsimd", lambda e, H: e.iota(
            iotaf2.bitcast(I32)[:], pattern=[[1, KSEL]], base=0,
            channel_multiplier=0), "s_gp")
        P.wait("vector", "s_gp", P.n("s_gp"))
        P.wait("vector", "s_dve", P.n("s_dve"))
        P.op("vector", lambda e, H: e.tensor_copy(
            out=iotaf2[:], in_=iotaf2.bitcast(I32)[:]), "s_cw")
        cw()
        P.op("vector", lambda e, H: e.tensor_scalar(
            out=m12[:], in0=iotaf2[:], scalar1=X["mpp"][:], scalar2=None,
            op0=OP.is_ge), "s_dve")
        P.wait("tensor", "s_dve", P.n("s_dve"))
        for q in range(NKQ):
            P.op("tensor", lambda e, H, q=q: e.matmul(
                psSb(0)[0:1, :], X["onesb"][:, 0:1],
                m12[:, q * 512:(q + 1) * 512], start=True, stop=True), "s_pe")
            P.wait("vector", "s_pe", P.n("s_pe"))
            P.op("vector", lambda e, H, q=q: e.tensor_scalar(
                out=psrow[:, q * 512:(q + 1) * 512], in0=psSb(0)[0:1, :],
                scalar1=-1.0, scalar2=None, op0=OP.add), "s_dve")
            P.wait("tensor", "s_dve", P.n("s_dve"))
        P.wait("vector", "s_dve", P.n("s_dve"))
        P.op("vector", lambda e, H: e.tensor_copy(
            out=psrowb[:], in_=psrow[:]))
        P.op("vector", lambda e, H: e.tensor_copy(out=X["t1c"][:],
                                                  in_=X["mpp"][:]))
        P.op("vector", lambda e, H: e.memset(X["t2c"][:], 0.0), "s_cw")
        cw()
        for jb in range(11, 5, -1):
            P.op("vector", lambda e, H, jb=jb: e.tensor_scalar(
                out=X["lnr"][:], in0=X["t1c"][:], scalar1=float(1 << jb),
                scalar2=None, op0=OP.is_ge), "s_cw")
            cw()
            P.op("vector", lambda e, H, jb=jb: e.scalar_tensor_tensor(
                out=X["t1c"][:], in0=X["lnr"][:], scalar=float(-(1 << jb)),
                in1=X["t1c"][:], op0=OP.mult, op1=OP.add), "s_cw")
            cw()
            P.op("vector", lambda e, H, jb=jb: e.scalar_tensor_tensor(
                out=X["t2c"][:], in0=X["lnr"][:], scalar=float(1 << (jb - 6)),
                in1=X["t2c"][:], op0=OP.mult, op1=OP.add), "s_cw")
            cw()
        P.op("vector", lambda e, H: e.tensor_copy(out=X["mhib"][:],
                                                  in_=X["t2c"][:]))
        P.op("vector", lambda e, H: e.tensor_copy(out=X["mlob"][:],
                                                  in_=X["t1c"][:]), "s_dve")
        P.wait("tensor", "s_dve", P.n("s_dve"))
        for q in range(NKQ):
            P.op("tensor", lambda e, H, q=q: e.matmul(
                psSb(0)[:, :], X["onesb"][0:1, :],
                psrowb[:, q * 512:(q + 1) * 512], start=True, stop=True),
                "s_pe")
            P.wait("vector", "s_pe", P.n("s_pe"))
            P.op("vector", lambda e, H, q=q: e.tensor_scalar(
                out=m12[:, q * 512:(q + 1) * 512], in0=psSb(0)[:, :],
                scalar1=X["iotapf"][:], scalar2=None, op0=OP.is_equal),
                "s_dve")
            P.wait("tensor", "s_dve", P.n("s_dve"))
            P.op("tensor", lambda e, H, q=q: e.matmul(
                psSb(1)[0:1, :], X["mhib"][:], m12[:, q * 512:(q + 1) * 512],
                start=True, stop=True), "s_pe")
            P.wait("vector", "s_pe", P.n("s_pe"))
            P.op("vector", lambda e, H, q=q: e.tensor_copy(
                out=workA[:, q * 512:(q + 1) * 512], in_=psSb(1)[0:1, :]),
                "s_dve")
            P.wait("tensor", "s_dve", P.n("s_dve"))
            P.op("tensor", lambda e, H, q=q: e.matmul(
                psSb(1)[0:1, :], X["mlob"][:], m12[:, q * 512:(q + 1) * 512],
                start=True, stop=True), "s_pe")
            P.wait("vector", "s_pe", P.n("s_pe"))
            P.op("vector", lambda e, H, q=q: e.tensor_copy(
                out=workB[:, q * 512:(q + 1) * 512], in_=psSb(1)[0:1, :]),
                "s_dve")
        P.wait("vector", "s_dve", P.n("s_dve"))
        P.op("vector", lambda e, H: e.scalar_tensor_tensor(
            out=workB[:], in0=workA[:], scalar=64.0, in1=workB[:],
            op0=OP.mult, op1=OP.add), "s_cw")
        cw()
        P.op("vector", lambda e, H: e.tensor_tensor(
            out=workB[:], in0=iotaf2[0:1, :], in1=workB[:], op=OP.subtract),
            "s_cw")
        cw()
        P.op("vector", lambda e, H: e.tensor_copy(
            out=jrowb[:], in_=workB[:]))
        P.op("vector", lambda e, H: e.tensor_copy(
            out=X["tpackb"][:], in_=fpV[:, 0:128]), "s_dve")
        P.wait("tensor", "s_dve", P.n("s_dve"))
        for q in range(NKQ):
            P.op("tensor", lambda e, H, q=q: e.matmul(
                psSb(0)[:, :], X["onesb"][0:1, :],
                jrowb[:, q * 512:(q + 1) * 512], start=True, stop=True),
                "s_pe")
            P.wait("vector", "s_pe", P.n("s_pe"))
            P.op("vector", lambda e, H, q=q: e.tensor_scalar(
                out=jhT[:], in0=psSb(0)[:, :], scalar1=X["iotapf"][:],
                scalar2=None, op0=OP.is_equal), "s_dve")
            P.wait("tensor", "s_dve", P.n("s_dve"))
            P.op("tensor", lambda e, H, q=q: e.matmul(
                psSb(1)[:, :], X["tpackb"][:], m12[:, q * 512:(q + 1) * 512],
                start=True, stop=True), "s_pe")
            P.wait("vector", "s_pe", P.n("s_pe"))
            P.op("vector", lambda e, H, q=q: e.tensor_tensor(
                out=r2T[:], in0=jhT[:], in1=psSb(1)[:, :], op=OP.mult),
                "s_dve")
            P.wait("tensor", "s_dve", P.n("s_dve"))
            P.op("tensor", lambda e, H, q=q: e.matmul(
                psSb(1)[0:1, :], X["onesb"][:, 0:1], r2T[:], start=True,
                stop=True), "s_pe")
            P.wait("vector", "s_pe", P.n("s_pe"))
            P.op("vector", lambda e, H, q=q: e.scalar_tensor_tensor(
                out=workB[:, q * 512:(q + 1) * 512],
                in0=psrow[:, q * 512:(q + 1) * 512], scalar=128.0,
                in1=psSb(1)[0:1, :], op0=OP.mult, op1=OP.add), "s_dve")
            P.wait("tensor", "s_dve", P.n("s_dve"))
        P.wait("vector", "s_dve", P.n("s_dve"))
        P.op("vector", lambda e, H: e.tensor_copy(
            out=idrowu[:], in_=workB[:]), "s_dve")
        id_done = P.n("s_dve")
        P.wait("scalar", "s_dve", id_done)
        P.op("scalar", lambda e, H: e.dma_start(
            out=d_ids.rearrange("(o k) c -> o (k c)", o=1), in_=idrowu[:]),
            "s_adma", 16)
        idw = P.n("s_adma")
        P.wait("sync", "s_adma", idw)
        for j in range(NSL):
            P.op("sync", lambda e, H, j=j: e.dma_start(
                out=X["idtiles"][:, j:j + 1],
                in_=d_ids[j * 128:(j + 1) * 128, :]), "s_in", 16)
        idt = P.n("s_in")
        if dbg:
            P.wait("scalar", "s_adma", idw)
            P.op("scalar", lambda e, H: e.dma_start(
                out=dbg_t["ids"][:, :], in_=d_ids[:, :]), "s_st", 16)

        P.wait("gpsimd", "s_in", idt)
        P.wait("gpsimd", "s_cc", cc_x1)
        pass
        gtoks = ptl
        P.wait("gpsimd", "s_dve", x1_done)
        for j in range(NSL):
            P.op("gpsimd", lambda e, H, j=j: e.indirect_dma_start(
                out=gtoks[:, j * D:(j + 1) * D], out_offset=None,
                in_=d_x1_out[:, :],
                in_offset=IndirectOffsetOnAxis(ap=X["idtiles"][:, j:j + 1],
                                               axis=0),
                bounds_check=NTOK - 1, oob_is_err=False), "s_gdma", 16)
        tg = P.n("s_gdma")
        if dbg:
            P.wait("gpsimd", "s_gdma", tg)
            for j in range(NSL):
                P.op("gpsimd", lambda e, H, j=j: e.dma_start(
                    out=dbg_t["toks"][j * 128:(j + 1) * 128, :],
                    in_=gtoks[:, j * D:(j + 1) * D]), "s_gdma", 16)
        P.wait("tensor", "s_gdma", tg)
        P.wait("tensor", "s_dve", id_done)
        tokens_fm = h1a
        for j in range(NSL):
            tb = j % 2
            if j >= 2:
                P.wait("tensor", "s_dve", P.n("s_dve"))
            for d in range(ND):
                P.op("tensor", lambda e, H, j=j, d=d, tb=tb: e.transpose(
                    psTb(tb)[:, d * 128:(d + 1) * 128],
                    gtoks[:, j * D + d * 128:j * D + (d + 1) * 128],
                    X["identb"][:]), sem="s_pe" if d == ND - 1 else None)
            tp2 = P.n("s_pe")
            P.wait("vector", "s_pe", tp2)
            for d in range(ND):
                P.op("vector", lambda e, H, j=j, d=d, tb=tb: e.tensor_copy(
                    out=tokens_fm[:, d * KSEL + j * 128:
                                  d * KSEL + (j + 1) * 128],
                    in_=psTb(tb)[:, d * 128:(d + 1) * 128]),
                    sem="s_dve" if d == ND - 1 else None)
        tfm_done = P.n("s_dve")

        P.wait("tensor", "s_dve", tfm_done)
        for q in range(NKQ):
            if q >= 1:
                P.wait("tensor", "s_act", P.n("s_act"))
            for d in range(ND):
                P.op("tensor", lambda e, H, q=q, d=d: e.matmul(
                    psSb(0)[0:E, :], X["wrb"][:, d * E:(d + 1) * E],
                    tokens_fm[:, d * KSEL + q * 512:d * KSEL + (q + 1) * 512],
                    start=(d == 0), stop=(d == ND - 1)),
                    sem="s_pe" if d == ND - 1 else None)
            gq = P.n("s_pe")
            P.wait("scalar", "s_pe", gq)
            P.op("scalar", lambda e, H, q=q: e.activation(
                out=gexp[:, q * 512:(q + 1) * 512], in_=psSb(0)[0:E, :],
                func=AF.Exp, bias=X["brt"][:], scale=1.0), "s_act")
        ge_done = P.n("s_act")
        P.wait("tensor", "s_act", ge_done)
        for q in range(NKQ):
            P.op("tensor", lambda e, H, q=q: e.matmul(
                psSb(1)[0:1, :], X["ones8f"][:, 0:1],
                gexp[:, q * 512:(q + 1) * 512], start=True, stop=True))
            P.op("tensor", lambda e, H, q=q: e.matmul(
                psSb(0)[0:1, :], X["oneh"][:],
                gexp[:, q * 512:(q + 1) * 512], start=True, stop=True), "s_pe")
            g2 = P.n("s_pe")
            P.wait("vector", "s_pe", g2)
            P.op("vector", lambda e, H, q=q: e.reciprocal(
                out=grecip[:, q * 512:(q + 1) * 512], in_=psSb(1)[0:1, :]),
                "s_cw")
            cw()
            P.op("vector", lambda e, H, q=q: e.tensor_tensor(
                out=gslotr[:, q * 512:(q + 1) * 512],
                in0=grecip[:, q * 512:(q + 1) * 512], in1=psSb(0)[0:1, :],
                op=OP.mult), "s_dve")
            P.wait("tensor", "s_dve", P.n("s_dve"))
        g_done = P.n("s_dve")
        P.wait("scalar", "s_dve", g_done)
        P.op("scalar", lambda e, H: e.dma_start(
            out=d_grow[:, :], in_=gslotr[:]), "s_adma", 16)
        gw = P.n("s_adma")
        P.wait("sync", "s_adma", gw)
        for j in range(NSL):
            P.op("sync", lambda e, H, j=j: e.dma_start(
                out=X["gslot_tm"][:, j:j + 1],
                in_=d_grow.rearrange("o (a b) -> (o a) b", b=1)[
                    j * 128:(j + 1) * 128, :]), "s_in", 16)
        gtm = P.n("s_in")
        if dbg:
            P.wait("scalar", "s_adma", gw)
            P.op("scalar", lambda e, H: e.dma_start(
                out=dbg_t["gslot"][:, :], in_=d_grow[:, :]), "s_st", 16)

        P.wait("vector", "s_in", gtm)
        P.wait("scalar", "s_dve", g_done)
        P.wait("gpsimd", "s_zero", P.n("s_zero"))
        dbank = [lambda i=i: psKb(i) for i in range(2)] + \
                [lambda i=i: psPb(i) for i in range(2)] + \
                [lambda i=i: psSb(i) for i in range(2)] + \
                [lambda i=i: psTfb(i) for i in range(2)]

        for tq in range(NKQ):
            if tq >= 1:
                P.wait("tensor", "s_dve", P.n("s_dve"))
            cast_marks = []
            for fc in range(NF):
                rows = FCH[fc]
                hb = fc % 2
                if fc >= 2:
                    P.wait("sync", "s_gp", cast_marks[fc - 2])
                else:
                    P.wait("sync", "s_act", P.n("s_act"))
                P.op("sync", lambda e, H, fc=fc, hb=hb, rows=rows:
                     e.dma_start(
                         out=X["wstg"][:, hb * 1024:hb * 1024 + ND * rows],
                         in_=wu[:, fc * 1024:fc * 1024 + ND * rows]),
                     "s_in", 16)
                wl = P.n("s_in")
                P.wait("gpsimd", "s_in", wl)
                P.wait("gpsimd", "s_pe", P.n("s_pe"))
                P.op("gpsimd", lambda e, H, fc=fc, hb=hb, rows=rows:
                     e.tensor_copy(
                         out=X["wub2"][:, hb * 1024:hb * 1024 + ND * rows],
                         in_=X["wstg"][:, hb * 1024:hb * 1024 + ND * rows]),
                     "s_gp")
                wc = P.n("s_gp")
                cast_marks.append(wc)
                P.wait("tensor", "s_gp", wc)
                if fc >= 2:
                    P.wait("tensor", "s_act", P.n("s_act"))
                for d in range(ND):
                    P.op("tensor", lambda e, H, tq=tq, fc=fc, d=d, hb=hb,
                         rows=rows: e.matmul(
                             psKb(fc % 2)[0:rows, :],
                             X["wub2"][:, hb * 1024 + d * rows:
                                       hb * 1024 + (d + 1) * rows],
                             tokens_fm[:, d * KSEL + tq * 512:
                                       d * KSEL + (tq + 1) * 512],
                             start=(d == 0), stop=(d == ND - 1)),
                         sem="s_pe" if d == ND - 1 else None)
                um = P.n("s_pe")
                P.wait("scalar", "s_pe", um)
                P.op("scalar", lambda e, H, fc=fc, rows=rows: e.activation(
                    out=mid[0:rows, fc * 512:(fc + 1) * 512],
                    in_=psKb(fc % 2)[0:rows, :], func=AF.Silu,
                    bias=X["but"][0:rows, fc:fc + 1], scale=1.0), "s_act")
            silu_done = P.n("s_act")
            if dbg and tq == 0:
                P.op("scalar", lambda e, H: e.dma_start(
                    out=dbg_t["mid0"][:, :], in_=mid[:, :]), "s_adma", 16)

            P.wait("tensor", "s_act", silu_done)
            pe_base = P.n("s_pe")
            cast_done = {}
            pe_marks = {}

            def d_load(fc):
                rows = FCH[fc]
                hb = fc % 2
                if fc >= 2:
                    P.wait("sync", "s_act", cast_done[fc - 2])
                else:
                    P.wait("sync", "s_gp", P.n("s_gp"))
                P.op("sync", lambda e, H, fc=fc, hb=hb, rows=rows:
                     e.dma_start(
                         out=X["wstg"][:, hb * 1024:hb * 1024 + 1024][
                             0:rows, :],
                         in_=wd[fc * 128:fc * 128 + rows, :]), "s_in", 16)
                return P.n("s_in")

            def d_cast(fc, wl):
                rows = FCH[fc]
                hb = fc % 2
                P.wait("scalar", "s_in", wl)
                P.wait("scalar", "s_pe",
                       pe_marks[fc - 2] if fc >= 2 else pe_base)
                P.op("scalar", lambda e, H, fc=fc, hb=hb, rows=rows:
                     e.activation(
                         out=X["wdb2"][0:rows, hb * 1024:hb * 1024 + 1024],
                         in_=X["wstg"][:, hb * 1024:hb * 1024 + 1024][
                             0:rows, :], func=AF.Copy, bias=0.0, scale=1.0),
                     "s_act")
                cast_done[fc] = P.n("s_act")

            wl0 = d_load(0)
            d_cast(0, wl0)
            for fc in range(NF):
                rows = FCH[fc]
                hb = fc % 2
                if fc + 1 < NF:
                    wl = d_load(fc + 1)
                P.wait("tensor", "s_act", cast_done[fc])
                for st4 in range(4):
                    for hh in range(2):
                        P.op("tensor", lambda e, H, tq=tq, fc=fc, st4=st4,
                             hh=hh, rows=rows, hb=hb: e.matmul(
                                 dbank[st4 * 2 + hh](),
                                 mid[0:rows, fc * 512 + st4 * 128:
                                     fc * 512 + (st4 + 1) * 128],
                                 X["wdb2"][0:rows, hb * 1024 + hh * 512:
                                           hb * 1024 + (hh + 1) * 512],
                                 start=(fc == 0), stop=(fc == NF - 1)),
                             sem="s_pe" if (st4 == 3 and hh == 1) else None)
                pe_marks[fc] = P.n("s_pe")
                if fc + 1 < NF:
                    d_cast(fc + 1, wl)
            dn = P.n("s_pe")
            P.wait("vector", "s_pe", dn)
            for st4 in range(4):
                j = tq * 4 + st4
                for hh in range(2):
                    P.op("vector", lambda e, H, st4=st4, hh=hh: e.tensor_tensor(
                        out=X["stsc"][:, hh * 512:(hh + 1) * 512],
                        in0=dbank[st4 * 2 + hh](),
                        in1=X["bdt"][:, hh * 512:(hh + 1) * 512], op=OP.add),
                        "s_cw")
                    cw()
                    P.op("vector", lambda e, H, j=j, hh=hh: e.tensor_scalar(
                        out=wrow[:, hh * 512:(hh + 1) * 512],
                        in0=X["stsc"][:, hh * 512:(hh + 1) * 512],
                        scalar1=X["gslot_tm"][:, j:j + 1], scalar2=None,
                        op0=OP.mult), "s_dve")
                wv = P.n("s_dve")
                P.wait("gpsimd", "s_dve", wv)
                P.op("gpsimd", lambda e, H, j=j: e.indirect_dma_start(
                    out=d_y_in[:, :],
                    out_offset=IndirectOffsetOnAxis(
                        ap=X["idtiles"][:, j:j + 1], axis=0),
                    in_=wrow[:, 0:D], in_offset=None,
                    bounds_check=NTOK - 1, oob_is_err=False), "s_gdma", 16)
                P.wait("vector", "s_gdma", P.n("s_gdma"))
        ffn_done = P.n("s_gdma")

        P.wait("gpsimd", "s_gdma", ffn_done)
        if dbg:
            P.op("gpsimd", lambda e, H: e.dma_start(
                out=dbg_t["yin"][:, :], in_=d_y_in[:, :]), "s_gdma", 16)
            P.wait("gpsimd", "s_gdma", P.n("s_gdma"))
        P.op("gpsimd", lambda e, H: e.collective_compute(
            "ReduceScatter", OP.add, replica_groups=[list(range(NCORE))],
            ins=[d_y_in.opt()], outs=[d_y_out.opt()]), "s_cc", 1)
        P.op("gpsimd", lambda e, H: e.collective_compute(
            "AllReduce", OP.add, replica_groups=[list(range(NCORE))],
            ins=[d_bar_in.opt()], outs=[d_bar_out.opt()]), "s_cc", 1)
        cc_y = P.n("s_cc")
        P.wait("gpsimd", "s_cc", cc_y)
        cc_delay(P, X)
        P.op("gpsimd", lambda e, H: e.memset(X["zrow"][:, 0:512], 0.0), "s_gp")
        ccy_rdy = P.n("s_gp")
        if dbg:
            P.wait("gpsimd", "s_cc", cc_y)
            P.op("gpsimd", lambda e, H: e.dma_start(
                out=dbg_t["y"][:, :], in_=d_y_out[:, :]), "s_gdma", 16)
        P.wait("sync", "s_cc", cc_y)
        P.wait("sync", "s_gp", ccy_rdy)
        for j in range(NT):
            P.wait("sync", "s_dve", P.n("s_dve"))
            P.op("sync", lambda e, H, j=j: e.dma_start(
                out=x1re[:, 0:D], in_=d_x1_in[j * 128:(j + 1) * 128, :]),
                "s_in", 16)
            P.op("sync", lambda e, H, j=j: e.dma_start(
                out=yre[:, 0:D], in_=d_y_out[j * 128:(j + 1) * 128, :]),
                "s_in", 16)
            l2 = P.n("s_in")
            P.wait("vector", "s_in", l2)
            P.wait("vector", "s_st", P.n("s_st"))
            P.op("vector", lambda e, H: e.tensor_tensor(
                out=accq[:, 0:D], in0=x1re[:, 0:D], in1=yre[:, 0:D],
                op=OP.add), "s_cw")
            cw()
            P.op("vector", lambda e, H: e.tensor_scalar(
                out=X["stsc"][:], in0=accq[:, 0:D], scalar1=1.0, scalar2=0.0,
                op0=OP.mult, op1=OP.add, accum_out=X["lnm"][:]), "s_dve")
            aq = P.n("s_dve")
            P.wait("scalar", "s_dve", aq)
            P.op("scalar", lambda e, H: e.activation(
                out=X["stsc"][:], in_=accq[:, 0:D], func=AF.Square, bias=0.0,
                scale=1.0), "s_act")
            P.wait("vector", "s_act", P.n("s_act"))
            P.op("vector", lambda e, H: e.tensor_scalar(
                out=X["stsc"][:], in0=X["stsc"][:], scalar1=1.0, scalar2=0.0,
                op0=OP.mult, op1=OP.add, accum_out=X["lnv"][:]), "s_cw")
            cw()
            P.op("vector", lambda e, H: e.tensor_scalar(
                out=X["lnm"][:], in0=X["lnm"][:], scalar1=1.0 / D,
                scalar2=None, op0=OP.mult), "s_cw")
            cw()
            P.op("vector", lambda e, H: e.tensor_tensor(
                out=X["lnr"][:], in0=X["lnm"][:], in1=X["lnm"][:],
                op=OP.mult), "s_cw")
            cw()
            P.op("vector", lambda e, H: e.scalar_tensor_tensor(
                out=X["lnv"][:], in0=X["lnv"][:], scalar=1.0 / D,
                in1=X["lnr"][:], op0=OP.mult, op1=OP.subtract), "s_dve")
            v2 = P.n("s_dve")
            P.wait("scalar", "s_dve", v2)
            P.op("scalar", lambda e, H: e.activation(
                out=X["lnv"][:], in_=X["lnv"][:], func=AF.Ln, bias=X["epsc"][:],
                scale=1.0), "s_cs")
            cs()
            P.op("scalar", lambda e, H: e.activation(
                out=X["lnv"][:], in_=X["lnv"][:], func=AF.Exp, bias=0.0,
                scale=-0.5), "s_act")
            a2 = P.n("s_act")
            P.wait("vector", "s_act", a2)
            P.op("vector", lambda e, H: e.tensor_scalar(
                out=osta[:, 0:D], in0=accq[:, 0:D], scalar1=X["lnm"][:],
                scalar2=X["lnv"][:], op0=OP.subtract, op1=OP.mult), "s_dve")
            o2 = P.n("s_dve")
            P.wait("scalar", "s_dve", o2)
            P.op("scalar", lambda e, H, j=j: e.dma_start(
                out=out[j * 128:(j + 1) * 128, :], in_=osta[:, 0:D]),
                "s_st", 16)
        P.wait("scalar", "s_st", P.n("s_st"))
        P.wait("gpsimd", "s_st", P.n("s_st"))
        P.wait("gpsimd", "s_gdma", P.n("s_gdma"))
        P.wait("vector", "s_st", P.n("s_st"))

        with PSE:
            with contextlib.ExitStack() as semctx:
                handles = {}
                for name in P.sems:
                    handles[name] = semctx.enter_context(nc.semaphore(name))
                block = semctx.enter_context(nc.Block())
                eng_items = {k: [] for k in
                             ["sync", "tensor", "scalar", "vector", "gpsimd"]}
                for eng, fn in P.items:
                    eng_items[eng].append(fn)

                @block.sync
                def _(eng):
                    for fn in eng_items["sync"]:
                        fn(eng, handles)

                @block.tensor
                def _(eng):
                    for fn in eng_items["tensor"]:
                        fn(eng, handles)

                @block.scalar
                def _(eng):
                    for fn in eng_items["scalar"]:
                        fn(eng, handles)

                @block.vector
                def _(eng):
                    for fn in eng_items["vector"]:
                        fn(eng, handles)

                @block.gpsimd
                def _(eng):
                    for fn in eng_items["gpsimd"]:
                        fn(eng, handles)

    return nc


def _prep_inputs(inputs, seq):
    x = np.asarray(inputs["L0_attn_x"])
    xt = np.ascontiguousarray(x.transpose(0, 2, 1))
    in_maps = []
    for c in range(NCORE):
        b, q = c // 4, c % 4
        m = {"xT": np.ascontiguousarray(xt[b][:, q * seq:(q + 1) * seq])}
        names = [("L0_attn_Wz0", "L0_attn_Wh0", "L0_attn_bz0", "L0_attn_bh0"),
                 ("L0_attn_Wz1", "L0_attn_Wh1", "L0_attn_bz1", "L0_attn_bh1")]
        bz2 = np.zeros((128, 2 * ND), np.float32)
        bh2 = np.zeros((128, 2 * ND), np.float32)
        def _wlay(w):
            a = np.asarray(w).T.reshape(ND, 128, D).transpose(1, 0, 2)
            blocks = [a[:, :, d * 128:(d + 1) * 128].reshape(128, ND * 128)
                      for d in range(ND)]
            return np.ascontiguousarray(np.concatenate(blocks, axis=1))

        for l, (wz_, wh_, bz_, bh_) in enumerate(names):
            m[f"wzT{l}"] = _wlay(inputs[wz_])
            m[f"whT{l}"] = _wlay(inputs[wh_])
            bz2[:, l * ND:(l + 1) * ND] = np.asarray(
                inputs[bz_]).reshape(ND, 128).T
            bh2[:, l * ND:(l + 1) * ND] = np.asarray(
                inputs[bh_]).reshape(ND, 128).T
        m["bzin"] = bz2
        m["bhin"] = bh2
        m["wr"] = np.ascontiguousarray(np.asarray(inputs["L0_ffn_Wr"]))
        m["br"] = np.ascontiguousarray(
            np.asarray(inputs["L0_ffn_br"]).reshape(E, 1))
        wu_ = np.asarray(inputs["L0_ffn_W_up"])[c // 2]
        wa = wu_.reshape(ND, 128, F_UP).transpose(1, 0, 2)
        wu2 = np.zeros((128, NF * ND * 128), np.float32)
        for fc in range(NF):
            rows = FCH[fc]
            wu2[:, fc * 1024:fc * 1024 + ND * rows] = wa[
                :, :, fc * 128:fc * 128 + rows].reshape(128, ND * rows)
        m["wu"] = wu2
        bu_ = np.zeros((128, NF), np.float32)
        buv = np.asarray(inputs["L0_ffn_b_up"])[c // 2]
        for fc in range(NF):
            bu_[0:FCH[fc], fc] = buv[fc * 128:fc * 128 + FCH[fc]]
        m["bu"] = bu_
        m["wd"] = np.ascontiguousarray(np.asarray(inputs["L0_ffn_W_down"])[c])
        m["bd"] = np.ascontiguousarray(
            np.asarray(inputs["L0_ffn_b_down"])[c].reshape(1, D))
        cm = np.zeros((128, NCORE), np.float32)
        for r in range(4 * b, 4 * b + q):
            cm[:, r] = 1.0
        m["cmask"] = cm
        m["rowsel"] = (np.arange(NCORE, dtype=np.uint32) * E + c).reshape(E, 1)
        oh = np.zeros((E, 1), np.float32)
        oh[c, 0] = 1.0
        m["onehot"] = oh
        in_maps.append(m)
    return in_maps


def kernel(**inputs):
    seq = 2048
    nc = build_program(seq=seq, dbg=False)
    in_maps = _prep_inputs(inputs, seq)
    res = run_bass_kernel_spmd(nc, in_maps, core_ids=list(range(NCORE)))
    full = np.concatenate([r["out"] for r in res.results], axis=0)
    return full.reshape(2, 8192, 1024)


# version 04 (final)
# speedup vs baseline: 1.2640x; 1.0184x over previous
import contextlib
import numpy as np
import concourse.bass as bass
import concourse.mybir as mybir
from concourse.bass import IndirectOffsetOnAxis
from concourse.bass_utils import run_bass_kernel_spmd
from concourse.alu_op_type import AluOpType as OP

F32 = mybir.dt.float32
BF16 = mybir.dt.bfloat16
U32 = mybir.dt.uint32
I32 = mybir.dt.int32
AF = mybir.ActivationFunctionType

D = 1024
ND = 8
E = 8
NCORE = 8
F_UP = 2730
NF = 22
FCH = [128] * 21 + [42]
EPS = 1e-5


class Prog:
    def __init__(self):
        self.items = []
        self.sems = {}

    def sem(self, name):
        self.sems.setdefault(name, 0)

    def n(self, name):
        return self.sems[name]

    def op(self, eng, emitfn, sem=None, k=1, emit_k=None):
        if sem is not None:
            self.sems[sem] = self.sems.get(sem, 0) + k
            tgt = self.sems[sem]
            ek = k if emit_k is None else emit_k

            def fn(e, H, emitfn=emitfn, sem=sem, ek=ek):
                emitfn(e, H).then_inc(H[sem], ek)
            self.items.append((eng, fn))
            return tgt
        self.items.append((eng, lambda e, H, emitfn=emitfn: emitfn(e, H)))
        return None

    def wait(self, eng, sem, val=None):
        if val is None:
            val = self.sems.get(sem, 0)
        if val > 0:
            self.items.append(
                (eng, lambda e, H, sem=sem, val=val: e.wait_ge(H[sem], val)))
        return val


def build_program(seq=2048, dbg=False):
    NT = seq // 128
    NQ = seq // 512
    NTOK = NCORE * seq
    KSEL = NTOK // E
    NSL = KSEL // 128
    NPQ = NTOK // 128
    BIS_F = NTOK // 64
    NKQ = KSEL // 512
    PACKW = 2 * NPQ
    NBITS = max(1, (NPQ - 1).bit_length())
    MIDW = NF * 512

    nc = bass.Bass("TRN2", target_bir_lowering=False, debug=False,
                   detect_race_conditions=False)

    xT = nc.dram_tensor("xT", [D, seq], F32, kind="ExternalInput").ap()
    wzT = [nc.dram_tensor(f"wzT{l}", [128, ND * D], F32,
                          kind="ExternalInput").ap() for l in range(2)]
    whT = [nc.dram_tensor(f"whT{l}", [128, ND * D], F32,
                          kind="ExternalInput").ap() for l in range(2)]
    bzin = nc.dram_tensor("bzin", [128, 2 * ND], F32, kind="ExternalInput").ap()
    bhin = nc.dram_tensor("bhin", [128, 2 * ND], F32, kind="ExternalInput").ap()
    wr = nc.dram_tensor("wr", [D, E], F32, kind="ExternalInput").ap()
    br = nc.dram_tensor("br", [E, 1], F32, kind="ExternalInput").ap()
    wu = nc.dram_tensor("wu", [128, NF * ND * 128], F32,
                        kind="ExternalInput").ap()
    bu = nc.dram_tensor("bu", [128, NF], F32, kind="ExternalInput").ap()
    wd = nc.dram_tensor("wd", [F_UP, D], F32, kind="ExternalInput").ap()
    bd = nc.dram_tensor("bd", [1, D], F32, kind="ExternalInput").ap()
    cmask = nc.dram_tensor("cmask", [128, NCORE], F32, kind="ExternalInput").ap()
    rowsel = nc.dram_tensor("rowsel", [E, 1], U32, kind="ExternalInput").ap()
    onehot = nc.dram_tensor("onehot", [E, 1], F32, kind="ExternalInput").ap()
    out = nc.dram_tensor("out", [seq, D], F32, kind="ExternalOutput").ap()
    dbg_t = {}
    if dbg:
        dbg_t["x1tm"] = nc.dram_tensor("dbg_x1tm", [seq, D], BF16,
                                       kind="ExternalOutput").ap()
        dbg_t["probs"] = nc.dram_tensor("dbg_probs", [E, seq], F32,
                                        kind="ExternalOutput").ap()
        dbg_t["tau"] = nc.dram_tensor("dbg_tau", [64, 4], F32,
                                      kind="ExternalOutput").ap()
        dbg_t["ids"] = nc.dram_tensor("dbg_ids", [KSEL, 1], U32,
                                      kind="ExternalOutput").ap()
        dbg_t["gslot"] = nc.dram_tensor("dbg_gslot", [1, KSEL], F32,
                                        kind="ExternalOutput").ap()
        dbg_t["y"] = nc.dram_tensor("dbg_y", [seq, D], BF16,
                                    kind="ExternalOutput").ap()
        dbg_t["toks"] = nc.dram_tensor("dbg_toks", [KSEL, D], BF16,
                                       kind="ExternalOutput").ap()
        dbg_t["mid0"] = nc.dram_tensor("dbg_mid0", [128, NF * 512], BF16,
                                       kind="ExternalOutput").ap()
        dbg_t["yin"] = nc.dram_tensor("dbg_yin", [NTOK, D], BF16,
                                      kind="ExternalOutput").ap()
        dbg_t["xbf"] = nc.dram_tensor("dbg_xbf", [128, ND * seq], BF16,
                                      kind="ExternalOutput").ap()
        dbg_t["h1"] = nc.dram_tensor("dbg_h1", [128, ND * seq], BF16,
                                     kind="ExternalOutput").ap()
        dbg_t["car"] = nc.dram_tensor("dbg_car", [128, NCORE * 32], F32,
                                      kind="ExternalOutput").ap()
        dbg_t["y1"] = nc.dram_tensor("dbg_y1", [128, ND * seq], BF16,
                                     kind="ExternalOutput").ap()
        dbg_t["hst"] = nc.dram_tensor("dbg_hst", [128, 2 * ND], F32,
                                      kind="ExternalOutput").ap()
        dbg_t["car0"] = nc.dram_tensor("dbg_car0", [128, NCORE * 32], F32,
                                       kind="ExternalOutput").ap()
        dbg_t["cmsk"] = nc.dram_tensor("dbg_cmsk", [128, NCORE], F32,
                                       kind="ExternalOutput").ap()

    d_car_in = nc.dram_tensor("d_car_in", [128, 32], F32).ap()
    d_car_out = nc.dram_tensor("d_car_out", [128 * NCORE, 32], F32).ap()
    d_probs_in = nc.dram_tensor("d_probs_in", [E, seq], F32).ap()
    d_probs_out = nc.dram_tensor("d_probs_out", [E * NCORE, seq], F32).ap()
    d_x1_in = nc.dram_tensor("d_x1_in", [seq, D], BF16).ap()
    d_x1_out = nc.dram_tensor("d_x1_out", [NTOK, D], BF16).ap()
    d_y_in = nc.dram_tensor("d_y_in", [NTOK, D], BF16).ap()
    d_y_out = nc.dram_tensor("d_y_out", [seq, D], BF16).ap()
    d_escore = nc.dram_tensor("d_escore", [NTOK, 1], F32).ap()
    d_ids = nc.dram_tensor("d_ids", [KSEL, 1], U32).ap()
    d_rows = nc.dram_tensor("d_rows", [2, NTOK], F32).ap()
    d_grow = nc.dram_tensor("d_grow", [1, KSEL], F32).ap()
    d_bar_in = nc.dram_tensor("d_bar_in", [8, 4], F32).ap()
    d_bar_out = nc.dram_tensor("d_bar_out", [8, 4], F32).ap()

    P = Prog()
    for s in ["s_in", "s_pe", "s_act", "s_dve", "s_gp", "s_gdma", "s_adma",
              "s_zero", "s_cc", "s_st", "s_cw", "s_cs"]:
        P.sem(s)

    def cc_delay(P, X, n=150):
        for _ in range(n):
            P.op("gpsimd", lambda e, H: e.memset(X["zrow"][:, 0:512], 0.0))

    def cw():
        P.wait("vector", "s_cw", P.n("s_cw"))

    def cs():
        P.wait("scalar", "s_cs", P.n("s_cs"))

    ctx = contextlib.ExitStack()
    X = {}

    def T(name, shape, dt):
        X[name] = ctx.enter_context(nc.sbuf_tensor(name, shape, dt))
        return X[name]

    T("xbf", [128, ND * seq], BF16)
    T("h1a", [128, ND * seq], BF16)
    T("hlc", [128, ND * seq], BF16)
    T("ptl", [128, ND * seq], BF16)
    T("cbuf", [128, seq], BF16)
    T("vbuf", [128, seq], BF16)
    T("spb", [128, 1024], BF16)
    T("sigk", [128, 512], BF16)
    T("wstg", [128, 2048], F32)
    T("wzb", [128, D], BF16)
    T("whb", [128, D], BF16)
    T("wub2", [128, 2048], BF16)
    T("wdb2", [128, 2048], BF16)
    T("iota128f", [128, 128], F32)
    T("iotapf", [128, 1], F32)
    T("iotapi", [128, 1], I32)
    T("identb", [128, 128], BF16)
    T("trilb", [128, 128], BF16)
    T("onesb", [128, 128], BF16)
    T("ones64", [64, 64], BF16)
    T("ones8f", [E, 1], F32)
    T("bzt", [128, 2 * ND], F32)
    T("bht", [128, 2 * ND], F32)
    T("bzn", [128, 2 * ND], F32)
    T("bh05", [128, 2 * ND], F32)
    T("wrb", [128, ND * E], BF16)
    T("brt", [E, 1], F32)
    T("oneh", [E, 1], F32)
    T("rsel", [E, 1], U32)
    T("cmsk", [128, NCORE], F32)
    T("carst", [128, 32], F32)
    T("carall", [128, NCORE * 32 + 64], F32)
    T("hstart", [128, 128], F32)
    T("tmpc", [128, 64], F32)
    T("hsB", [128, 128], F32)
    T("mstat", [128, NT], F32)
    T("rstdv", [128, NT], F32)
    T("mrv", [128, NT], F32)
    T("stsc", [128, 1024], F32)
    T("zrow", [128, 1024], BF16)
    T("zero1", [128, 1], F32)
    T("zero1b", [128, 1], BF16)
    T("sc64", [64, BIS_F], F32)
    T("j64", [64, BIS_F], BF16)
    T("lo", [64, 1], F32)
    T("hi", [64, 1], F32)
    T("midv", [64, 1], F32)
    T("cnt", [64, 1], F32)
    T("cntb", [64, 1], BF16)
    T("bmask", [64, 1], I32)
    T("nbmask", [64, 1], I32)
    T("mtot", [64, 1], F32)
    T("rneed", [64, 1], F32)
    T("tau128", [128, 1], F32)
    T("r128", [128, 1], F32)
    T("npp", [128, 1], F32)
    T("mpp", [128, 1], F32)
    T("mppb", [128, 1], BF16)
    T("mhib", [128, 1], BF16)
    T("mlob", [128, 1], BF16)
    T("t1c", [128, 1], F32)
    T("t2c", [128, 1], F32)
    T("tpackb", [128, 128], BF16)
    T("idtiles", [128, NSL], U32)
    T("gslot_tm", [128, NSL], F32)
    T("but", [128, NF], F32)
    T("bdt", [128, D], F32)
    T("lnm", [128, 1], F32)
    T("lnv", [128, 1], F32)
    T("lnr", [128, 1], F32)
    T("epsc", [128, 1], F32)

    with ctx:
        xbf, h1a, hlc, ptl = X["xbf"], X["h1a"], X["hlc"], X["ptl"]

        def bh2(n):
            return n // 2
        rawsb = xbf[:, bh2(0):bh2(8192)].bitcast(F32)[0:E, :]
        rows8 = xbf[:, bh2(8192):bh2(16384)].bitcast(F32)[0:E, :]
        prsb = xbf[:, bh2(16384):bh2(24576)].bitcast(F32)[0:E, :]
        iotaf2 = xbf[:, bh2(24576):bh2(24576) + 2048].bitcast(F32)
        iotaf2 = xbf[:, bh2(24576):bh2(32768)].bitcast(F32)
        psrow = xbf[:, bh2(0):bh2(8192)].bitcast(F32)[0:1, :]
        workA = xbf[:, bh2(8192):bh2(16384)].bitcast(F32)[0:1, :]
        workB = xbf[:, bh2(16384):bh2(24576)].bitcast(F32)[0:1, :]
        idrowu = xbf[:, bh2(8192):bh2(16384)].bitcast(U32)[0:1, :]
        gexp = xbf[:, bh2(0):bh2(8192)].bitcast(F32)[0:E, :]
        grecip = xbf[:, bh2(16384):bh2(24576)].bitcast(F32)[0:1, :]
        gslotr = xbf[:, bh2(24576):bh2(32768)].bitcast(F32)[0:1, :]
        mid = xbf[:, 0:MIDW]
        _cur = [0]

        def _cv(nbf):
            a, b = _cur[0], _cur[0] + nbf
            _cur[0] = b
            return h1a[:, a:b]
        m12 = _cv(KSEL)
        jhT = _cv(512)
        r2T = _cv(512)
        psrowb = _cv(KSEL)[0:1, :]
        jrowb = _cv(KSEL)[0:1, :]
        mgt = _cv(2 * NPQ).bitcast(F32)
        meq = _cv(2 * NPQ).bitcast(F32)
        msel = _cv(2 * NPQ).bitcast(F32)
        tcum = _cv(2 * NPQ).bitcast(F32)
        scrm = _cv(2 * NPQ).bitcast(F32)
        sc128 = _cv(2 * NPQ).bitcast(F32)
        fpV = _cv(2 * PACKW).bitcast(F32)
        fpM = _cv(2 * PACKW).bitcast(F32)
        fpmv = _cv(2 * PACKW).bitcast(F32)
        fpnm = _cv(2 * PACKW).bitcast(F32)
        fpB = [_cv(2 * PACKW).bitcast(F32) for _ in range(NBITS)]
        fpmvi = _cv(PACKW).bitcast(mybir.dt.uint8)[:, 0:PACKW]
        assert _cur[0] <= ND * seq, _cur[0]
        wrow = ptl[:, 0:1024]
        wrow2 = ptl[:, 7168:8192]
        accq = ptl[:, 1024:1024 + 2048].bitcast(F32)
        x1re = ptl[:, 3072:4096]
        yre = ptl[:, 4096:5120]
        osta = ptl[:, 5120:7168].bitcast(F32)

        PSE = contextlib.ExitStack()
        psK = PSE.enter_context(nc.psum_tensor("psK", [128, 1024], F32))
        psP = PSE.enter_context(nc.psum_tensor("psP", [128, 1024], F32))
        psT = PSE.enter_context(nc.psum_tensor("psT", [128, 2048], BF16))
        psS = PSE.enter_context(nc.psum_tensor("psS", [128, 1024], F32))
        psTf = psT.ap().bitcast(F32)

        def psKb(i):
            return psK[:, i * 512:(i + 1) * 512]

        def psPb(i):
            return psP[:, i * 512:(i + 1) * 512]

        def psSb(i):
            return psS[:, i * 512:(i + 1) * 512]

        def psTb(i):
            return psT[:, i * 1024:(i + 1) * 1024]

        def psTfb(i):
            return psTf[:, i * 512:(i + 1) * 512]

        def dt_(t, d):
            return t[:, d * seq:(d + 1) * seq]

        P.op("sync", lambda e, H: e.dma_start(out=X["bzt"][:], in_=bzin[:, :]),
             "s_in", 16)
        P.op("sync", lambda e, H: e.dma_start(out=X["bht"][:], in_=bhin[:, :]),
             "s_in", 16)
        P.op("sync", lambda e, H: e.dma_start(
            out=X["wstg"][:, 0:ND * E],
            in_=wr.rearrange("(c p) e -> p c e", p=128)), "s_in", 16)
        P.op("sync", lambda e, H: e.dma_start(out=X["brt"][:], in_=br[:, :]),
             "s_in", 16)
        P.op("sync", lambda e, H: e.dma_start(out=X["cmsk"][:], in_=cmask[:, :]),
             "s_in", 16)
        P.op("sync", lambda e, H: e.dma_start(out=X["oneh"][:], in_=onehot[:, :]),
             "s_in", 16)
        P.op("sync", lambda e, H: e.dma_start(out=X["rsel"][:], in_=rowsel[:, :]),
             "s_in", 16)
        P.op("sync", lambda e, H: e.dma_start(out=X["but"][:], in_=bu[:, :]),
             "s_in", 16)
        P.op("sync", lambda e, H: e.dma_start(
            out=X["bdt"][:], in_=bd[0:1, :].partition_broadcast(128)),
            "s_in", 16)
        in0 = P.n("s_in")

        P.op("gpsimd", lambda e, H: e.iota(
            X["iotapi"][:], pattern=[[0, 1]], base=0, channel_multiplier=1))
        P.op("gpsimd", lambda e, H: e.iota(
            X["iota128f"].bitcast(I32)[:], pattern=[[1, 128]], base=0,
            channel_multiplier=0), "s_gp")
        gp0 = P.n("s_gp")

        P.wait("vector", "s_gp", gp0)
        P.op("vector", lambda e, H: e.tensor_copy(out=X["iotapf"][:],
                                                  in_=X["iotapi"][:]))
        P.op("vector", lambda e, H: e.tensor_copy(
            out=X["iota128f"][:], in_=X["iota128f"].bitcast(I32)[:]), "s_cw")
        cw()
        P.op("vector", lambda e, H: e.tensor_tensor(
            out=X["identb"][:], in0=X["iota128f"][:],
            in1=X["iotapf"][:].to_broadcast([128, 128]), op=OP.is_equal))
        P.op("vector", lambda e, H: e.tensor_tensor(
            out=X["trilb"][:], in0=X["iota128f"][:],
            in1=X["iotapf"][:].to_broadcast([128, 128]), op=OP.is_gt))
        P.op("vector", lambda e, H: e.memset(X["onesb"][:], 1.0))
        P.op("vector", lambda e, H: e.memset(X["ones64"][:], 1.0))
        P.op("vector", lambda e, H: e.memset(X["ones8f"][:], 1.0))
        P.op("vector", lambda e, H: e.memset(X["zero1"][:], 0.0))
        P.op("vector", lambda e, H: e.memset(X["zero1b"][:], 0.0))
        P.op("vector", lambda e, H: e.memset(X["epsc"][:], EPS))
        P.op("vector", lambda e, H: e.memset(X["zrow"][:], 0.0))
        P.op("vector", lambda e, H: e.memset(X["carst"][:], 0.0))
        P.op("vector", lambda e, H: e.memset(X["hstart"][:], 0.0))
        P.op("vector", lambda e, H: e.memset(X["hsB"][:], 0.0))
        P.op("vector", lambda e, H: e.memset(
            X["carall"][:, NCORE * 32:NCORE * 32 + 64], 0.0))
        P.wait("vector", "s_in", in0)
        P.op("vector", lambda e, H: e.tensor_scalar(
            out=X["bzn"][:], in0=X["bzt"][:], scalar1=-1.0, scalar2=None,
            op0=OP.mult))
        P.op("vector", lambda e, H: e.tensor_scalar(
            out=X["bh05"][:], in0=X["bht"][:], scalar1=0.5, scalar2=None,
            op0=OP.add))
        P.op("vector", lambda e, H: e.tensor_copy(
            out=X["wrb"][:], in_=X["wstg"][:, 0:ND * E]), "s_dve")
        v0 = P.n("s_dve")

        P.wait("scalar", "s_dve", v0)
        P.op("scalar", lambda e, H: e.dma_start(
            out=d_bar_in[:, :], in_=X["zrow"].bitcast(F32)[0:8, 0:4]),
            "s_zero", 16)
        bar_z = P.n("s_zero")
        for i in range(NTOK // 128):
            P.op("scalar", lambda e, H, i=i: e.dma_start(
                out=d_y_in[i * 128:(i + 1) * 128, :], in_=X["zrow"][:, 0:D]),
                "s_zero", 16)

        P.wait("sync", "s_dve", v0)
        for d in range(ND):
            P.wait("sync", "s_gp", P.n("s_gp"))
            P.op("sync", lambda e, H, d=d: e.dma_start(
                out=X["wstg"][:, 0:seq], in_=xT[d * 128:(d + 1) * 128, :]),
                "s_in", 16)
            xl = P.n("s_in")
            P.wait("gpsimd", "s_in", xl)
            P.op("gpsimd", lambda e, H, d=d: e.tensor_copy(
                out=dt_(xbf, d), in_=X["wstg"][:, 0:seq]), "s_gp")
        xld = P.n("s_gp")
        if dbg:
            P.wait("scalar", "s_gp", xld)
            P.op("scalar", lambda e, H: e.dma_start(
                out=dbg_t["xbf"][:, :], in_=xbf[:]), "s_adma", 16)

        def mingru_layer(l, src, dst, ccol):
            for d in range(ND):
                P.wait("sync", "s_gp", P.n("s_gp"))
                P.op("sync", lambda e, H, d=d, l=l: e.dma_start(
                    out=X["wstg"][:, 0:D],
                    in_=wzT[l][:, d * D:(d + 1) * D]), "s_in", 16)
                wl1 = P.n("s_in")
                P.wait("gpsimd", "s_in", wl1)
                P.wait("gpsimd", "s_pe", P.n("s_pe"))
                P.op("gpsimd", lambda e, H: e.tensor_copy(
                    out=X["wzb"][:], in_=X["wstg"][:, 0:D]), "s_gp")
                P.wait("sync", "s_gp", P.n("s_gp"))
                P.op("sync", lambda e, H, d=d, l=l: e.dma_start(
                    out=X["wstg"][:, D:2 * D],
                    in_=whT[l][:, d * D:(d + 1) * D]), "s_in", 16)
                wl2 = P.n("s_in")
                P.wait("gpsimd", "s_in", wl2)
                P.op("gpsimd", lambda e, H: e.tensor_copy(
                    out=X["whb"][:], in_=X["wstg"][:, D:2 * D]), "s_gp")
                wcast = P.n("s_gp")

                P.wait("tensor", "s_gp", wcast)
                for q in range(NQ):
                    qb = q % 2
                    if not (l == 0 and d == 0 and q < 2):
                        P.wait("tensor", "s_act", P.n("s_act"))
                        P.wait("tensor", "s_dve", P.n("s_dve"))
                    for kc in range(ND):
                        P.op("tensor", lambda e, H, q=q, kc=kc, qb=qb,
                             src=src: e.matmul(
                                 psKb(qb), X["wzb"][:, kc * 128:(kc + 1) * 128],
                                 dt_(src, kc)[:, q * 512:(q + 1) * 512],
                                 start=(kc == 0), stop=(kc == ND - 1)))
                    for kc in range(ND):
                        P.op("tensor", lambda e, H, q=q, kc=kc, qb=qb,
                             src=src: e.matmul(
                                 psPb(qb), X["whb"][:, kc * 128:(kc + 1) * 128],
                                 dt_(src, kc)[:, q * 512:(q + 1) * 512],
                                 start=(kc == 0), stop=(kc == ND - 1)),
                             sem="s_pe" if kc == ND - 1 else None)
                    mmd = P.n("s_pe")
                    P.wait("scalar", "s_pe", mmd)
                    P.op("scalar", lambda e, H, q=q, qb=qb, d=d, l=l:
                         e.activation(
                             out=X["cbuf"][:, q * 512:(q + 1) * 512],
                             in_=psKb(qb), func=AF.Sigmoid,
                             bias=X["bzn"][:, l * ND + d:l * ND + d + 1],
                             scale=-1.0))
                    P.op("scalar", lambda e, H, q=q, qb=qb, d=d, l=l:
                         e.activation(
                             out=X["spb"][:, qb * 512:(qb + 1) * 512],
                             in_=psPb(qb), func=AF.Sigmoid,
                             bias=X["bht"][:, l * ND + d:l * ND + d + 1],
                             scale=1.0), "s_act")
                    acd = P.n("s_act")
                    P.wait("vector", "s_act", acd)
                    P.op("vector", lambda e, H, q=q, qb=qb, d=d, l=l:
                         e.scalar_tensor_tensor(
                             out=X["vbuf"][:, q * 512:(q + 1) * 512],
                             in0=psPb(qb),
                             scalar=X["bh05"][:, l * ND + d:l * ND + d + 1],
                             in1=X["spb"][:, qb * 512:(qb + 1) * 512],
                             op0=OP.add, op1=OP.max))
                    P.op("vector", lambda e, H, q=q: e.tensor_scalar(
                        out=X["sigk"][:],
                        in0=X["cbuf"][:, q * 512:(q + 1) * 512],
                        scalar1=-1.0, scalar2=1.0, op0=OP.mult, op1=OP.add))
                    P.op("vector", lambda e, H, q=q: e.tensor_tensor(
                        out=X["vbuf"][:, q * 512:(q + 1) * 512],
                        in0=X["vbuf"][:, q * 512:(q + 1) * 512],
                        in1=X["sigk"][:], op=OP.mult), "s_dve")
                P.op("vector", lambda e, H, d=d, dst=dst: e.tensor_tensor_scan(
                    out=dt_(dst, d), data0=X["cbuf"][:], data1=X["vbuf"][:],
                    initial=0.0, op0=OP.mult, op1=OP.add))
                P.op("vector", lambda e, H, d=d: e.tensor_tensor_scan(
                    out=dt_(ptl, d), data0=X["cbuf"][:],
                    data1=X["zero1b"][:].to_broadcast([128, seq]), initial=1.0,
                    op0=OP.mult, op1=OP.add))
                P.op("vector", lambda e, H, d=d, cc=ccol, dst=dst:
                     e.tensor_copy(out=X["carst"][:, cc + d:cc + d + 1],
                                   in_=dt_(dst, d)[:, seq - 1:seq]))
                P.op("vector", lambda e, H, d=d, cc=ccol: e.tensor_copy(
                    out=X["carst"][:, cc + 8 + d:cc + 8 + d + 1],
                    in_=dt_(ptl, d)[:, seq - 1:seq]), "s_dve")
            sc_done = P.n("s_dve")
            P.wait("gpsimd", "s_dve", sc_done)
            P.wait("gpsimd", "s_zero", bar_z)
            P.op("gpsimd", lambda e, H: e.dma_start(
                out=d_car_in[:, :], in_=X["carst"][:]), "s_gdma", 16)
            P.wait("gpsimd", "s_gdma")
            P.op("gpsimd", lambda e, H: e.collective_compute(
                "AllGather", OP.bypass, replica_groups=[list(range(NCORE))],
                ins=[d_car_in.opt()], outs=[d_car_out.opt()]), "s_cc", 1)
            P.op("gpsimd", lambda e, H: e.collective_compute(
                "AllReduce", OP.add, replica_groups=[list(range(NCORE))],
                ins=[d_bar_in.opt()], outs=[d_bar_out.opt()]), "s_cc", 1)
            P.wait("gpsimd", "s_cc")
            pass
            P.op("gpsimd", lambda e, H: e.dma_start(
                out=X["carall"][:, 0:NCORE * 32],
                in_=d_car_out.rearrange("(r p) c -> p r c", p=128)),
                "s_gdma", 16)
            if dbg and ccol == 0:
                P.op("gpsimd", lambda e, H: e.dma_start(
                    out=dbg_t["car0"][:, :], in_=X["carall"][:, 0:NCORE * 32]),
                    "s_gdma", 16)
                P.op("gpsimd", lambda e, H: e.dma_start(
                    out=dbg_t["cmsk"][:, :], in_=X["cmsk"][:]), "s_gdma", 16)
            cl = P.n("s_gdma")
            P.wait("vector", "s_gdma", cl)
            hso = l * 64
            bufs = ["hstart", "hsB"]
            P.op("vector", lambda e, H, hso=hso: e.memset(
                X["hstart"][:, hso:hso + 64], 0.5), "s_cw")
            cw()
            for r in range(NCORE):
                hcol = r * 32 + ccol
                pcol = r * 32 + ccol + 8
                sa = bufs[r % 2]
                da = bufs[(r + 1) % 2]
                P.op("vector", lambda e, H, pcol=pcol, sa=sa, hso=hso:
                     e.tensor_tensor(
                         out=X["tmpc"][:], in0=X["carall"][:, pcol:pcol + 64],
                         in1=X[sa][:, hso:hso + 64], op=OP.mult))
                P.op("vector", lambda e, H, hcol=hcol: e.tensor_tensor(
                    out=X["tmpc"][:], in0=X["tmpc"][:],
                    in1=X["carall"][:, hcol:hcol + 64], op=OP.add))
                P.op("vector", lambda e, H, r=r, sa=sa, da=da, hso=hso:
                     e.scalar_tensor_tensor(
                         out=X[da][:, hso:hso + 64], in0=X["tmpc"][:],
                         scalar=X["cmsk"][:, r:r + 1],
                         in1=X[sa][:, hso:hso + 64], op0=OP.mult, op1=OP.add))
                P.op("vector", lambda e, H, r=r, sa=sa, hso=hso:
                     e.tensor_scalar(
                         out=X["tmpc"][:], in0=X[sa][:, hso:hso + 64],
                         scalar1=X["cmsk"][:, r:r + 1], scalar2=None,
                         op0=OP.mult))
                P.op("vector", lambda e, H, da=da, hso=hso: e.tensor_tensor(
                    out=X[da][:, hso:hso + 64], in0=X[da][:, hso:hso + 64],
                    in1=X["tmpc"][:], op=OP.subtract), "s_cw")
                cw()
            for d in range(ND):
                P.op("vector", lambda e, H, d=d, hso=hso, dst=dst:
                     e.scalar_tensor_tensor(
                         out=dt_(dst, d), in0=dt_(ptl, d),
                         scalar=X["hstart"][:, hso + d:hso + d + 1],
                         in1=dt_(dst, d), op0=OP.mult, op1=OP.add),
                     sem="s_dve" if d == ND - 1 else None)
            return P.n("s_dve")

        P.wait("tensor", "s_gp", xld)
        P.wait("vector", "s_gp", xld)
        l0_done = mingru_layer(0, xbf, h1a, 0)
        if dbg:
            P.wait("scalar", "s_dve", l0_done)
            P.op("scalar", lambda e, H: e.dma_start(
                out=dbg_t["hst"][:, :], in_=X["hstart"][:, 0:16]),
                "s_adma", 16)
        if dbg:
            P.wait("scalar", "s_dve", l0_done)
            P.op("scalar", lambda e, H: e.dma_start(
                out=dbg_t["h1"][:, :], in_=h1a[:]), "s_adma", 16)
            P.op("scalar", lambda e, H: e.dma_start(
                out=dbg_t["car"][:, :], in_=X["carall"][:, 0:NCORE * 32]),
                "s_adma", 16)
            P.wait("vector", "s_adma", P.n("s_adma"))
        mingru_layer(1, h1a, hlc, 16)

        for d in range(ND):
            P.op("vector", lambda e, H, d=d: e.tensor_tensor(
                out=dt_(hlc, d), in0=dt_(hlc, d), in1=dt_(xbf, d), op=OP.add),
                sem="s_dve" if d == ND - 1 else None)
        y1_done = P.n("s_dve")
        if dbg:
            P.wait("scalar", "s_dve", y1_done)
            P.op("scalar", lambda e, H: e.dma_start(
                out=dbg_t["y1"][:, :], in_=hlc[:]), "s_adma", 16)
            P.wait("vector", "s_adma", P.n("s_adma"))

        P.wait("tensor", "s_dve", y1_done)
        for q in range(NQ):
            if q >= 2:
                P.wait("tensor", "s_act", P.n("s_act"))
            for d in range(ND):
                P.op("tensor", lambda e, H, q=q, d=d: e.matmul(
                    psSb(q % 2)[0:E, :], X["wrb"][:, d * E:(d + 1) * E],
                    dt_(hlc, d)[:, q * 512:(q + 1) * 512],
                    start=(d == 0), stop=(d == ND - 1)),
                    sem="s_pe" if d == ND - 1 else None)
            rq = P.n("s_pe")
            P.wait("scalar", "s_pe", rq)
            P.op("scalar", lambda e, H, q=q: e.copy(
                out=rawsb[:, q * 512:(q + 1) * 512], in_=psSb(q % 2)[0:E, :]),
                "s_act")
        raw_done = P.n("s_act")

        P.wait("tensor", "s_act", raw_done)
        for j in range(NT):
            tb = j % 2
            if j >= 2:
                P.wait("tensor", "s_dve", P.n("s_dve"))
            for d in range(ND):
                P.op("tensor", lambda e, H, j=j, d=d, tb=tb: e.transpose(
                    psTb(tb)[:, d * 128:(d + 1) * 128],
                    dt_(hlc, d)[:, j * 128:(j + 1) * 128], X["identb"][:]),
                    sem="s_pe" if d == ND - 1 else None)
            tp = P.n("s_pe")
            P.wait("vector", "s_pe", tp)
            P.op("vector", lambda e, H, j=j, tb=tb: e.tensor_copy(
                out=ptl[:, j * 1024:(j + 1) * 1024], in_=psTb(tb)[:]), "s_dve")
        y1tm_done = P.n("s_dve")
        y1tm = ptl

        for j in range(NT):
            P.op("vector", lambda e, H, j=j: e.tensor_scalar(
                out=X["stsc"][:], in0=y1tm[:, j * 1024:(j + 1) * 1024],
                scalar1=1.0, scalar2=0.0, op0=OP.mult, op1=OP.add,
                accum_out=X["mstat"][:, j:j + 1]), "s_dve")
            sqv = P.n("s_dve")
            P.wait("scalar", "s_dve", sqv)
            P.op("scalar", lambda e, H, j=j: e.activation(
                out=X["stsc"][:], in_=y1tm[:, j * 1024:(j + 1) * 1024],
                func=AF.Square, bias=0.0, scale=1.0), "s_act")
            P.wait("vector", "s_act", P.n("s_act"))
            P.op("vector", lambda e, H, j=j: e.tensor_scalar(
                out=X["stsc"][:], in0=X["stsc"][:], scalar1=1.0, scalar2=0.0,
                op0=OP.mult, op1=OP.add, accum_out=X["rstdv"][:, j:j + 1]),
                "s_dve")
        P.op("vector", lambda e, H: e.tensor_scalar(
            out=X["mstat"][:], in0=X["mstat"][:], scalar1=1.0 / D,
            scalar2=None, op0=OP.mult), "s_cw")
        cw()
        P.op("vector", lambda e, H: e.tensor_tensor(
            out=X["mrv"][:], in0=X["mstat"][:], in1=X["mstat"][:],
            op=OP.mult), "s_cw")
        cw()
        P.op("vector", lambda e, H: e.scalar_tensor_tensor(
            out=X["rstdv"][:], in0=X["rstdv"][:], scalar=1.0 / D,
            in1=X["mrv"][:], op0=OP.mult, op1=OP.subtract), "s_dve")
        var_done = P.n("s_dve")
        P.wait("scalar", "s_dve", var_done)
        P.op("scalar", lambda e, H: e.activation(
            out=X["rstdv"][:], in_=X["rstdv"][:], func=AF.Ln, bias=X["epsc"][:],
            scale=1.0), "s_cs")
        cs()
        P.op("scalar", lambda e, H: e.activation(
            out=X["rstdv"][:], in_=X["rstdv"][:], func=AF.Exp, bias=0.0,
            scale=-0.5), "s_act")
        rstd_done = P.n("s_act")
        P.wait("vector", "s_act", rstd_done)
        P.op("vector", lambda e, H: e.tensor_tensor(
            out=X["mrv"][:], in0=X["mstat"][:], in1=X["rstdv"][:], op=OP.mult),
            "s_dve")
        mr_done = P.n("s_dve")

        P.wait("scalar", "s_dve", mr_done)
        for j in range(NT):
            P.op("scalar", lambda e, H, j=j: e.dma_start(
                out=d_rows[0:1, j * 128:(j + 1) * 128].rearrange("o t -> t o"),
                in_=X["rstdv"][:, j:j + 1]), "s_adma", 16)
            P.op("scalar", lambda e, H, j=j: e.dma_start(
                out=d_rows[1:2, j * 128:(j + 1) * 128].rearrange("o t -> t o"),
                in_=X["mrv"][:, j:j + 1]), "s_adma", 16)
        srw = P.n("s_adma")
        P.wait("tensor", "s_act", raw_done)
        for d in range(ND):
            P.op("tensor", lambda e, H, d=d: e.matmul(
                psSb(0)[0:E, 0:1], X["wrb"][:, d * E:(d + 1) * E],
                X["onesb"][:, 0:1], start=(d == 0), stop=(d == ND - 1)),
                sem="s_pe" if d == ND - 1 else None)
        sw_done = P.n("s_pe")
        P.wait("vector", "s_pe", sw_done)
        P.op("vector", lambda e, H: e.tensor_scalar(
            out=X["cnt"][0:E, :], in0=psSb(0)[0:E, 0:1], scalar1=-1.0,
            scalar2=None, op0=OP.mult), "s_dve")
        negsw_done = P.n("s_dve")

        P.wait("sync", "s_adma", srw)
        P.op("sync", lambda e, H: e.dma_start(
            out=rows8[:, 0:seq], in_=d_rows[0:1, 0:seq].partition_broadcast(E)),
            "s_in", 16)
        r8a = P.n("s_in")
        P.wait("vector", "s_in", r8a)
        P.op("vector", lambda e, H: e.tensor_tensor(
            out=rawsb[:, 0:seq], in0=rawsb[:, 0:seq], in1=rows8[:, 0:seq],
            op=OP.mult), "s_dve")
        lg1 = P.n("s_dve")
        P.wait("sync", "s_dve", lg1)
        P.op("sync", lambda e, H: e.dma_start(
            out=rows8[:, 0:seq], in_=d_rows[1:2, 0:seq].partition_broadcast(E)),
            "s_in", 16)
        r8b = P.n("s_in")
        P.wait("vector", "s_in", r8b)
        P.wait("vector", "s_dve", negsw_done)
        P.op("vector", lambda e, H: e.scalar_tensor_tensor(
            out=rawsb[:, 0:seq], in0=rows8[:, 0:seq],
            scalar=X["cnt"][0:E, 0:1], in1=rawsb[:, 0:seq],
            op0=OP.mult, op1=OP.add), "s_dve")
        lg2 = P.n("s_dve")
        P.wait("scalar", "s_dve", lg2)
        P.op("scalar", lambda e, H: e.activation(
            out=prsb[:, 0:seq], in_=rawsb[:, 0:seq], func=AF.Exp,
            bias=X["brt"][:], scale=1.0), "s_act")
        exp_done = P.n("s_act")
        P.wait("tensor", "s_act", exp_done)
        for q in range(NQ):
            P.op("tensor", lambda e, H, q=q: e.matmul(
                psSb(1)[0:1, :], X["ones8f"][:, 0:1],
                prsb[:, q * 512:(q + 1) * 512], start=True, stop=True), "s_pe")
            dq = P.n("s_pe")
            P.wait("vector", "s_pe", dq)
            P.op("vector", lambda e, H, q=q: e.reciprocal(
                out=rawsb[0:1, q * 512:(q + 1) * 512], in_=psSb(1)[0:1, :]),
                "s_dve")
            P.wait("tensor", "s_dve", P.n("s_dve"))
        rec_done = P.n("s_dve")
        P.wait("scalar", "s_dve", rec_done)
        P.op("scalar", lambda e, H: e.dma_start(
            out=d_rows[0:1, 0:seq], in_=rawsb[0:1, 0:seq]), "s_adma", 16)
        P.wait("sync", "s_adma", P.n("s_adma"))
        P.op("sync", lambda e, H: e.dma_start(
            out=rows8[:, 0:seq], in_=d_rows[0:1, 0:seq].partition_broadcast(E)),
            "s_in", 16)
        r8c = P.n("s_in")
        P.wait("vector", "s_in", r8c)
        P.op("vector", lambda e, H: e.tensor_tensor(
            out=prsb[:, 0:seq], in0=prsb[:, 0:seq], in1=rows8[:, 0:seq],
            op=OP.mult), "s_dve")
        probs_done = P.n("s_dve")
        P.wait("scalar", "s_dve", probs_done)
        P.op("scalar", lambda e, H: e.dma_start(
            out=d_probs_in[:, :], in_=prsb[:, 0:seq]), "s_adma", 16)
        pw = P.n("s_adma")
        P.wait("gpsimd", "s_adma", pw)
        P.op("gpsimd", lambda e, H: e.collective_compute(
            "AllGather", OP.bypass, replica_groups=[list(range(NCORE))],
            ins=[d_probs_in.opt()], outs=[d_probs_out.opt()]), "s_cc", 1)
        P.op("gpsimd", lambda e, H: e.collective_compute(
            "AllReduce", OP.add, replica_groups=[list(range(NCORE))],
            ins=[d_bar_in.opt()], outs=[d_bar_out.opt()]), "s_cc", 1)
        cc_probs = P.n("s_cc")
        if dbg:
            P.wait("scalar", "s_dve", probs_done)
            P.op("scalar", lambda e, H: e.dma_start(
                out=dbg_t["probs"][:, :], in_=prsb[:, 0:seq]), "s_st", 16)

        P.wait("vector", "s_dve", probs_done)
        for j in range(NT):
            P.op("vector", lambda e, H, j=j: e.tensor_scalar(
                out=hlc[:, j * 1024:(j + 1) * 1024],
                in0=y1tm[:, j * 1024:(j + 1) * 1024],
                scalar1=X["mstat"][:, j:j + 1], scalar2=X["rstdv"][:, j:j + 1],
                op0=OP.subtract, op1=OP.mult),
                sem="s_dve" if j == NT - 1 else None)
        x1_done = P.n("s_dve")
        x1tm = hlc
        P.wait("scalar", "s_dve", x1_done)
        for j in range(NT):
            P.op("scalar", lambda e, H, j=j: e.dma_start(
                out=d_x1_in[j * 128:(j + 1) * 128, :],
                in_=x1tm[:, j * 1024:(j + 1) * 1024]), "s_adma", 16)
        x1w = P.n("s_adma")
        P.wait("gpsimd", "s_adma", x1w)
        P.op("gpsimd", lambda e, H: e.collective_compute(
            "AllGather", OP.bypass, replica_groups=[list(range(NCORE))],
            ins=[d_x1_in.opt()], outs=[d_x1_out.opt()]), "s_cc", 1)
        P.op("gpsimd", lambda e, H: e.collective_compute(
            "AllReduce", OP.add, replica_groups=[list(range(NCORE))],
            ins=[d_bar_in.opt()], outs=[d_bar_out.opt()]), "s_cc", 1)
        cc_x1 = P.n("s_cc")
        if dbg:
            P.wait("scalar", "s_adma", x1w)
            for j in range(NT):
                P.op("scalar", lambda e, H, j=j: e.dma_start(
                    out=dbg_t["x1tm"][j * 128:(j + 1) * 128, :],
                    in_=x1tm[:, j * 1024:(j + 1) * 1024]), "s_st", 16)

        P.wait("gpsimd", "s_cc", cc_probs)
        pass
        P.op("gpsimd", lambda e, H: e.indirect_dma_start(
            out=prsb[:, 0:seq], out_offset=None, in_=d_probs_out[:, :],
            in_offset=IndirectOffsetOnAxis(ap=X["rsel"][:], axis=0),
            bounds_check=E * NCORE - 1, oob_is_err=False), "s_gdma", 16)
        eg = P.n("s_gdma")
        P.wait("gpsimd", "s_gdma", eg)
        P.op("gpsimd", lambda e, H: e.dma_start(
            out=d_escore.rearrange("(r t) o -> r (t o)", r=E),
            in_=prsb[:, 0:seq]), "s_gdma", 16)
        ew = P.n("s_gdma")
        P.wait("sync", "s_gdma", ew)
        P.op("sync", lambda e, H: e.dma_start(
            out=X["sc64"][:],
            in_=d_escore.rearrange("(p f) o -> p (f o)", p=64)), "s_in", 16)
        P.op("sync", lambda e, H: e.dma_start(
            out=sc128[:],
            in_=d_escore.rearrange("(p f) o -> p (f o)", p=128)), "s_in", 16)
        scl = P.n("s_in")

        P.wait("vector", "s_in", scl)
        P.op("vector", lambda e, H: e.memset(X["lo"][:], 0.0))
        P.op("vector", lambda e, H: e.memset(X["hi"][:], 1.0), "s_dve")
        P.wait("tensor", "s_dve", P.n("s_dve"))
        for it in range(40):
            P.wait("vector", "s_dve", P.n("s_dve"))
            P.op("vector", lambda e, H: e.tensor_tensor(
                out=X["midv"][:], in0=X["lo"][:], in1=X["hi"][:], op=OP.add),
                "s_cw")
            cw()
            P.op("vector", lambda e, H: e.tensor_scalar(
                out=X["midv"][:], in0=X["midv"][:], scalar1=0.5, scalar2=None,
                op0=OP.mult), "s_cw")
            cw()
            P.op("vector", lambda e, H: e.tensor_scalar(
                out=X["j64"][:], in0=X["sc64"][:], scalar1=X["midv"][:],
                scalar2=0.0, op0=OP.is_gt, op1=OP.add, accum_out=X["cnt"][:]),
                "s_cw")
            cw()
            P.op("vector", lambda e, H: e.tensor_copy(
                out=X["cntb"][:], in_=X["cnt"][:]), "s_dve")
            P.wait("tensor", "s_dve", P.n("s_dve"))
            P.op("tensor", lambda e, H: e.matmul(
                psSb(0)[0:64, 0:1], X["ones64"][:], X["cntb"][:], start=True,
                stop=True), "s_pe")
            P.wait("vector", "s_pe", P.n("s_pe"))
            P.op("vector", lambda e, H: e.tensor_scalar(
                out=X["bmask"][:], in0=psSb(0)[0:64, 0:1],
                scalar1=float(KSEL), scalar2=None, op0=OP.is_ge))
            P.op("vector", lambda e, H: e.tensor_scalar(
                out=X["nbmask"][:], in0=psSb(0)[0:64, 0:1],
                scalar1=float(KSEL), scalar2=None, op0=OP.is_lt), "s_cw")
            cw()
            P.op("vector", lambda e, H: e.select(
                out=X["lo"][:], mask=X["bmask"][:], on_true=X["midv"][:],
                on_false=X["lo"][:]))
            P.op("vector", lambda e, H: e.select(
                out=X["hi"][:], mask=X["nbmask"][:], on_true=X["midv"][:],
                on_false=X["hi"][:]), "s_dve")
            P.wait("tensor", "s_dve", P.n("s_dve"))
        P.wait("vector", "s_dve", P.n("s_dve"))
        P.op("vector", lambda e, H: e.tensor_scalar(
            out=X["j64"][:], in0=X["sc64"][:], scalar1=X["hi"][:], scalar2=0.0,
            op0=OP.is_gt, op1=OP.add, accum_out=X["cnt"][:]), "s_cw")
        cw()
        P.op("vector", lambda e, H: e.tensor_copy(
            out=X["cntb"][:], in_=X["cnt"][:]), "s_dve")
        P.wait("tensor", "s_dve", P.n("s_dve"))
        P.op("tensor", lambda e, H: e.matmul(
            psSb(0)[0:64, 0:1], X["ones64"][:], X["cntb"][:], start=True,
            stop=True), "s_pe")
        P.wait("vector", "s_pe", P.n("s_pe"))
        P.op("vector", lambda e, H: e.tensor_copy(
            out=X["mtot"][:], in_=psSb(0)[0:64, 0:1]), "s_cw")
        cw()
        P.op("vector", lambda e, H: e.tensor_scalar(
            out=X["rneed"][:], in0=X["mtot"][:], scalar1=-1.0,
            scalar2=float(KSEL), op0=OP.mult, op1=OP.add), "s_cw")
        cw()
        P.op("vector", lambda e, H: e.tensor_copy(out=X["tau128"][0:64, :],
                                                  in_=X["hi"][:]))
        P.op("vector", lambda e, H: e.tensor_copy(out=X["tau128"][64:128, :],
                                                  in_=X["hi"][:]))
        P.op("vector", lambda e, H: e.tensor_copy(out=X["r128"][0:64, :],
                                                  in_=X["rneed"][:]))
        P.op("vector", lambda e, H: e.tensor_copy(out=X["r128"][64:128, :],
                                                  in_=X["rneed"][:]), "s_dve")
        bis_done = P.n("s_dve")
        P.wait("vector", "s_dve", bis_done)
        if dbg:
            P.wait("scalar", "s_dve", bis_done)
            P.op("scalar", lambda e, H: e.dma_start(
                out=dbg_t["tau"][:, 0:1], allow_slow_non_contiguous=True, in_=X["hi"][:]), "s_st", 16)
            P.op("scalar", lambda e, H: e.dma_start(
                out=dbg_t["tau"][:, 1:2], allow_slow_non_contiguous=True, in_=X["mtot"][:]), "s_st", 16)
            P.op("scalar", lambda e, H: e.dma_start(
                out=dbg_t["tau"][:, 2:3], allow_slow_non_contiguous=True, in_=X["rneed"][:]), "s_st", 16)

        P.op("vector", lambda e, H: e.tensor_scalar(
            out=mgt[:], in0=sc128[:], scalar1=X["tau128"][:], scalar2=None,
            op0=OP.is_gt))
        P.op("vector", lambda e, H: e.tensor_scalar(
            out=meq[:], in0=sc128[:], scalar1=X["tau128"][:], scalar2=None,
            op0=OP.is_equal), "s_cw")
        cw()
        P.op("vector", lambda e, H: e.tensor_tensor_scan(
            out=tcum[:], data0=meq[:],
            data1=X["zero1"][:].to_broadcast([128, NPQ]), initial=0.0,
            op0=OP.add, op1=OP.add), "s_cw")
        cw()
        P.op("vector", lambda e, H: e.tensor_copy(
            out=X["mppb"][:], in_=tcum[:, NPQ - 1:NPQ]), "s_dve")
        P.wait("tensor", "s_dve", P.n("s_dve"))
        P.op("tensor", lambda e, H: e.matmul(
            psSb(0)[:, 0:1], X["trilb"][:], X["mppb"][:], start=True,
            stop=True), "s_pe")
        P.wait("vector", "s_pe", P.n("s_pe"))
        P.op("vector", lambda e, H: e.tensor_copy(
            out=X["t1c"][:], in_=psSb(0)[:, 0:1]), "s_cw")
        cw()
        P.op("vector", lambda e, H: e.tensor_tensor(
            out=tcum[:], in0=tcum[:], in1=meq[:], op=OP.subtract))
        P.op("vector", lambda e, H: e.tensor_scalar(
            out=tcum[:], in0=tcum[:], scalar1=X["t1c"][:], scalar2=None,
            op0=OP.add), "s_cw")
        cw()
        P.op("vector", lambda e, H: e.tensor_scalar(
            out=scrm[:], in0=tcum[:], scalar1=X["r128"][:], scalar2=None,
            op0=OP.is_lt), "s_cw")
        cw()
        P.op("vector", lambda e, H: e.tensor_tensor(
            out=scrm[:], in0=scrm[:], in1=meq[:], op=OP.mult), "s_cw")
        cw()
        P.op("vector", lambda e, H: e.tensor_tensor(
            out=msel[:], in0=mgt[:], in1=scrm[:], op=OP.add), "s_dve")
        msel_done = P.n("s_dve")
        P.wait("vector", "s_dve", msel_done)

        P.op("vector", lambda e, H: e.tensor_scalar(
            out=scrm[:], in0=msel[:], scalar1=-1.0, scalar2=1.0, op0=OP.mult,
            op1=OP.add), "s_cw")
        cw()
        P.op("vector", lambda e, H: e.tensor_tensor_scan(
            out=tcum[:], data0=scrm[:],
            data1=X["zero1"][:].to_broadcast([128, NPQ]), initial=0.0,
            op0=OP.add, op1=OP.add), "s_cw")
        cw()
        P.op("vector", lambda e, H: e.tensor_tensor(
            out=tcum[:], in0=tcum[:], in1=scrm[:], op=OP.subtract), "s_cw")
        cw()
        P.op("vector", lambda e, H: e.memset(fpM[:], 0.0))
        P.op("vector", lambda e, H: e.memset(fpV[:], 0.0))
        for k in range(NBITS):
            P.op("vector", lambda e, H, k=k: e.memset(fpB[k][:], 0.0))
        P.op("vector", lambda e, H: e.tensor_copy(
            out=fpV[:, 0:NPQ], in_=X["iota128f"][:, 0:NPQ]))
        P.op("vector", lambda e, H: e.tensor_copy(
            out=fpM[:, 0:NPQ], in_=msel[:]))
        P.op("vector", lambda e, H: e.tensor_tensor(
            out=fpmv[:, 0:NPQ], in0=tcum[:], in1=msel[:], op=OP.mult), "s_cw")
        cw()
        for k in range(NBITS - 1, -1, -1):
            P.op("vector", lambda e, H, k=k: e.tensor_scalar(
                out=fpB[k][:, 0:NPQ], in0=fpmv[:, 0:NPQ],
                scalar1=float(1 << k), scalar2=None, op0=OP.is_ge), "s_cw")
            cw()
            P.op("vector", lambda e, H, k=k: e.scalar_tensor_tensor(
                out=fpmv[:, 0:NPQ], in0=fpB[k][:, 0:NPQ],
                scalar=float(-(1 << k)), in1=fpmv[:, 0:NPQ], op0=OP.mult,
                op1=OP.add), "s_cw")
            cw()
        P.op("vector", lambda e, H: e.engine_nop() if False else
             e.memset(fpnm[:], 0.0), "s_dve")
        P.wait("vector", "s_dve", P.n("s_dve"))
        for k in range(NBITS):
            s = 1 << k
            P.op("vector", lambda e, H, k=k: e.tensor_tensor(
                out=fpmv[:], in0=fpM[:], in1=fpB[k][:], op=OP.mult), "s_cw")
            cw()
            P.op("vector", lambda e, H: e.tensor_copy(
                out=fpmvi[:], in_=fpmv[:]), "s_cw")
            cw()
            P.op("vector", lambda e, H, s=s: e.select(
                out=fpV[:, 0:PACKW - s], mask=fpmvi[:, s:PACKW],
                on_true=fpV[:, s:PACKW], on_false=fpV[:, 0:PACKW - s]))
            for j in range(k + 1, NBITS):
                P.op("vector", lambda e, H, s=s, j=j: e.select(
                    out=fpB[j][:, 0:PACKW - s], mask=fpmvi[:, s:PACKW],
                    on_true=fpB[j][:, s:PACKW],
                    on_false=fpB[j][:, 0:PACKW - s]))
            P.op("vector", lambda e, H: e.tensor_scalar(
                out=fpnm[:], in0=fpmv[:], scalar1=-1.0, scalar2=1.0,
                op0=OP.mult, op1=OP.add), "s_cw")
            cw()
            P.op("vector", lambda e, H: e.tensor_tensor(
                out=fpM[:], in0=fpM[:], in1=fpnm[:], op=OP.mult), "s_cw")
            cw()
            P.op("vector", lambda e, H, s=s: e.tensor_tensor(
                out=fpM[:, 0:PACKW - s], in0=fpM[:, 0:PACKW - s],
                in1=fpmv[:, s:PACKW], op=OP.max), "s_dve")
            P.wait("vector", "s_dve", P.n("s_dve"))
        pack_done = P.n("s_dve")

        P.op("vector", lambda e, H: e.tensor_scalar(
            out=scrm[:], in0=msel[:], scalar1=1.0, scalar2=0.0, op0=OP.mult,
            op1=OP.add, accum_out=X["npp"][:]), "s_cw")
        cw()
        P.op("vector", lambda e, H: e.tensor_copy(
            out=X["mppb"][:], in_=X["npp"][:]), "s_dve")
        P.wait("tensor", "s_dve", P.n("s_dve"))
        P.op("tensor", lambda e, H: e.matmul(
            psSb(0)[:, 0:1], X["trilb"][:], X["mppb"][:], start=True,
            stop=True), "s_pe")
        P.wait("vector", "s_pe", P.n("s_pe"))
        P.op("vector", lambda e, H: e.tensor_copy(
            out=X["mpp"][:], in_=psSb(0)[:, 0:1]), "s_dve")
        P.op("gpsimd", lambda e, H: e.iota(
            iotaf2.bitcast(I32)[:], pattern=[[1, KSEL]], base=0,
            channel_multiplier=0), "s_gp")
        P.wait("vector", "s_gp", P.n("s_gp"))
        P.wait("vector", "s_dve", P.n("s_dve"))
        P.op("vector", lambda e, H: e.tensor_copy(
            out=iotaf2[:], in_=iotaf2.bitcast(I32)[:]), "s_cw")
        cw()
        P.op("vector", lambda e, H: e.tensor_scalar(
            out=m12[:], in0=iotaf2[:], scalar1=X["mpp"][:], scalar2=None,
            op0=OP.is_ge), "s_dve")
        P.wait("tensor", "s_dve", P.n("s_dve"))
        for q in range(NKQ):
            P.op("tensor", lambda e, H, q=q: e.matmul(
                psSb(0)[0:1, :], X["onesb"][:, 0:1],
                m12[:, q * 512:(q + 1) * 512], start=True, stop=True), "s_pe")
            P.wait("vector", "s_pe", P.n("s_pe"))
            P.op("vector", lambda e, H, q=q: e.tensor_scalar(
                out=psrow[:, q * 512:(q + 1) * 512], in0=psSb(0)[0:1, :],
                scalar1=-1.0, scalar2=None, op0=OP.add), "s_dve")
            P.wait("tensor", "s_dve", P.n("s_dve"))
        P.wait("vector", "s_dve", P.n("s_dve"))
        P.op("vector", lambda e, H: e.tensor_copy(
            out=psrowb[:], in_=psrow[:]))
        P.op("vector", lambda e, H: e.tensor_copy(out=X["t1c"][:],
                                                  in_=X["mpp"][:]))
        P.op("vector", lambda e, H: e.memset(X["t2c"][:], 0.0), "s_cw")
        cw()
        for jb in range(11, 5, -1):
            P.op("vector", lambda e, H, jb=jb: e.tensor_scalar(
                out=X["lnr"][:], in0=X["t1c"][:], scalar1=float(1 << jb),
                scalar2=None, op0=OP.is_ge), "s_cw")
            cw()
            P.op("vector", lambda e, H, jb=jb: e.scalar_tensor_tensor(
                out=X["t1c"][:], in0=X["lnr"][:], scalar=float(-(1 << jb)),
                in1=X["t1c"][:], op0=OP.mult, op1=OP.add), "s_cw")
            cw()
            P.op("vector", lambda e, H, jb=jb: e.scalar_tensor_tensor(
                out=X["t2c"][:], in0=X["lnr"][:], scalar=float(1 << (jb - 6)),
                in1=X["t2c"][:], op0=OP.mult, op1=OP.add), "s_cw")
            cw()
        P.op("vector", lambda e, H: e.tensor_copy(out=X["mhib"][:],
                                                  in_=X["t2c"][:]))
        P.op("vector", lambda e, H: e.tensor_copy(out=X["mlob"][:],
                                                  in_=X["t1c"][:]), "s_dve")
        P.wait("tensor", "s_dve", P.n("s_dve"))
        for q in range(NKQ):
            P.op("tensor", lambda e, H, q=q: e.matmul(
                psSb(0)[:, :], X["onesb"][0:1, :],
                psrowb[:, q * 512:(q + 1) * 512], start=True, stop=True),
                "s_pe")
            P.wait("vector", "s_pe", P.n("s_pe"))
            P.op("vector", lambda e, H, q=q: e.tensor_scalar(
                out=m12[:, q * 512:(q + 1) * 512], in0=psSb(0)[:, :],
                scalar1=X["iotapf"][:], scalar2=None, op0=OP.is_equal),
                "s_dve")
            P.wait("tensor", "s_dve", P.n("s_dve"))
            P.op("tensor", lambda e, H, q=q: e.matmul(
                psSb(1)[0:1, :], X["mhib"][:], m12[:, q * 512:(q + 1) * 512],
                start=True, stop=True), "s_pe")
            P.wait("vector", "s_pe", P.n("s_pe"))
            P.op("vector", lambda e, H, q=q: e.tensor_copy(
                out=workA[:, q * 512:(q + 1) * 512], in_=psSb(1)[0:1, :]),
                "s_dve")
            P.wait("tensor", "s_dve", P.n("s_dve"))
            P.op("tensor", lambda e, H, q=q: e.matmul(
                psSb(1)[0:1, :], X["mlob"][:], m12[:, q * 512:(q + 1) * 512],
                start=True, stop=True), "s_pe")
            P.wait("vector", "s_pe", P.n("s_pe"))
            P.op("vector", lambda e, H, q=q: e.tensor_copy(
                out=workB[:, q * 512:(q + 1) * 512], in_=psSb(1)[0:1, :]),
                "s_dve")
        P.wait("vector", "s_dve", P.n("s_dve"))
        P.op("vector", lambda e, H: e.scalar_tensor_tensor(
            out=workB[:], in0=workA[:], scalar=64.0, in1=workB[:],
            op0=OP.mult, op1=OP.add), "s_cw")
        cw()
        P.op("vector", lambda e, H: e.tensor_tensor(
            out=workB[:], in0=iotaf2[0:1, :], in1=workB[:], op=OP.subtract),
            "s_cw")
        cw()
        P.op("vector", lambda e, H: e.tensor_copy(
            out=jrowb[:], in_=workB[:]))
        P.op("vector", lambda e, H: e.tensor_copy(
            out=X["tpackb"][:], in_=fpV[:, 0:128]), "s_dve")
        P.wait("tensor", "s_dve", P.n("s_dve"))
        for q in range(NKQ):
            P.op("tensor", lambda e, H, q=q: e.matmul(
                psSb(0)[:, :], X["onesb"][0:1, :],
                jrowb[:, q * 512:(q + 1) * 512], start=True, stop=True),
                "s_pe")
            P.wait("vector", "s_pe", P.n("s_pe"))
            P.op("vector", lambda e, H, q=q: e.tensor_scalar(
                out=jhT[:], in0=psSb(0)[:, :], scalar1=X["iotapf"][:],
                scalar2=None, op0=OP.is_equal), "s_dve")
            P.wait("tensor", "s_dve", P.n("s_dve"))
            P.op("tensor", lambda e, H, q=q: e.matmul(
                psSb(1)[:, :], X["tpackb"][:], m12[:, q * 512:(q + 1) * 512],
                start=True, stop=True), "s_pe")
            P.wait("vector", "s_pe", P.n("s_pe"))
            P.op("vector", lambda e, H, q=q: e.tensor_tensor(
                out=r2T[:], in0=jhT[:], in1=psSb(1)[:, :], op=OP.mult),
                "s_dve")
            P.wait("tensor", "s_dve", P.n("s_dve"))
            P.op("tensor", lambda e, H, q=q: e.matmul(
                psSb(1)[0:1, :], X["onesb"][:, 0:1], r2T[:], start=True,
                stop=True), "s_pe")
            P.wait("vector", "s_pe", P.n("s_pe"))
            P.op("vector", lambda e, H, q=q: e.scalar_tensor_tensor(
                out=workB[:, q * 512:(q + 1) * 512],
                in0=psrow[:, q * 512:(q + 1) * 512], scalar=128.0,
                in1=psSb(1)[0:1, :], op0=OP.mult, op1=OP.add), "s_dve")
            P.wait("tensor", "s_dve", P.n("s_dve"))
        P.wait("vector", "s_dve", P.n("s_dve"))
        P.op("vector", lambda e, H: e.tensor_copy(
            out=idrowu[:], in_=workB[:]), "s_dve")
        id_done = P.n("s_dve")
        P.wait("scalar", "s_dve", id_done)
        P.op("scalar", lambda e, H: e.dma_start(
            out=d_ids.rearrange("(o k) c -> o (k c)", o=1), in_=idrowu[:]),
            "s_adma", 16)
        idw = P.n("s_adma")
        P.wait("sync", "s_adma", idw)
        for j in range(NSL):
            P.op("sync", lambda e, H, j=j: e.dma_start(
                out=X["idtiles"][:, j:j + 1],
                in_=d_ids[j * 128:(j + 1) * 128, :]), "s_in", 16)
        idt = P.n("s_in")
        if dbg:
            P.wait("scalar", "s_adma", idw)
            P.op("scalar", lambda e, H: e.dma_start(
                out=dbg_t["ids"][:, :], in_=d_ids[:, :]), "s_st", 16)

        P.wait("gpsimd", "s_in", idt)
        P.wait("gpsimd", "s_cc", cc_x1)
        pass
        gtoks = ptl
        P.wait("gpsimd", "s_dve", x1_done)
        for j in range(NSL):
            P.op("gpsimd", lambda e, H, j=j: e.indirect_dma_start(
                out=gtoks[:, j * D:(j + 1) * D], out_offset=None,
                in_=d_x1_out[:, :],
                in_offset=IndirectOffsetOnAxis(ap=X["idtiles"][:, j:j + 1],
                                               axis=0),
                bounds_check=NTOK - 1, oob_is_err=False), "s_gdma", 16)
        tg = P.n("s_gdma")
        if dbg:
            P.wait("gpsimd", "s_gdma", tg)
            for j in range(NSL):
                P.op("gpsimd", lambda e, H, j=j: e.dma_start(
                    out=dbg_t["toks"][j * 128:(j + 1) * 128, :],
                    in_=gtoks[:, j * D:(j + 1) * D]), "s_gdma", 16)
        P.wait("tensor", "s_gdma", tg)
        P.wait("tensor", "s_dve", id_done)
        tokens_fm = h1a
        for j in range(NSL):
            tb = j % 2
            if j >= 2:
                P.wait("tensor", "s_dve", P.n("s_dve"))
            for d in range(ND):
                P.op("tensor", lambda e, H, j=j, d=d, tb=tb: e.transpose(
                    psTb(tb)[:, d * 128:(d + 1) * 128],
                    gtoks[:, j * D + d * 128:j * D + (d + 1) * 128],
                    X["identb"][:]), sem="s_pe" if d == ND - 1 else None)
            tp2 = P.n("s_pe")
            P.wait("vector", "s_pe", tp2)
            for d in range(ND):
                P.op("vector", lambda e, H, j=j, d=d, tb=tb: e.tensor_copy(
                    out=tokens_fm[:, d * KSEL + j * 128:
                                  d * KSEL + (j + 1) * 128],
                    in_=psTb(tb)[:, d * 128:(d + 1) * 128]),
                    sem="s_dve" if d == ND - 1 else None)
        tfm_done = P.n("s_dve")

        P.wait("tensor", "s_dve", tfm_done)
        for q in range(NKQ):
            if q >= 1:
                P.wait("tensor", "s_act", P.n("s_act"))
            for d in range(ND):
                P.op("tensor", lambda e, H, q=q, d=d: e.matmul(
                    psSb(0)[0:E, :], X["wrb"][:, d * E:(d + 1) * E],
                    tokens_fm[:, d * KSEL + q * 512:d * KSEL + (q + 1) * 512],
                    start=(d == 0), stop=(d == ND - 1)),
                    sem="s_pe" if d == ND - 1 else None)
            gq = P.n("s_pe")
            P.wait("scalar", "s_pe", gq)
            P.op("scalar", lambda e, H, q=q: e.activation(
                out=gexp[:, q * 512:(q + 1) * 512], in_=psSb(0)[0:E, :],
                func=AF.Exp, bias=X["brt"][:], scale=1.0), "s_act")
        ge_done = P.n("s_act")
        P.wait("tensor", "s_act", ge_done)
        for q in range(NKQ):
            P.op("tensor", lambda e, H, q=q: e.matmul(
                psSb(1)[0:1, :], X["ones8f"][:, 0:1],
                gexp[:, q * 512:(q + 1) * 512], start=True, stop=True))
            P.op("tensor", lambda e, H, q=q: e.matmul(
                psSb(0)[0:1, :], X["oneh"][:],
                gexp[:, q * 512:(q + 1) * 512], start=True, stop=True), "s_pe")
            g2 = P.n("s_pe")
            P.wait("vector", "s_pe", g2)
            P.op("vector", lambda e, H, q=q: e.reciprocal(
                out=grecip[:, q * 512:(q + 1) * 512], in_=psSb(1)[0:1, :]),
                "s_cw")
            cw()
            P.op("vector", lambda e, H, q=q: e.tensor_tensor(
                out=gslotr[:, q * 512:(q + 1) * 512],
                in0=grecip[:, q * 512:(q + 1) * 512], in1=psSb(0)[0:1, :],
                op=OP.mult), "s_dve")
            P.wait("tensor", "s_dve", P.n("s_dve"))
        g_done = P.n("s_dve")
        P.wait("scalar", "s_dve", g_done)
        P.op("scalar", lambda e, H: e.dma_start(
            out=d_grow[:, :], in_=gslotr[:]), "s_adma", 16)
        gw = P.n("s_adma")
        P.wait("sync", "s_adma", gw)
        for j in range(NSL):
            P.op("sync", lambda e, H, j=j: e.dma_start(
                out=X["gslot_tm"][:, j:j + 1],
                in_=d_grow.rearrange("o (a b) -> (o a) b", b=1)[
                    j * 128:(j + 1) * 128, :]), "s_in", 16)
        gtm = P.n("s_in")
        if dbg:
            P.wait("scalar", "s_adma", gw)
            P.op("scalar", lambda e, H: e.dma_start(
                out=dbg_t["gslot"][:, :], in_=d_grow[:, :]), "s_st", 16)

        P.wait("vector", "s_in", gtm)
        P.wait("scalar", "s_dve", g_done)
        P.wait("gpsimd", "s_zero", P.n("s_zero"))
        dbank = [lambda i=i: psKb(i) for i in range(2)] + \
                [lambda i=i: psPb(i) for i in range(2)] + \
                [lambda i=i: psSb(i) for i in range(2)] + \
                [lambda i=i: psTfb(i) for i in range(2)]

        scat_marks = []
        for tq in range(NKQ):
            if tq >= 1:
                P.wait("tensor", "s_dve", P.n("s_dve"))
            cast_marks = []
            for fc in range(NF):
                rows = FCH[fc]
                hb = fc % 2
                if fc >= 2:
                    P.wait("sync", "s_gp", cast_marks[fc - 2])
                else:
                    P.wait("sync", "s_act", P.n("s_act"))
                P.op("sync", lambda e, H, fc=fc, hb=hb, rows=rows:
                     e.dma_start(
                         out=X["wstg"][:, hb * 1024:hb * 1024 + ND * rows],
                         in_=wu[:, fc * 1024:fc * 1024 + ND * rows]),
                     "s_in", 16)
                wl = P.n("s_in")
                P.wait("gpsimd", "s_in", wl)
                P.wait("gpsimd", "s_pe", P.n("s_pe"))
                P.op("gpsimd", lambda e, H, fc=fc, hb=hb, rows=rows:
                     e.tensor_copy(
                         out=X["wub2"][:, hb * 1024:hb * 1024 + ND * rows],
                         in_=X["wstg"][:, hb * 1024:hb * 1024 + ND * rows]),
                     "s_gp")
                wc = P.n("s_gp")
                cast_marks.append(wc)
                P.wait("tensor", "s_gp", wc)
                if fc >= 2:
                    P.wait("tensor", "s_act", P.n("s_act"))
                for d in range(ND):
                    P.op("tensor", lambda e, H, tq=tq, fc=fc, d=d, hb=hb,
                         rows=rows: e.matmul(
                             psKb(fc % 2)[0:rows, :],
                             X["wub2"][:, hb * 1024 + d * rows:
                                       hb * 1024 + (d + 1) * rows],
                             tokens_fm[:, d * KSEL + tq * 512:
                                       d * KSEL + (tq + 1) * 512],
                             start=(d == 0), stop=(d == ND - 1)),
                         sem="s_pe" if d == ND - 1 else None)
                um = P.n("s_pe")
                P.wait("scalar", "s_pe", um)
                P.op("scalar", lambda e, H, fc=fc, rows=rows: e.activation(
                    out=mid[0:rows, fc * 512:(fc + 1) * 512],
                    in_=psKb(fc % 2)[0:rows, :], func=AF.Silu,
                    bias=X["but"][0:rows, fc:fc + 1], scale=1.0), "s_act")
            silu_done = P.n("s_act")
            if dbg and tq == 0:
                P.op("scalar", lambda e, H: e.dma_start(
                    out=dbg_t["mid0"][:, :], in_=mid[:, :]), "s_adma", 16)

            P.wait("tensor", "s_act", silu_done)
            pe_base = P.n("s_pe")
            cast_done = {}
            pe_marks = {}

            def d_load(fc):
                rows = FCH[fc]
                hb = fc % 2
                if fc >= 2:
                    P.wait("sync", "s_act", cast_done[fc - 2])
                else:
                    P.wait("sync", "s_gp", P.n("s_gp"))
                P.op("sync", lambda e, H, fc=fc, hb=hb, rows=rows:
                     e.dma_start(
                         out=X["wstg"][:, hb * 1024:hb * 1024 + 1024][
                             0:rows, :],
                         in_=wd[fc * 128:fc * 128 + rows, :]), "s_in", 16)
                return P.n("s_in")

            def d_cast(fc, wl):
                rows = FCH[fc]
                hb = fc % 2
                P.wait("scalar", "s_in", wl)
                P.wait("scalar", "s_pe",
                       pe_marks[fc - 2] if fc >= 2 else pe_base)
                P.op("scalar", lambda e, H, fc=fc, hb=hb, rows=rows:
                     e.activation(
                         out=X["wdb2"][0:rows, hb * 1024:hb * 1024 + 1024],
                         in_=X["wstg"][:, hb * 1024:hb * 1024 + 1024][
                             0:rows, :], func=AF.Copy, bias=0.0, scale=1.0),
                     "s_act")
                cast_done[fc] = P.n("s_act")

            wl0 = d_load(0)
            d_cast(0, wl0)
            for fc in range(NF):
                rows = FCH[fc]
                hb = fc % 2
                if fc + 1 < NF:
                    wl = d_load(fc + 1)
                P.wait("tensor", "s_act", cast_done[fc])
                for st4 in range(4):
                    for hh in range(2):
                        P.op("tensor", lambda e, H, tq=tq, fc=fc, st4=st4,
                             hh=hh, rows=rows, hb=hb: e.matmul(
                                 dbank[st4 * 2 + hh](),
                                 mid[0:rows, fc * 512 + st4 * 128:
                                     fc * 512 + (st4 + 1) * 128],
                                 X["wdb2"][0:rows, hb * 1024 + hh * 512:
                                           hb * 1024 + (hh + 1) * 512],
                                 start=(fc == 0), stop=(fc == NF - 1)),
                             sem="s_pe" if (st4 == 3 and hh == 1) else None)
                pe_marks[fc] = P.n("s_pe")
                if fc + 1 < NF:
                    d_cast(fc + 1, wl)
            dn = P.n("s_pe")
            P.wait("vector", "s_pe", dn)
            for st4 in range(4):
                j = tq * 4 + st4
                wrt = wrow if j % 2 == 0 else wrow2
                if len(scat_marks) >= 2:
                    P.wait("vector", "s_gdma", scat_marks[-2])
                for hh in range(2):
                    P.op("vector", lambda e, H, st4=st4, hh=hh: e.tensor_tensor(
                        out=X["stsc"][:, hh * 512:(hh + 1) * 512],
                        in0=dbank[st4 * 2 + hh](),
                        in1=X["bdt"][:, hh * 512:(hh + 1) * 512], op=OP.add),
                        "s_cw")
                    cw()
                    P.op("vector", lambda e, H, j=j, hh=hh, wrt=wrt:
                         e.tensor_scalar(
                             out=wrt[:, hh * 512:(hh + 1) * 512],
                             in0=X["stsc"][:, hh * 512:(hh + 1) * 512],
                             scalar1=X["gslot_tm"][:, j:j + 1], scalar2=None,
                             op0=OP.mult), "s_dve")
                wv = P.n("s_dve")
                P.wait("gpsimd", "s_dve", wv)
                P.op("gpsimd", lambda e, H, j=j, wrt=wrt: e.indirect_dma_start(
                    out=d_y_in[:, :],
                    out_offset=IndirectOffsetOnAxis(
                        ap=X["idtiles"][:, j:j + 1], axis=0),
                    in_=wrt[:, 0:D], in_offset=None,
                    bounds_check=NTOK - 1, oob_is_err=False), "s_gdma", 16)
                scat_marks.append(P.n("s_gdma"))
        ffn_done = P.n("s_gdma")

        P.wait("gpsimd", "s_gdma", ffn_done)
        if dbg:
            P.op("gpsimd", lambda e, H: e.dma_start(
                out=dbg_t["yin"][:, :], in_=d_y_in[:, :]), "s_gdma", 16)
            P.wait("gpsimd", "s_gdma", P.n("s_gdma"))
        P.op("gpsimd", lambda e, H: e.collective_compute(
            "ReduceScatter", OP.add, replica_groups=[list(range(NCORE))],
            ins=[d_y_in.opt()], outs=[d_y_out.opt()]), "s_cc", 1)
        P.op("gpsimd", lambda e, H: e.collective_compute(
            "AllReduce", OP.add, replica_groups=[list(range(NCORE))],
            ins=[d_bar_in.opt()], outs=[d_bar_out.opt()]), "s_cc", 1)
        cc_y = P.n("s_cc")
        P.wait("gpsimd", "s_cc", cc_y)
        cc_delay(P, X)
        P.op("gpsimd", lambda e, H: e.memset(X["zrow"][:, 0:512], 0.0), "s_gp")
        ccy_rdy = P.n("s_gp")
        if dbg:
            P.wait("gpsimd", "s_cc", cc_y)
            P.op("gpsimd", lambda e, H: e.dma_start(
                out=dbg_t["y"][:, :], in_=d_y_out[:, :]), "s_gdma", 16)
        P.wait("sync", "s_cc", cc_y)
        P.wait("sync", "s_gp", ccy_rdy)
        for j in range(NT):
            P.wait("sync", "s_dve", P.n("s_dve"))
            P.op("sync", lambda e, H, j=j: e.dma_start(
                out=x1re[:, 0:D], in_=d_x1_in[j * 128:(j + 1) * 128, :]),
                "s_in", 16)
            P.op("sync", lambda e, H, j=j: e.dma_start(
                out=yre[:, 0:D], in_=d_y_out[j * 128:(j + 1) * 128, :]),
                "s_in", 16)
            l2 = P.n("s_in")
            P.wait("vector", "s_in", l2)
            P.wait("vector", "s_st", P.n("s_st"))
            P.op("vector", lambda e, H: e.tensor_tensor(
                out=accq[:, 0:D], in0=x1re[:, 0:D], in1=yre[:, 0:D],
                op=OP.add), "s_cw")
            cw()
            P.op("vector", lambda e, H: e.tensor_scalar(
                out=X["stsc"][:], in0=accq[:, 0:D], scalar1=1.0, scalar2=0.0,
                op0=OP.mult, op1=OP.add, accum_out=X["lnm"][:]), "s_dve")
            aq = P.n("s_dve")
            P.wait("scalar", "s_dve", aq)
            P.op("scalar", lambda e, H: e.activation(
                out=X["stsc"][:], in_=accq[:, 0:D], func=AF.Square, bias=0.0,
                scale=1.0), "s_act")
            P.wait("vector", "s_act", P.n("s_act"))
            P.op("vector", lambda e, H: e.tensor_scalar(
                out=X["stsc"][:], in0=X["stsc"][:], scalar1=1.0, scalar2=0.0,
                op0=OP.mult, op1=OP.add, accum_out=X["lnv"][:]), "s_cw")
            cw()
            P.op("vector", lambda e, H: e.tensor_scalar(
                out=X["lnm"][:], in0=X["lnm"][:], scalar1=1.0 / D,
                scalar2=None, op0=OP.mult), "s_cw")
            cw()
            P.op("vector", lambda e, H: e.tensor_tensor(
                out=X["lnr"][:], in0=X["lnm"][:], in1=X["lnm"][:],
                op=OP.mult), "s_cw")
            cw()
            P.op("vector", lambda e, H: e.scalar_tensor_tensor(
                out=X["lnv"][:], in0=X["lnv"][:], scalar=1.0 / D,
                in1=X["lnr"][:], op0=OP.mult, op1=OP.subtract), "s_dve")
            v2 = P.n("s_dve")
            P.wait("scalar", "s_dve", v2)
            P.op("scalar", lambda e, H: e.activation(
                out=X["lnv"][:], in_=X["lnv"][:], func=AF.Ln, bias=X["epsc"][:],
                scale=1.0), "s_cs")
            cs()
            P.op("scalar", lambda e, H: e.activation(
                out=X["lnv"][:], in_=X["lnv"][:], func=AF.Exp, bias=0.0,
                scale=-0.5), "s_act")
            a2 = P.n("s_act")
            P.wait("vector", "s_act", a2)
            P.op("vector", lambda e, H: e.tensor_scalar(
                out=osta[:, 0:D], in0=accq[:, 0:D], scalar1=X["lnm"][:],
                scalar2=X["lnv"][:], op0=OP.subtract, op1=OP.mult), "s_dve")
            o2 = P.n("s_dve")
            P.wait("scalar", "s_dve", o2)
            P.op("scalar", lambda e, H, j=j: e.dma_start(
                out=out[j * 128:(j + 1) * 128, :], in_=osta[:, 0:D]),
                "s_st", 16)
        P.wait("scalar", "s_st", P.n("s_st"))
        P.wait("gpsimd", "s_st", P.n("s_st"))
        P.wait("gpsimd", "s_gdma", P.n("s_gdma"))
        P.wait("vector", "s_st", P.n("s_st"))

        with PSE:
            with contextlib.ExitStack() as semctx:
                handles = {}
                for name in P.sems:
                    handles[name] = semctx.enter_context(nc.semaphore(name))
                block = semctx.enter_context(nc.Block())
                eng_items = {k: [] for k in
                             ["sync", "tensor", "scalar", "vector", "gpsimd"]}
                for eng, fn in P.items:
                    eng_items[eng].append(fn)

                @block.sync
                def _(eng):
                    for fn in eng_items["sync"]:
                        fn(eng, handles)

                @block.tensor
                def _(eng):
                    for fn in eng_items["tensor"]:
                        fn(eng, handles)

                @block.scalar
                def _(eng):
                    for fn in eng_items["scalar"]:
                        fn(eng, handles)

                @block.vector
                def _(eng):
                    for fn in eng_items["vector"]:
                        fn(eng, handles)

                @block.gpsimd
                def _(eng):
                    for fn in eng_items["gpsimd"]:
                        fn(eng, handles)

    return nc


def _prep_inputs(inputs, seq):
    x = np.asarray(inputs["L0_attn_x"])
    xt = np.ascontiguousarray(x.transpose(0, 2, 1))
    in_maps = []
    for c in range(NCORE):
        b, q = c // 4, c % 4
        m = {"xT": np.ascontiguousarray(xt[b][:, q * seq:(q + 1) * seq])}
        names = [("L0_attn_Wz0", "L0_attn_Wh0", "L0_attn_bz0", "L0_attn_bh0"),
                 ("L0_attn_Wz1", "L0_attn_Wh1", "L0_attn_bz1", "L0_attn_bh1")]
        bz2 = np.zeros((128, 2 * ND), np.float32)
        bh2 = np.zeros((128, 2 * ND), np.float32)
        def _wlay(w):
            a = np.asarray(w).T.reshape(ND, 128, D).transpose(1, 0, 2)
            blocks = [a[:, :, d * 128:(d + 1) * 128].reshape(128, ND * 128)
                      for d in range(ND)]
            return np.ascontiguousarray(np.concatenate(blocks, axis=1))

        for l, (wz_, wh_, bz_, bh_) in enumerate(names):
            m[f"wzT{l}"] = _wlay(inputs[wz_])
            m[f"whT{l}"] = _wlay(inputs[wh_])
            bz2[:, l * ND:(l + 1) * ND] = np.asarray(
                inputs[bz_]).reshape(ND, 128).T
            bh2[:, l * ND:(l + 1) * ND] = np.asarray(
                inputs[bh_]).reshape(ND, 128).T
        m["bzin"] = bz2
        m["bhin"] = bh2
        m["wr"] = np.ascontiguousarray(np.asarray(inputs["L0_ffn_Wr"]))
        m["br"] = np.ascontiguousarray(
            np.asarray(inputs["L0_ffn_br"]).reshape(E, 1))
        wu_ = np.asarray(inputs["L0_ffn_W_up"])[c // 2]
        wa = wu_.reshape(ND, 128, F_UP).transpose(1, 0, 2)
        wu2 = np.zeros((128, NF * ND * 128), np.float32)
        for fc in range(NF):
            rows = FCH[fc]
            wu2[:, fc * 1024:fc * 1024 + ND * rows] = wa[
                :, :, fc * 128:fc * 128 + rows].reshape(128, ND * rows)
        m["wu"] = wu2
        bu_ = np.zeros((128, NF), np.float32)
        buv = np.asarray(inputs["L0_ffn_b_up"])[c // 2]
        for fc in range(NF):
            bu_[0:FCH[fc], fc] = buv[fc * 128:fc * 128 + FCH[fc]]
        m["bu"] = bu_
        m["wd"] = np.ascontiguousarray(np.asarray(inputs["L0_ffn_W_down"])[c])
        m["bd"] = np.ascontiguousarray(
            np.asarray(inputs["L0_ffn_b_down"])[c].reshape(1, D))
        cm = np.zeros((128, NCORE), np.float32)
        for r in range(4 * b, 4 * b + q):
            cm[:, r] = 1.0
        m["cmask"] = cm
        m["rowsel"] = (np.arange(NCORE, dtype=np.uint32) * E + c).reshape(E, 1)
        oh = np.zeros((E, 1), np.float32)
        oh[c, 0] = 1.0
        m["onehot"] = oh
        in_maps.append(m)
    return in_maps


def kernel(**inputs):
    seq = 2048
    nc = build_program(seq=seq, dbg=False)
    in_maps = _prep_inputs(inputs, seq)
    res = run_bass_kernel_spmd(nc, in_maps, core_ids=list(range(NCORE)))
    full = np.concatenate([r["out"] for r in res.results], axis=0)
    return full.reshape(2, 8192, 1024)
